# version 20
# baseline (speedup 1.0000x reference)
import numpy as np
import ml_dtypes

import concourse.bass as bass
import concourse.mybir as mybir
from concourse.bass_utils import run_bass_kernel_spmd

F32, BF16, I32 = mybir.dt.float32, mybir.dt.bfloat16, mybir.dt.int32
AF = mybir.ActivationFunctionType
ALU = mybir.AluOpType
AX = mybir.AxisListType

D = 1024
FQ = 512
NCB = 8
NEXP = 8
S = 16384
EPS_LN = 1e-5
N_CORES = 8

ISL_START = [0, 4096, 8192, 12288]
ISL_LEN = [4096, 4096, 4096, 4096]

SS = None

WLOAD_ALIASES = ("wmr0", "wmr1", "crow", "scrow", "wabf", "wabloF",
                 "rowbuf0", "rowbuf1")
ET_RES = tuple("EtL1" if e < 2 else ("EtL2" if e < 5 else "EtH")
               for e in range(8))
ET_ALIASES = ("h0pre", "shpk", "aTc", "bTc", "AcumTc", "UTe", "Hsum",
              "OH8", "E1", "Sall", "wmr0", "wmr1", "crow", "scrow",
              "wabf", "wabloF", "rowbuf0", "rowbuf1")


class Prog:
    def __init__(self):
        self.q = {e: [] for e in ("sync", "tensor", "vector", "scalar",
                                  "gpsimd")}

    def add(self, ek, f):
        self.q[ek].append(f)


class Sched:

    def __init__(self, prog, sems, ticks):
        self.p = prog
        self.sems = sems
        self.ticks = ticks
        self.res = {}
        self.rmap = {}

    def key_for(self, ek, dma):
        if ek == "sync":
            return "SYD"
        if ek == "gpsimd":
            return "GD" if dma else "GC"
        return {"tensor": "T", "vector": "V", "scalar": "S"}[ek]

    def stage(self, ek, fn, reads=(), writes=(), dma=False, ndma=1):
        key = self.key_for(ek, dma)
        pub_key = {"sync": "SYP", "gpsimd": "GP"}.get(ek) if dma else None
        deps = {}

        def add_dep(k2, v2):
            if v2 > deps.get(k2, -1):
                deps[k2] = v2

        reads = tuple(self.rmap.get(r, r) for r in reads)
        writes = tuple(self.rmap.get(w, w) for w in writes)
        for r in reads:
            rec = self.res.get(r)
            if rec and rec["w"]:
                add_dep(*rec["w"])
        for w in writes:
            rec = self.res.get(w)
            if rec:
                if rec["w"]:
                    add_dep(*rec["w"])
                for rd in rec["r"]:
                    add_dep(*rd)
        inc = 16 * ndma if dma else 1
        end_val = self.sems[key][1] + inc
        self.sems[key][1] = end_val
        if dma:
            pub_val = self.sems[pub_key][1] + 1
            self.sems[pub_key][1] = pub_val
            done = (pub_key, pub_val)
        else:
            done = (key, end_val)
        for w in writes:
            self.res[w] = {"w": done, "r": []}
        for r in reads:
            rec = self.res.setdefault(r, {"w": None, "r": []})
            rec["r"].append(done)

        handle = {k: self.sems[k][0] for k in self.sems}
        tick = self.ticks[ek]
        dep_items = sorted(deps.items())
        my_handle = self.sems[key][0]
        pub_handle = self.sems[pub_key][0] if dma else None

        def run(eng):
            global SS

            def ss(inst):
                tick[1] += 1
                inst.then_inc(tick[0], 1, skip_validation=True)
                eng.wait_ge(tick[0], tick[1])
                return inst
            SS = ss
            for k2, v2 in dep_items:
                eng.wait_ge(handle[k2], v2)
            ret = fn(eng)
            if isinstance(ret, list):
                assert dma and len(ret) == ndma, (dma, ndma, len(ret))
                for inst in ret:
                    inst.then_inc(my_handle, 16, skip_validation=True)
            elif dma:
                assert ndma == 1
                ret.then_inc(my_handle, 16, skip_validation=True)
            else:
                ret.then_inc(my_handle, 1, skip_validation=True)
            if dma:
                eng.wait_ge(my_handle, end_val)
                eng.nop().then_inc(pub_handle, 1, skip_validation=True)
        self.p.add(ek, run)


def emit_ln(ch, t):
    def st01(eng):
        eng.bn_stats(t["bn12"][:, 0:6], t["xt"][:, 0:512])
        SS(eng.bn_stats(t["bn12"][:, 6:12], t["xt"][:, 512:1024]))
        return eng.bn_aggr(t["mv"], t["bn12"])
    ch.stage("vector", st01, reads=("xt",), writes=("bn12", "mv"))

    def st02(eng):
        SS(eng.activation(t["lnv"], t["mv"][:, 1:2], AF.Ln, bias=EPS_LN))
        return eng.activation(t["rstd"], t["lnv"], AF.Exp, scale=-0.5)
    ch.stage("scalar", st02, reads=("mv", "constv"), writes=("lnv", "rstd"))

    ch.stage("vector", lambda e: e.tensor_scalar(
        t["h0"], t["xt"], t["mv"][:, 0:1], t["rstd"][:, 0:1],
        ALU.subtract, ALU.mult),
        reads=("xt", "mv", "rstd"), writes=("h0",))


def emit_ln_b(ch, t):
    def st01(eng):
        eng.bn_stats(t["bn12"][:, 0:6], t["xt"][:, 0:512])
        SS(eng.bn_stats(t["bn12"][:, 6:12], t["xt"][:, 512:1024]))
        return eng.bn_aggr(t["mv"], t["bn12"])
    ch.stage("vector", st01, reads=("xt",), writes=("bn12", "mv"))

    def st02(eng):
        SS(eng.activation(t["lnv"], t["mv"][:, 1:2], AF.Ln, bias=EPS_LN))
        return eng.activation(t["rstd"], t["lnv"], AF.Exp, scale=-0.5)
    ch.stage("scalar", st02, reads=("mv", "constv"), writes=("lnv", "rstd"))

    ch.stage("vector", lambda e: e.tensor_scalar(
        t["h0b"], t["xt"], t["mv"][:, 0:1], t["rstd"][:, 0:1],
        ALU.subtract, ALU.mult),
        reads=("xt", "mv", "rstd"), writes=("h0",))


def emit_transposes_h0_b(ch, t):
    def st04(eng):
        last = None
        pb = t["psv"][:].bitcast(BF16)
        for cb in range(8):
            last = eng.transpose(pb[0:128, cb * 128:(cb + 1) * 128],
                                 t["h0b"][:, cb * 128:(cb + 1) * 128],
                                 t["I128B"][:])
        return last
    ch.stage("tensor", st04, reads=("h0", "I128B"), writes=("psv",))


def emit_transposes_h0(ch, t):
    def st04(eng):
        last = None
        for cb in range(8):
            last = eng.transpose(t["psv"][:, cb * 128:(cb + 1) * 128],
                                 t["h0"][:, cb * 128:(cb + 1) * 128],
                                 t["I128F"][:])
        return last
    ch.stage("tensor", st04, reads=("h0", "I128F"), writes=("psv",))


def emit_qka(ch, t):
    ONESR = t["ONESB"][0:1, 0:128]

    def st06(eng):
        last = None
        for wnm, ps in (("Wq", "psq"), ("Wk", "psk"), ("Wa", "psa")):
            for cb in range(8):
                last = eng.matmul(t[ps][:, 0:512],
                                  t["h0f"][:, cb * 128:(cb + 1) * 128],
                                  t[wnm][:, cb * 512:(cb + 1) * 512],
                                  start=(cb == 0), stop=False,
                                  skip_group_check=True)
            u0 = {"Wq": "u0q", "Wk": "u0k", "Wa": "u0a"}[wnm]
            last = eng.matmul(t[ps][:, 0:512], ONESR, t[u0][:],
                              start=False, stop=(wnm != "Wa"),
                              skip_group_check=True)
            if wnm == "Wa":
                last = eng.matmul(t[ps][:, 0:512], ONESR, t["wabrow"][:],
                                  start=False, stop=False,
                                  skip_group_check=True)
                last = eng.matmul(t[ps][:, 0:512], ONESR, t["wablo"][:],
                                  start=False, stop=True,
                                  skip_group_check=True)
        return last
    ch.stage("tensor", st06,
             reads=("h0f", "Wq", "Wk", "Wa", "u0q", "u0k", "u0a",
                    "wabrow", "wablo", "ONESB"),
             writes=("psq", "psk", "psa"))


def emit_ab(ch, t):
    ch.stage("vector", lambda e: e.tensor_copy(t["qs"], t["psq"][:, 0:512]),
             reads=("psq",), writes=("sp1",))

    def st08(eng):
        SS(eng.tensor_mul(t["qk"], t["psk"][:, 0:512], t["qs"]))
        return eng.tensor_reduce(
            t["abpad"][:, 32:40],
            t["qk"].rearrange("p (h s) -> p h s", h=8), AX.X, ALU.add)
    ch.stage("vector", st08, reads=("psk", "sp1"), writes=("sp2", "abpad"))

    def st09(eng):
        SS(eng.activation(t["sp1"], t["psa"][:, 0:512], AF.Abs))
        SS(eng.activation(t["sp2"], t["sp1"], AF.Exp, scale=-1.0))
        return eng.activation(t["sp1"], t["sp2"], AF.Ln, bias=1.0)
    ch.stage("scalar", st09, reads=("psa", "constv", "sp1", "sp2"),
             writes=("sp1", "sp2"))

    def st10(eng):
        SS(eng.tensor_scalar(t["sp2"], t["psa"][:, 0:512], 0.0, None,
                             ALU.max))
        SS(eng.tensor_add(t["sp1"], t["sp1"], t["sp2"]))
        return eng.tensor_reduce(
            t["abpad"][:, 0:8],
            t["sp1"].rearrange("p (h s) -> p h s", h=8),
            AX.X, ALU.add, negate=True)
    ch.stage("vector", st10, reads=("psa", "sp1", "sp2"),
             writes=("sp1", "sp2", "abpad"))


def emit_ab_tr(ch, t):
    ch.stage("tensor", lambda e: e.transpose(
        t["pss"][0:64, 0:128], t["abpad"][:, 0:64], t["I128F"][:]),
        reads=("abpad", "I128F"), writes=("pss",))


def emit_gates(ch, t):
    def st12(eng):
        eng.tensor_copy(t["aT"], t["pss"][0:8, 0:128])
        return eng.tensor_copy(t["bT"], t["pss"][32:40, 0:128])
    ch.stage("vector", st12, reads=("pss",), writes=("aT", "bT"))

    def st13(eng):
        SS(eng.tensor_tensor_scan(t["acuU"][0:8, :], t["aT"],
                                  t["zeros8c"][:, 0:128], 0.0, ALU.add,
                                  ALU.add))
        SS(eng.tensor_sub(t["bmT"], t["bT"], t["acuU"][0:8, :]))
        SS(eng.tensor_copy(t["bmhi"], t["bmT"]))
        SS(eng.tensor_sub(t["bmlof"], t["bmT"], t["bmhi"]))
        return eng.tensor_copy(t["bmlo"], t["bmlof"])
    ch.stage("vector", st13, reads=("aT", "bT", "zeros8c"),
             writes=("acuU", "bmT", "bmhi", "bmlo", "bmlof"))

    def st19(eng):
        last = None
        for h in range(8):
            last = eng.matmul(t["psy"][:, h * 128:(h + 1) * 128],
                              t["OH8B"][:, h * 128:(h + 1) * 128],
                              t["bmhi"], start=True, stop=False,
                              skip_group_check=True)
            last = eng.matmul(t["psy"][:, h * 128:(h + 1) * 128],
                              t["OH8B"][:, h * 128:(h + 1) * 128],
                              t["bmlo"], start=False, stop=False,
                              skip_group_check=True)
            last = eng.matmul(t["psy"][:, h * 128:(h + 1) * 128],
                              t["I128B"][:], t["MASKB"][:],
                              start=False, stop=True,
                              skip_group_check=True)
        return last
    ch.stage("tensor", st19, reads=("OH8B", "bmhi", "bmlo", "MASKB",
                                    "I128B"),
             writes=("psy",))

    def st14(eng):
        SS(eng.activation(t["eAT"], t["acuU"][0:8, :], AF.Exp))
        return eng.activation(t["acuU"][32:40, :], t["bmT"], AF.Exp,
                              bias=t["acuU"][0:8, 127:128])
    ch.stage("scalar", st14, reads=("acuU", "bmT"), writes=("eAT", "acuU"))

    ch.stage("tensor", lambda e: e.transpose(
        t["pss"][0:128, 0:40], t["acuU"][:, :], t["I128F"][0:40, 0:40]),
        reads=("acuU", "I128F"), writes=("pss",))

    def st16(eng):
        eng.tensor_copy(t["acutm"], t["pss"][0:128, 0:8])
        eng.tensor_copy(t["utm"], t["pss"][0:128, 32:40])
        return eng.tensor_reduce(t["dZm"], t["acuU"][32:40, :], AX.X,
                                 ALU.add)
    ch.stage("vector", st16, reads=("pss", "acuU"),
             writes=("acutm", "utm", "dZm"))


def emit_v(ch, t):
    ONESR = t["ONESB"][0:1, 0:128]

    def st17(eng):
        last = None
        for hf in range(2):
            for cb in range(8):
                cbase = cb * D + hf * 512
                last = eng.matmul(t["psv"][:, hf * 512:(hf + 1) * 512],
                                  t["h0f"][:, cb * 128:(cb + 1) * 128],
                                  t["Wv"][:, cbase:cbase + 512],
                                  start=(cb == 0), stop=False,
                                  skip_group_check=True)
            last = eng.matmul(t["psv"][:, hf * 512:(hf + 1) * 512],
                              ONESR, t["u0v"][0:1, hf * 512:(hf + 1) * 512],
                              start=False, stop=True, skip_group_check=True)
        return last
    ch.stage("tensor", st17, reads=("h0f", "Wv", "u0v", "ONESB"),
             writes=("psv",))


def emit_warm(ch, t):
    ch.stage("tensor", lambda e: e.matmul(
        t["psq"][0:128, 0:512], t["ONESB"][0:1, 0:128],
        t["wabrow"][0:1, 0:512], start=True, stop=True,
        skip_group_check=True),
        reads=("ONESB", "wabrow"), writes=("psq",))


def emit_v_ev(ch, t):
    ch.stage("vector", lambda e: e.tensor_copy(t["Vb"], t["psv"][:]),
             reads=("psv",), writes=("Vb",))


def emit_score(ch, t):
    def st21(eng):
        last = None
        for h in range(8):
            last = eng.activation(t["Sall"][:, h * 128:(h + 1) * 128],
                                  t["psy"][:, h * 128:(h + 1) * 128],
                                  AF.Exp, bias=t["acutm"][:, h:h + 1])
        return last
    ch.stage("scalar", st21, reads=("psy", "acutm"), writes=("Sall",))

    ch.stage("vector", lambda e: e.tensor_reduce(
        t["zl"], t["Sall"].rearrange("p (h s) -> p h s", h=8),
        AX.X, ALU.add), reads=("Sall",), writes=("zl",))

    ch.stage("tensor", lambda e: e.transpose(t["pss"][0:8, 0:128],
                                             t["zl"], t["I128F"][:]),
             reads=("zl", "I128F"), writes=("pss",))

    def st24(eng):
        SS(eng.scalar_tensor_tensor(t["zT"], t["eAT"], t["Zst"][:, 0:1],
                                    t["pss"][0:8, 0:128], ALU.mult, ALU.add))
        SS(eng.reciprocal(t["rT"], t["zT"]))
        SS(eng.tensor_mul(t["etf"], t["eAT"], t["rT"]))
        return eng.tensor_copy(t["etb"], t["etf"])
    ch.stage("vector", st24, reads=("eAT", "Zst", "pss"),
             writes=("zT", "rT", "etf", "etb"))

    ch.stage("tensor", lambda e: e.transpose(t["pss"][0:128, 0:8],
                                             t["rT"], t["I128F"][0:8, 0:8]),
             reads=("rT", "I128F"), writes=("pss",))
    ch.stage("vector", lambda e: e.tensor_copy(t["rtm"],
                                               t["pss"][0:128, 0:8]),
             reads=("pss",), writes=("rtm",))


def emit_score_b(ch, t):
    def st27(eng):
        SS(eng.tensor_scalar(t["ssum"], t["Sall"][:, 0:128],
                             t["rtm"][:, 0:1], None, ALU.mult))
        for h in range(1, 8):
            SS(eng.scalar_tensor_tensor(t["ssum"],
                                        t["Sall"][:, h * 128:(h + 1) * 128],
                                        t["rtm"][:, h:h + 1], t["ssum"],
                                        ALU.mult, ALU.add))
        return eng.tensor_copy(t["ssb"], t["ssum"])
    ch.stage("vector", st27, reads=("Sall", "rtm"), writes=("ssum", "ssb"))

    ch.stage("tensor", lambda e: e.transpose(
        t["pss"][:].bitcast(BF16)[0:128, 0:128], t["ssb"], t["I128B"][:]),
        reads=("ssb", "I128B"), writes=("pss",))
    ch.stage("vector", lambda e: e.tensor_copy(
        t["sstsb"], t["pss"][:].bitcast(BF16)[0:128, 0:128]),
        reads=("pss",), writes=("sstsb",))


def emit_scan2(ch, t):
    def st29a(eng):
        last = None
        for cb in range(8):
            last = eng.matmul(t["psy"][:, cb * 128:(cb + 1) * 128],
                              t["Vb"][:, cb * 128:(cb + 1) * 128],
                              t["sstsb"], start=True, stop=False,
                              skip_group_check=True)
            last = eng.matmul(t["psy"][:, cb * 128:(cb + 1) * 128],
                              t["Nbf"][:, cb * 128:(cb + 1) * 128],
                              t["etb"], start=False, stop=True,
                              skip_group_check=True)
        last = eng.matmul(t["pss"][0:8, 0:512], t["utm"],
                          t["Vb"][:, 0:512], start=True, stop=True,
                          skip_group_check=True)
        return last
    ch.stage("tensor", st29a, reads=("Vb", "sstsb", "Nbf", "etb", "utm"),
             writes=("psy", "pss"))

    ch.stage("vector", lambda e: e.tensor_copy(t["yfm"], t["psy"][:]),
             reads=("psy",), writes=("yfm",))

    ch.stage("vector", lambda e: e.scalar_tensor_tensor(
        t["Nst"][:, 0:512], t["Nst"][:, 0:512], t["eAT"][:, 127:128],
        t["pss"][0:8, 0:512], ALU.mult, ALU.add),
        reads=("Nst", "eAT", "pss"), writes=("Nst",))

    ch.stage("tensor", lambda e: e.matmul(
        t["pss"][0:8, 0:512], t["utm"], t["Vb"][:, 512:1024],
        start=True, stop=True, skip_group_check=True),
        reads=("utm", "Vb"), writes=("pss",))

    def st30b(eng):
        SS(eng.scalar_tensor_tensor(t["Nst"][:, 512:1024],
                                    t["Nst"][:, 512:1024],
                                    t["eAT"][:, 127:128], t["pss"][0:8, 0:512],
                                    ALU.mult, ALU.add))
        return eng.scalar_tensor_tensor(t["Zst"][:], t["Zst"][:],
                                        t["eAT"][:, 127:128], t["dZm"],
                                        ALU.mult, ALU.add)
    ch.stage("vector", st30b, reads=("Nst", "eAT", "pss", "dZm", "Zst"),
             writes=("Nst", "Zst"))

    ch.stage("vector", lambda e: e.tensor_copy(t["Nbf"][:], t["Nst"][:]),
             reads=("Nst",), writes=("Nbf",))


def emit_wo(ch, t):
    def st34(eng):
        last = None
        for hf in range(2):
            for cb in range(8):
                cbase = cb * D + hf * 512
                last = eng.matmul(t["psy"][:, hf * 512:(hf + 1) * 512],
                                  t["yfm"][:, cb * 128:(cb + 1) * 128],
                                  t["Wo"][:, cbase:cbase + 512],
                                  start=(cb == 0), stop=(cb == 7),
                                  skip_group_check=True)
        return last
    ch.stage("tensor", st34, reads=("yfm", "Wo"), writes=("psy",))

    ch.stage("vector", lambda e: e.tensor_add(t["xt"], t["xt"],
                                              t["psy"][:]),
             reads=("xt", "psy"), writes=("xt",))


def emit_poll(eng, t, core, flslot, cfgslot, payload_dst, payload_src):
    rt = t["_regs"][f"_rt_{core}"]
    rx = t["_regs"][f"_rx_{core}"]
    re_ = t["_regs"][f"_re_{core}"]
    rd = t["_regs"][f"_rd_{core}"]
    po = t["_posem"]
    eng.load(re_, t["cfg"][0:1, cfgslot:cfgslot + 1])
    eng.reg_add(rt, rt, 16)
    eng.dma_start(t["flagsb"][:], t["_shfl"][0:1, flslot:flslot + 1]
                  ).then_inc(po, 16, skip_validation=True)
    eng.wait_ge(po, rt)
    eng.load(rx, t["flagsb"][0:1, 0:1])
    eng.reg_alu(rd, rx, re_, ALU.subtract)
    with eng.While(rd):
        eng.reg_add(rt, rt, 16)
        eng.dma_start(t["flagsb"][:], t["_shfl"][0:1, flslot:flslot + 1]
                      ).then_inc(po, 16, skip_validation=True)
        eng.wait_ge(po, rt)
        eng.load(rx, t["flagsb"][0:1, 0:1])
        eng.reg_alu(rd, rx, re_, ALU.subtract)
    eng.reg_add(rt, rt, 16)
    eng.dma_start(payload_dst, payload_src).then_inc(po, 16,
                                                     skip_validation=True)
    eng.wait_ge(po, rt)
    return eng.nop()


PARITY_SMALL = ['mv', 'bn12', 'lnv', 'rstd', 'abpad', 'aT', 'bT', 'bmT', 'acuU', 'eAT', 'acutm', 'utm', 'dZm', 'zl', 'zT', 'rT', 'etf', 'etb', 'rtm', 'ssum', 'ssb', 'sstsb', 'lg', 'm8', 'dd', 'ee', 'rr', 'w2', 'w1', 'mk1', 'mk2', 'wful', 'bmhi', 'bmlo', 'bmlof']


class ChP:
    __slots__ = ("ch", "rmap")

    def __init__(self, ch, rmap):
        self.ch = ch
        self.rmap = rmap

    def stage(self, ek, fn, reads=(), writes=(), dma=False, ndma=1):
        self.ch.stage(ek, fn,
                      tuple(self.rmap.get(r, r) for r in reads),
                      tuple(self.rmap.get(w, w) for w in writes),
                      dma=dma, ndma=ndma)


def set_parity(ch, t, p, arena_par, gatile_ok, bffn=False):
    tt = dict(t)
    rmap = {}
    for nm in PARITY_SMALL:
        if p == 0:
            tt[nm] = t["_par0"][nm]
        else:
            tt[nm] = t["_par1"][nm]
            rmap[nm] = nm + "_p1"
    if gatile_ok and p == 1:
        tt["xt"] = t["_par1"]["xt"]
        rmap["xt"] = "gatile"
    else:
        tt["xt"] = t["_par0"]["xt"]
    pa = p if arena_par else 0
    src_map = t["_parA0"] if pa == 0 else t["_parA1"]
    for nm in ("h0", "h0f", "sp1", "sp2", "Vb", "yfm", "E1", "Sall"):
        tt[nm] = src_map[nm]
        if pa != 0:
            rmap[nm] = nm + "_ar1"
    tt["qs"] = tt["sp1"]
    tt["qk"] = tt["sp2"]
    tt["h0b"] = tt["h0"].bitcast(BF16)[0:128, 0:1024]
    tt["h2lo"] = t["h2lo"][:]
    if bffn and p == 1:
        tt["h0f"] = t["_parA0"]["Vb"]
        rmap["h0f"] = "Vb"
        tt["h2lo"] = t["_parA0"]["yfm"]
        rmap["h2lo"] = "yfm"
    tt["_ch"] = ChP(ch, rmap)
    return tt


def emit_core(core, t, io, ch, sh):
    isl = core // 2
    role_a = (core % 2 == 0)
    i0 = ISL_START[isl]
    ilen = ISL_LEN[isl]
    ntile = ilen // 128
    pfx_total = i0
    half = (pfx_total // 2 // 512) * 512
    pfx_lo, pfx_hi = (0, half) if role_a else (half, pfx_total)
    npc = (pfx_hi - pfx_lo) // 512
    ONESR = t["ONESB"][0:1, 0:128]

    def init_rt(eng):
        for key, nm in (("_rt", "rt"), ("_rx", "rx"), ("_re", "re"),
                        ("_rd", "rd")):
            t["_regs"][f"{key}_{core}"] = eng.alloc_register(f"{nm}_{core}")
        eng.reg_mov(t["_regs"][f"_rt_{core}"], 0)
        return eng.nop()
    ch.stage("gpsimd", init_rt)

    def st_loads(eng):
        out = []
        for nm in ("I128F", "I128B", "MASK", "ONESB"):
            out.append(eng.dma_start(t[nm][:], io[nm]))
        out.append(eng.dma_start(t["OH8"], io["OH8"]))
        out.append(eng.dma_start(t["crow"], io["CVEC"]))
        out.append(eng.dma_start(t["constv"][:], io["CONSTV"]))
        out.append(eng.dma_start(t["cfg"][:], io["CFG"]))
        out.append(eng.dma_start(t["wabf"], io["WAB"]))
        if not role_a:
            out.append(eng.dma_start(
                t["gf"][:].rearrange("p (cb w) -> p cb w", cb=NCB),
                io["GT"].rearrange("(cb p) w -> p cb w", p=128)))
        return out
    nload = 10 if not role_a else 9
    ch.stage("sync", st_loads, dma=True, ndma=nload,
             writes=("I128F", "I128B", "MASK", "ONESB", "OH8", "crow",
                     "constv", "cfg", "wabf") + (("gf",) if not role_a
                                                 else ()))

    def st_memset(eng):
        eng.memset(t["zeros8c"][:], 0.0)
        eng.memset(t["Nst"][:], 0.0)
        eng.memset(t["Nbf"][:], 0.0)
        eng.memset(t["Zst"][:], 0.0)
        eng.memset(t["zrow8b"][:], 0.0)
        return eng.memset(t["Pprod"][:], 1.0)
    ch.stage("vector", st_memset,
             writes=("zeros8c", "Nst", "Nbf", "Zst", "zrow8b",
                     "Pprod"))

    ch.stage("gpsimd", lambda e: e.dma_start(t["wabrow"][:], t["wabf"]),
             dma=True, reads=("wabf",), writes=("wabrow",))

    def st_wab2(eng):
        SS(eng.tensor_sub(t["wabloF"], t["wabf"], t["wabrow"][:]))
        return eng.tensor_copy(t["wablo"][:], t["wabloF"])
    ch.stage("vector", st_wab2, reads=("wabf", "wabrow"),
             writes=("wabloF", "wablo"))

    ch.stage("scalar", lambda e: e.activation(t["scrow"], t["crow"], AF.Silu),
             reads=("crow",), writes=("scrow",))

    def st_sct(eng):
        last = None
        for cb in range(8):
            last = eng.transpose(t["pss"][0:128, cb:cb + 1],
                                 t["scrow"][0:1, cb * 128:(cb + 1) * 128],
                                 t["I128F"][0:1, 0:1])
        return last
    ch.stage("tensor", st_sct, reads=("scrow", "I128F"), writes=("pss",))
    ch.stage("vector", lambda e: e.tensor_copy(t["sc"][:], t["pss"][:, 0:8]),
             reads=("pss",), writes=("sc",))

    wmrs = ("wmr0A", "wmr1A") if role_a else ("wmr0B", "wmr1B")
    rbn = "rowbufA" if role_a else "rowbuf0"

    wlist = (("Wq", io["WQT"]), ("Wk", io["WKT"]), ("Wa", io["WAT"])) + (
        (("Wv", io["WVT"]), ("Wo", io["WOT"])) if role_a else ())
    wnames = tuple(nm for nm, _ in wlist)

    def emit_wload1():
        def st_wload1(eng):
            out = []
            for nm, src in wlist[:3]:
                out.append(eng.dma_start(
                    t[nm].rearrange("p (cb w) -> p cb w", cb=NCB),
                    src.rearrange("(cb p) w -> p cb w", p=128)))
            return out
        ch.stage("gpsimd", st_wload1, dma=True, ndma=3,
                 writes=("Wq", "Wk", "Wa"))

    def emit_wload2():
        if role_a:
            def st_wload2(eng):
                out = []
                for nm, src in wlist[3:]:
                    out.append(eng.dma_start(
                        t[nm].rearrange("p (cb w) -> p cb w", cb=NCB),
                        src.rearrange("(cb p) w -> p cb w", p=128)))
                return out
            ch.stage("gpsimd", st_wload2, dma=True, ndma=2,
                     writes=("Wv", "Wo", "crow", "scrow", "wabf",
                             "wabloF"))
        else:
            def st_eth(eng):
                out = []
                for e_ in (5, 6, 7):
                    out.append(eng.dma_start(
                        t["Et"].rearrange("p (e cb w) -> p e cb w",
                                          e=NEXP, cb=NCB)[:, e_],
                        io["EXPT"][e_].rearrange("(cb p) w -> p cb w",
                                                 p=128)))
                return out
            ch.stage("gpsimd", st_eth, dma=True, ndma=3, writes=("EtH",))

    for r in range(12):
        wmr = wmrs[r % 2]
        rb = rbn

        def st_wmr(eng, r=r, wmr=wmr):
            return eng.dma_start(
                t[wmr].rearrange("p (cb w) -> p cb w", cb=8),
                io["WMOD"].rearrange("(cb p) w -> p cb w", p=128)
                [:, :, r * 512:(r + 1) * 512])
        ch.stage("sync", st_wmr, dma=True, writes=(wmr,))

        if r:
            def st_modt(eng, r=r - 1, rb=rbn):
                last = None
                for i in range(4):
                    s = r * 4 + i
                    last = eng.transpose(
                        t["pss"][0:128, s:s + 1],
                        t[rb][0:1, i * 128:(i + 1) * 128],
                        t["I128F"][0:1, 0:1])
                return last
            ch.stage("tensor", st_modt, reads=(rbn, "I128F"),
                     writes=("pss",))

        def st_modr(eng, r=r, wmr=wmr):
            last = None
            for cb2 in range(8):
                last = eng.matmul(t["psq"][0:1, 0:512],
                                  t["sc"][:, cb2:cb2 + 1],
                                  t[wmr][:, cb2 * 512:(cb2 + 1) * 512],
                                  start=(cb2 == 0), stop=(cb2 == 7),
                                  skip_group_check=True)
            return last
        ch.stage("tensor", st_modr, reads=(wmr, "sc"), writes=("psq",))
        ch.stage("vector", lambda e, rb=rb: e.tensor_copy(
            t[rb][:], t["psq"][0:1, 0:512]), reads=("psq",), writes=(rb,))
        if r == 4:
            emit_wload1()
        elif r == 7:
            emit_wload2()

    def st_modt_last(eng):
        last = None
        for i in range(4):
            s = 44 + i
            last = eng.transpose(t["pss"][0:128, s:s + 1],
                                 t[rbn][0:1, i * 128:(i + 1) * 128],
                                 t["I128F"][0:1, 0:1])
        return last
    ch.stage("tensor", st_modt_last, reads=(rbn, "I128F"),
             writes=("pss",))
    ch.stage("vector", lambda e: e.tensor_copy(t["modT"][:],
                                               t["pss"][:, 0:48]),
             reads=("pss",), writes=("modT",))

    if not role_a:
        ch.stage("vector", lambda e: e.nop(),
                 writes=("h0pre", "shpk", "aTc", "bTc", "AcumTc", "UTe",
                         "Hsum", wmrs[0], wmrs[1], rbn))
    ch.stage("vector", lambda e: e.memset(t["Hsum"][:], 0.0),
             writes=("Hsum",))

    gcol = (16, 24) if role_a else (40, 48)

    def st_mod2(eng):
        eng.tensor_scalar(t["ca1"][:], t["modT"][:, 8:16], 1.0, None, ALU.add)
        if not role_a:
            eng.tensor_scalar(t["cb1f"][:], t["modT"][:, 32:40], 1.0, None,
                              ALU.add)
            eng.tensor_copy(t["sbf"][:], t["modT"][:, 24:32])
        SS(eng.tensor_scalar(t["t8a"][:], t["modT"][:, gcol[0]:gcol[1]], 0.0,
                             None, ALU.is_ge))
        SS(eng.tensor_scalar(t["t8a"][:], t["t8a"][:], 2.0, -1.0, ALU.mult,
                             ALU.add))
        SS(eng.tensor_scalar(t["t8b"][:], t["modT"][:, gcol[0]:gcol[1]], -1.0,
                             None, ALU.mult))
        SS(eng.tensor_max(t["t8b"][:], t["t8b"][:],
                          t["modT"][:, gcol[0]:gcol[1]]))
        SS(eng.tensor_scalar(t["t8b"][:], t["t8b"][:], 0.25, None, ALU.max))
        SS(eng.tensor_mul(t["gafm"][:], t["t8a"][:], t["t8b"][:]))
        SS(eng.tensor_copy(t["sab"][:], t["modT"][:, 0:8]))
        return eng.tensor_copy(t["sab_b"][:], t["sab"][:])
    ch.stage("vector", st_mod2, reads=("modT", "constv"),
             writes=("ca1", "cb1f", "sbf", "t8a", "t8b", "gafm", "sab",
                     "sab_b"))

    ch.stage("tensor", lambda e: e.transpose(t["pss"][0:8, 0:128],
                                             t["gafm"][:], t["I128F"][:]),
             reads=("gafm", "I128F"), writes=("pss",))
    ch.stage("vector", lambda e: e.tensor_copy(t["gaT"][:],
                                               t["pss"][0:8, 0:128]),
             reads=("pss",), writes=("gaT",))

    def st_gat(eng):
        last = None
        for cb in range(8):
            last = eng.matmul(t["psy"][:, cb * 128:(cb + 1) * 128],
                              t["OH8"][:, cb * 128:(cb + 1) * 128],
                              t["gaT"][:], start=True, stop=True,
                              skip_group_check=True)
        return last
    ch.stage("tensor", st_gat, reads=("OH8", "gaT"), writes=("psy",))
    ch.stage("vector", lambda e: e.tensor_copy(t["gatile"][:], t["psy"][:]),
             reads=("psy",), writes=("gatile",))

    if not role_a:
        def st_etl1(eng):
            out = []
            for e_ in (0, 1):
                out.append(eng.dma_start(
                    t["Et"].rearrange("p (e cb w) -> p e cb w",
                                      e=NEXP, cb=NCB)[:, e_],
                    io["EXPT"][e_].rearrange("(cb p) w -> p cb w", p=128)))
            return out
        ch.stage("gpsimd", st_etl1, dma=True, ndma=2,
                 writes=("EtL1", "crow", "scrow", "wabf", "wabloF",
                         "rowbuf0"))

    u0list = (("Wq", "u0q", FQ), ("Wk", "u0k", FQ), ("Wa", "u0a", FQ)) + (
        (("Wv", "u0v", D),) if role_a else ())

    def emit_folds():
      for wnm, unm, wid in u0list:
        for hf in range(wid // 512):
            def st_u0(eng, wnm=wnm, wid=wid, hf=hf):
                last = None
                for cb in range(8):
                    cbase = cb * wid + hf * 512
                    last = eng.matmul(t["psq"][0:1, 0:512],
                                      t["sab_b"][:, cb:cb + 1],
                                      t[wnm][:, cbase:cbase + 512],
                                      start=(cb == 0), stop=(cb == 7),
                                      skip_group_check=True)
                return last
            ch.stage("tensor", st_u0, reads=("sab_b", wnm), writes=("psq",))
            if wnm == "Wq":
                ch.stage("vector", lambda e: e.tensor_scalar(
                    t["u0q"][:], t["psq"][0:1, 0:512], 0.125, None, ALU.mult),
                    reads=("psq",), writes=("u0q",))
            else:
                ch.stage("vector", lambda e, unm=unm, hf=hf: e.tensor_copy(
                    t[unm][0:1, hf * 512:(hf + 1) * 512],
                    t["psq"][0:1, 0:512]), reads=("psq",), writes=(unm,))

      def st_scale(eng):
        last = None
        for cb in range(8):
            last = eng.tensor_scalar(
                t["Wq"][:, cb * 512:(cb + 1) * 512],
                t["Wq"][:, cb * 512:(cb + 1) * 512],
                t["ca1"][:, cb:cb + 1], 0.125, ALU.mult, ALU.mult)
            for nm in ("Wk", "Wa"):
                last = eng.tensor_scalar(
                    t[nm][:, cb * 512:(cb + 1) * 512],
                    t[nm][:, cb * 512:(cb + 1) * 512],
                    t["ca1"][:, cb:cb + 1], None, ALU.mult)
            if role_a:
                last = eng.tensor_scalar(
                    t["Wv"][:, cb * 1024:(cb + 1) * 1024],
                    t["Wv"][:, cb * 1024:(cb + 1) * 1024],
                    t["ca1"][:, cb:cb + 1], None, ALU.mult)
        return last
      ch.stage("vector", st_scale, reads=("ca1",) + wnames, writes=wnames)

    def emit_scale_wo():
        def st_swo(eng):
            last = None
            for cb in range(8):
                last = eng.tensor_mul(
                    t["Wo"][:, cb * 1024:(cb + 1) * 1024],
                    t["Wo"][:, cb * 1024:(cb + 1) * 1024], t["gatile"][:])
            return last
        ch.stage("vector", st_swo, reads=("gatile", "Wo"), writes=("Wo",))

    if not role_a:
        def st_cg(eng):
            last = None
            for cb in range(8):
                last = eng.tensor_copy(t["cghi"][:, cb * 8:(cb + 1) * 8],
                                       t["gf"][:, cb * 8:(cb + 1) * 8])
            SS(last)
            for cb in range(8):
                last = eng.tensor_sub(t["cgloF"][:, cb * 8:(cb + 1) * 8],
                                      t["gf"][:, cb * 8:(cb + 1) * 8],
                                      t["cghi"][:, cb * 8:(cb + 1) * 8])
            SS(last)
            return eng.tensor_copy(t["cglo"][:], t["cgloF"][:])
        ch.stage("vector", st_cg, reads=("gf",),
                 writes=("cghi", "cgloF", "cglo"))

    nsub = npc * 4

    def pfx_front(tt, r0):
        tt["_ch"].stage("sync", lambda e, r0=r0, t=tt: e.dma_start(
            t["xt"], io["XF"][r0:r0 + 128, :]), dma=True, writes=("xt",))

    def pfx_tr(tt, t4):
        c = tt["_ch"]
        emit_transposes_h0_b(c, tt)

        def p05(eng, t4=t4, t=tt):
            eng.tensor_copy(t["h0f"],
                            t["psv"][:].bitcast(BF16)[0:128, 0:1024])
            return eng.tensor_copy(t["h0pre"][:, t4 * D:(t4 + 1) * D],
                                   t["h0b"])
        c.stage("vector", p05, reads=("psv", "h0"), writes=("h0f", "h0pre"))

    def pfx_ab(tt, t4):
        c = tt["_ch"]
        emit_ab(c, tt)
        emit_ab_tr(c, tt)

        def p12(eng, t4=t4, t=tt):
            eng.tensor_copy(t["aTc"][:, t4 * 128:(t4 + 1) * 128],
                            t["pss"][0:8, 0:128])
            return eng.tensor_copy(t["bTc"][:, t4 * 128:(t4 + 1) * 128],
                                   t["pss"][32:40, 0:128])
        c.stage("vector", p12, reads=("pss",), writes=("aTc", "bTc"))

    def pfx_combine():
        def p14(eng):
            SS(eng.tensor_tensor_scan(t["AcumTc"][:], t["aTc"][:],
                                      t["zeros8c"][:], 0.0, ALU.add, ALU.add))
            return eng.tensor_sub(t["UTe"][:], t["bTc"][:], t["AcumTc"][:])
        ch.stage("vector", p14, reads=("aTc", "bTc", "zeros8c"),
                 writes=("AcumTc", "UTe"))

        def p15(eng):
            SS(eng.activation(t["UTe"][:], t["UTe"][:], AF.Exp,
                              bias=t["AcumTc"][:, 511:512]))
            return eng.activation(t["ePc"][:], t["AcumTc"][:, 511:512],
                                  AF.Exp)
        ch.stage("scalar", p15, reads=("UTe", "AcumTc"),
                 writes=("UTe", "ePc"))

        ch.stage("vector", lambda e: e.tensor_reduce(t["dZc"][:], t["UTe"][:],
                                                     AX.X, ALU.add),
                 reads=("UTe",), writes=("dZc",))

        def p17(eng):
            last = None
            for t4 in range(4):
                last = eng.transpose(t["pss"][0:128, t4 * 8:(t4 + 1) * 8],
                                     t["UTe"][:, t4 * 128:(t4 + 1) * 128],
                                     t["I128F"][0:8, 0:8])
            return last
        ch.stage("tensor", p17, reads=("UTe", "I128F"), writes=("pss",))
        ch.stage("vector", lambda e: e.tensor_copy(t["utm4"][:],
                                                   t["pss"][0:128, 0:32]),
                 reads=("pss",), writes=("utm4",))

        for hf in range(2):
            def p19(eng, hf=hf):
                last = None
                for t4 in range(4):
                    last = eng.matmul(
                        t["pss"][0:8, 0:512],
                        t["utm4"][:, t4 * 8:(t4 + 1) * 8],
                        t["h0pre"][:, t4 * D + hf * 512:
                                   t4 * D + hf * 512 + 512],
                        start=(t4 == 0), stop=(t4 == 3),
                        skip_group_check=True)
                return last
            ch.stage("tensor", p19, reads=("utm4", "h0pre"), writes=("pss",))

            def p20(eng, hf=hf):
                return eng.scalar_tensor_tensor(
                    t["Hsum"][:, hf * 512:(hf + 1) * 512],
                    t["Hsum"][:, hf * 512:(hf + 1) * 512],
                    t["ePc"][:, 0:1], t["pss"][0:8, 0:512],
                    ALU.mult, ALU.add)
            ch.stage("vector", p20, reads=("Hsum", "ePc", "pss"),
                     writes=("Hsum",))

        def p22(eng):
            SS(eng.scalar_tensor_tensor(t["Zst"][:], t["Zst"][:],
                                        t["ePc"][:, 0:1], t["dZc"][:],
                                        ALU.mult, ALU.add))
            return eng.tensor_mul(t["Pprod"][:], t["Pprod"][:], t["ePc"][:])
        ch.stage("vector", p22, reads=("Zst", "ePc", "dZc", "Pprod"),
                 writes=("Zst", "Pprod"))

    subs = [set_parity(ch, t, n % 2, arena_par=role_a, gatile_ok=role_a,
                       bffn=(not role_a))
            for n in range(nsub)]
    if nsub:
        pfx_front(subs[0], pfx_lo)
        emit_ln_b(subs[0]["_ch"], subs[0])
        pfx_tr(subs[0], 0)
        emit_folds()
        if role_a:
            emit_scale_wo()
        emit_qka(subs[0]["_ch"], subs[0])
        for n in range(nsub):
            if n + 1 < nsub:
                pfx_front(subs[n + 1], pfx_lo + (n + 1) * 128)
            pfx_ab(subs[n], n % 4)
            if n + 1 < nsub:
                emit_ln_b(subs[n + 1]["_ch"], subs[n + 1])
            if n % 4 == 3:
                pfx_combine()
            if n + 1 < nsub:
                pfx_tr(subs[n + 1], (n + 1) % 4)
                emit_qka(subs[n + 1]["_ch"], subs[n + 1])
    tctr = nsub

    if not role_a:
        if pfx_total:
            def b_pack(eng):
                eng.tensor_copy(t["shpk"][:, 0:D], t["Hsum"][:])
                eng.tensor_copy(t["shpk"][:, D:D + 1], t["Zst"][:])
                return eng.tensor_copy(t["shpk"][:, D + 1:D + 2],
                                       t["Pprod"][:])
            ch.stage("vector", b_pack, reads=("Hsum", "Zst", "Pprod"),
                     writes=("shpk",))
            ch.stage("gpsimd", lambda e: e.dma_start(sh["SHH"].ap(),
                                                     t["shpk"][:]),
                     dma=True, reads=("shpk",), writes=("SHHres",))
            ch.stage("gpsimd", lambda e: e.dma_start(
                t["_shfl"][0:1, 0:1], t["cfg"][0:1, 0:1]), dma=True,
                reads=("SHHres", "cfg"), writes=("SHFLres",))
    else:
        if pfx_total:
            ch.stage("gpsimd", lambda e: emit_poll(
                e, t, core, 0, 0, t["shpk"][:], sh["SHH"].ap()),
                reads=("cfg",), writes=("shpk", "flagsb"))

            def a_comb(eng):
                SS(eng.scalar_tensor_tensor(t["Hsum"][:], t["Hsum"][:],
                                            t["shpk"][:, D + 1:D + 2],
                                            t["shpk"][:, 0:D],
                                            ALU.mult, ALU.add))
                return eng.scalar_tensor_tensor(t["Zst"][:], t["Zst"][:],
                                                t["shpk"][:, D + 1:D + 2],
                                                t["shpk"][:, D:D + 1],
                                                ALU.mult, ALU.add)
            ch.stage("vector", a_comb, reads=("Hsum", "Zst", "shpk"),
                     writes=("Hsum", "Zst"))

            def pf1(eng):
                last = None
                for cb in range(8):
                    last = eng.transpose(
                        t["pss"][0:128, cb * 8:(cb + 1) * 8],
                        t["Hsum"][:, cb * 128:(cb + 1) * 128],
                        t["I128F"][0:8, 0:8])
                return last
            ch.stage("tensor", pf1, reads=("Hsum", "I128F"), writes=("pss",))
            ch.stage("vector", lambda e: e.tensor_copy(t["HsT"][:],
                                                       t["pss"][0:128, 0:64]),
                     reads=("pss",), writes=("HsT",))
            ch.stage("tensor", lambda e: e.transpose(
                t["pss"][0:1, 0:8], t["Zst"][:, 0:1], t["I128F"][0:8, 0:8]),
                reads=("Zst", "I128F"), writes=("pss",))
            ch.stage("vector", lambda e: e.tensor_copy(t["zrowb"][:],
                                                       t["pss"][0:1, 0:8]),
                     reads=("pss",), writes=("zrowb",))
            for hf in range(2):
                def pf3(eng, hf=hf):
                    last = None
                    for cb in range(8):
                        last = eng.matmul(
                            t["psq"][0:8, 0:512],
                            t["HsT"][:, cb * 8:(cb + 1) * 8],
                            t["Wv"][:, cb * D + hf * 512:
                                    cb * D + hf * 512 + 512],
                            start=(cb == 0), stop=False,
                            skip_group_check=True)
                    last = eng.matmul(t["psq"][0:8, 0:512], t["zrowb"][:],
                                      t["u0v"][0:1, hf * 512:(hf + 1) * 512],
                                      start=False, stop=True,
                                      skip_group_check=True)
                    return last
                ch.stage("tensor", pf3, reads=("HsT", "Wv", "zrowb", "u0v"),
                         writes=("psq",))
                ch.stage("vector", lambda e, hf=hf: e.tensor_copy(
                    t["Nst"][:, hf * 512:(hf + 1) * 512],
                    t["psq"][0:8, 0:512]), reads=("psq",), writes=("Nst",))
            ch.stage("vector", lambda e: e.tensor_copy(t["Nbf"][:],
                                                       t["Nst"][:]),
                     reads=("Nst",), writes=("Nbf",))

    if role_a:
        mts = [set_parity(ch, t, (tctr + j) % 2, arena_par=True,
                          gatile_ok=True) for j in range(ntile)]

        def a_front(j):
            tt = mts[j]
            r0 = i0 + j * 128
            tt["_ch"].stage("sync", lambda e, r0=r0, t=tt: e.dma_start(
                t["xt"], io["XF"][r0:r0 + 128, :]), dma=True,
                writes=("xt",))

        def a_tr(j):
            tt = mts[j]
            emit_transposes_h0_b(tt["_ch"], tt)
            tt["_ch"].stage("vector", lambda e, t=tt: e.tensor_copy(
                t["h0f"], t["psv"][:].bitcast(BF16)[0:128, 0:1024]),
                reads=("psv",), writes=("h0f",))

        def a_out(j):
            tt = mts[j]
            tt["_ch"].stage("gpsimd", lambda e, j=j, t=tt: e.dma_start(
                sh["SHX"].ap()[j * 128:(j + 1) * 128, :], t["xt"]),
                dma=True, reads=("xt",), writes=(f"SHXres{j}",))
            tt["_ch"].stage("gpsimd", lambda e, j=j: e.dma_start(
                t["_shfl"][0:1, 1 + j:2 + j], t["cfg"][0:1, 1 + j:2 + j]),
                dma=True, reads=(f"SHXres{j}", "cfg"), writes=("SHFLres",))

        xtsl = [(t["_par0"]["xt"], "xt"), (t["_par1"]["xt"], "gatile"),
                (t["xt3"][:], "xt3")]
        for j in range(ntile):
            ap, nm = xtsl[j % 3]
            mts[j]["xt"] = ap
            mts[j]["_ch"].rmap["xt"] = nm
        ch.stage("vector", lambda e: e.nop(), writes=("xt3", "wmr0A"))
        def st_oh8b(eng):
            SS(eng.tensor_copy(t["OH8B"][:], t["OH8"]))
            return eng.tensor_copy(t["MASKB"][:], t["MASK"][:])
        ch.stage("vector", st_oh8b,
                 reads=("OH8", "MASK", "wmr0A"),
                 writes=("OH8B", "MASKB", "bmhi", "bmlo", "bmlof",
                         "bmhi_p1", "bmlo_p1", "bmlof_p1"))

        a_front(0)
        emit_ln_b(mts[0]["_ch"], mts[0])
        a_tr(0)
        if pfx_total == 0:
            emit_folds()
            emit_scale_wo()
        emit_qka(mts[0]["_ch"], mts[0])
        emit_ab(mts[0]["_ch"], mts[0])
        for j in range(ntile):
            tt = mts[j]
            if j + 1 < ntile:
                a_front(j + 1)
            emit_v(tt["_ch"], tt)
            emit_ab_tr(tt["_ch"], tt)
            emit_gates(tt["_ch"], tt)
            emit_v_ev(tt["_ch"], tt)
            if j + 1 < ntile:
                emit_ln_b(mts[j + 1]["_ch"], mts[j + 1])
                a_tr(j + 1)
                emit_qka(mts[j + 1]["_ch"], mts[j + 1])
            emit_score(tt["_ch"], tt)
            emit_score_b(tt["_ch"], tt)
            emit_scan2(tt["_ch"], tt)
            if j + 1 < ntile:
                emit_ab(mts[j + 1]["_ch"], mts[j + 1])
            emit_wo(tt["_ch"], tt)
            a_out(j)
        tctr += ntile
        ch.stage("gpsimd", lambda e: e.nop(), reads=("SHFLres",))
    else:
        def st_etl2(eng):
            out = []
            for e_ in (2, 3, 4):
                out.append(eng.dma_start(
                    t["Et"].rearrange("p (e cb w) -> p e cb w",
                                      e=NEXP, cb=NCB)[:, e_],
                    io["EXPT"][e_].rearrange("(cb p) w -> p cb w", p=128)))
            return out
        ch.stage("gpsimd", st_etl2, dma=True, ndma=3,
                 writes=("EtL2", "wmr0B", "wmr1B", "h0pre", "shpk", "aTc",
                         "bTc", "AcumTc", "UTe", "Hsum", "OH8", "E1",
                         "Sall"))

        for e_ in range(NEXP):
            def st_egb(eng, e_=e_):
                last = None
                for cb in range(8):
                    base = (e_ * NCB + cb) * D
                    last = eng.tensor_mul(t["Et"][:, base:base + D],
                                          t["Et"][:, base:base + D],
                                          t["gatile"][:])
                return last
            ch.stage("vector", st_egb, reads=(ET_RES[e_], "gatile"),
                     writes=(ET_RES[e_],))

        bts = [set_parity(ch, t, (tctr + j) % 2, arena_par=False,
                          gatile_ok=True, bffn=True) for j in range(ntile)]
        tctr += ntile

        def b_poll(j):
            tt = bts[j]
            tt["_ch"].stage("gpsimd", lambda e, j=j, t=tt: emit_poll(
                e, t, core, 1 + j, 1 + j, t["xt"],
                sh["SHX"].ap()[j * 128:(j + 1) * 128, :]),
                reads=("cfg",), writes=("xt", "flagsb"))

        def b_trb(j):
            tt = bts[j]
            c = tt["_ch"]
            emit_transposes_h0(c, tt)

            def f05(eng, t=tt):
                last = None
                for cb in range(8):
                    last = eng.tensor_scalar(
                        t["h0"][:, cb * 128:(cb + 1) * 128],
                        t["psv"][:, cb * 128:(cb + 1) * 128],
                        t["cb1f"][:, cb:cb + 1], t["sbf"][:, cb:cb + 1],
                        ALU.mult, ALU.add)
                SS(last)
                SS(eng.tensor_copy(t["h0f"], t["h0"]))
                return eng.tensor_sub(t["h2lo"], t["h0"], t["h0f"])
            c.stage("vector", f05, reads=("psv", "cb1f", "sbf", "h0"),
                    writes=("h0", "h0f", "h2lo"))

        def b_lg(j):
            tt = bts[j]
            c = tt["_ch"]

            def f_lg(eng, t=tt):
                last = None
                first = True
                for (mat, gw) in ((t["h0f"], "cghi"), (t["h0f"], "cglo"),
                                  (t["h2lo"], "cghi")):
                    for cb in range(8):
                        last = eng.matmul(t["psq"][0:128, 0:8],
                                          mat[:, cb * 128:(cb + 1) * 128],
                                          t[gw][:, cb * 8:(cb + 1) * 8],
                                          start=first, stop=False,
                                          skip_group_check=True)
                        first = False
                last = eng.matmul(t["psq"][0:128, 0:8], ONESR,
                                  t["zrow8b"][:], start=False, stop=True,
                                  skip_group_check=True)
                return last
            c.stage("tensor", f_lg,
                    reads=("h0f", "h2lo", "cghi", "cglo", "zrow8b",
                           "ONESB"),
                    writes=("psq",))
            c.stage("vector", lambda e, t=tt: e.tensor_copy(
                t["lg"], t["psq"][0:128, 0:8]),
                reads=("psq",), writes=("lg",))

        def b_top2(j):
            tt = bts[j]
            c = tt["_ch"]

            def f_top2(eng, t=tt):
                SS(eng.max(t["m8"], t["lg"]))
                return eng.tensor_sub(t["dd"], t["m8"][:, 1:2],
                                      t["m8"][:, 0:1])
            c.stage("vector", f_top2, reads=("lg",), writes=("m8", "dd"))
            c.stage("scalar", lambda e, t=tt: e.activation(t["ee"], t["dd"],
                                                           AF.Exp),
                    reads=("dd",), writes=("ee",))

            def f_w12(eng, t=tt):
                SS(eng.tensor_scalar(t["rr"], t["ee"], 1.0, None,
                                     ALU.add))
                SS(eng.reciprocal(t["rr"], t["rr"]))
                SS(eng.tensor_mul(t["w2"], t["ee"], t["rr"]))
                SS(eng.tensor_scalar(t["w1"], t["w2"], -1.0, 1.0,
                                     ALU.mult, ALU.add))
                SS(eng.tensor_scalar(t["mk1"], t["lg"], t["m8"][:, 0:1],
                                     None, ALU.is_equal))
                SS(eng.tensor_scalar(t["mk2"], t["lg"], t["m8"][:, 1:2],
                                     None, ALU.is_equal))
                SS(eng.tensor_scalar(t["mk1"], t["mk1"],
                                     t["w1"][:, 0:1], None, ALU.mult))
                return eng.scalar_tensor_tensor(t["wful"], t["mk2"],
                                                t["w2"][:, 0:1], t["mk1"],
                                                ALU.mult, ALU.add)
            c.stage("vector", f_w12, reads=("ee", "lg", "m8", "constv"),
                    writes=("rr", "w2", "w1", "mk1", "mk2", "wful"))

        def b_experts(j):
            tt = bts[j]
            c = tt["_ch"]
            for e_ in range(NEXP):
                psn = "psv" if e_ % 2 == 0 else "psy"

                def f_em(eng, e_=e_, psn=psn, t=tt):
                    last = None
                    for hf in range(2):
                        for cb in range(8):
                            base = (e_ * NCB + cb) * D + hf * 512
                            last = eng.matmul(
                                t[psn][:, hf * 512:(hf + 1) * 512],
                                t["h0f"][:, cb * 128:(cb + 1) * 128],
                                t["Et"][:, base:base + 512],
                                start=(cb == 0), stop=(cb == 7),
                                skip_group_check=True)
                    return last
                c.stage("tensor", f_em, reads=("h0f", ET_RES[e_]),
                        writes=(psn,))
                c.stage("vector",
                        lambda eng, e_=e_, psn=psn, t=tt:
                        eng.scalar_tensor_tensor(
                            t["xt"], t[psn][:], t["wful"][:, e_:e_ + 1],
                            t["xt"], ALU.mult, ALU.add),
                        reads=(psn, "wful", "xt"), writes=("xt",))

        def b_out(j):
            tt = bts[j]
            tt["_ch"].stage("gpsimd", lambda e, j=j, t=tt: e.dma_start(
                io["OUT"][j * 128:(j + 1) * 128, :], t["xt"]), dma=True,
                reads=("xt",), writes=(f"OUTres{j}",))

        b_poll(0)
        emit_ln(bts[0]["_ch"], bts[0])
        b_trb(0)
        b_lg(0)
        b_top2(0)
        for j in range(ntile):
            if j + 1 < ntile:
                b_poll(j + 1)
                emit_ln(bts[j + 1]["_ch"], bts[j + 1])
            b_experts(j)
            b_out(j)
            if j + 1 < ntile:
                b_trb(j + 1)
                b_lg(j + 1)
                b_top2(j + 1)
        ch.stage("gpsimd", lambda e: e.nop(),
                 reads=tuple(f"OUTres{j}" for j in range(ntile)))


def build_nc():
    from contextlib import ExitStack
    nc = bass.Bass("TRN2", target_bir_lowering=False)
    dram = nc.dram_tensor
    io = {}
    for nm, sh_, dt in [
        ("XF", [S, D], F32), ("WQT", [D, FQ], F32), ("WKT", [D, FQ], F32),
        ("WAT", [D, FQ], F32), ("WVT", [D, D], F32), ("WOT", [D, D], F32),
        ("WAB", [1, FQ], F32), ("GT", [D, NEXP], F32),
        ("EXPT", [NEXP, D, D], F32), ("WMOD", [D, 6 * D], F32),
        ("CVEC", [1, D], F32),
        ("I128F", [128, 128], F32), ("I128B", [128, 128], BF16),
        ("MASK", [128, 128], F32), ("OH8", [8, 8 * 128], F32),
        ("ONESB", [1, 128], BF16), ("CONSTV", [128, 8], F32),
        ("CFG", [1, 64], I32),
    ]:
        io[nm] = dram(nm, sh_, dt, kind="ExternalInput").ap()
    io["OUT"] = dram("OUT", [max(ISL_LEN), D], F32,
                     kind="ExternalOutput").ap()

    sh = {
        "SHX": dram("SHX", [max(ISL_LEN), D], F32, kind="Internal",
                    addr_space="Shared"),
        "SHH": dram("SHH", [8, D + 8], F32, kind="Internal",
                    addr_space="Shared"),
        "SHFL": dram("SHFL", [1, 64], I32, kind="Internal",
                     addr_space="Shared"),
    }

    es = ExitStack()
    t = {}

    def SB(name, shape, dt):
        t[name] = es.enter_context(nc.sbuf_tensor('sb_' + name, shape, dt))

    def PSUM(name, shape, dt):
        t[name] = es.enter_context(nc.psum_tensor('ps_' + name, shape, dt))

    SB("arena", [128, 77824], BF16)
    ar = t["arena"][:]
    t["Et"] = ar[:, 12288:77824]
    t["Wq"] = ar[:, 0:4096]
    t["Wk"] = ar[:, 4096:8192]
    t["Wa"] = ar[:, 8192:12288]
    t["Wv"] = ar[:, 12288:20480]
    t["Wo"] = ar[:, 20480:28672]
    arf = ar.bitcast(F32)
    t["wmr0"] = arf[:, 0:4096]
    t["wmr1"] = arf[:, 4096:8192]
    t["wmr0A"] = arf[:, 30208:34304]
    t["wmr1A"] = arf[:, 34304:38400]
    t["rowbufA"] = arf[0:1, 38400:38912]
    t["wmr0B"] = arf[:, 14336:18432]
    t["wmr1B"] = arf[:, 18432:22528]
    t["xt3"] = arf[:, 30208:31232]
    t["crow"] = arf[0:1, 8192:9216]
    t["scrow"] = arf[0:1, 9216:10240]
    t["wabf"] = arf[0:1, 10240:10752]
    t["wabloF"] = arf[0:1, 10752:11264]
    t["rowbuf0"] = arf[0:1, 11264:11776]
    t["rowbuf1"] = arf[0:1, 11776:12288]
    t["h0pre"] = ar[:, 28672:32768]
    t["shpk"] = arf[0:8, 16384:17416]
    t["aTc"] = arf[0:8, 17920:18432]
    t["bTc"] = arf[0:8, 18432:18944]
    t["AcumTc"] = arf[0:8, 18944:19456]
    t["UTe"] = arf[0:8, 19456:19968]
    t["Hsum"] = arf[0:8, 19968:20992]
    t["E1"] = arf[:, 21504:22528]
    t["Sall"] = arf[:, 22528:23552]
    t["OH8"] = arf[0:8, 23552:24576]

    for n, sh_, dt in [
        ("I128F", [128, 128], F32), ("I128B", [128, 128], BF16),
        ("MASK", [128, 128], F32),
        ("ONESB", [1, 128], BF16), ("zeros8c", [8, 512], F32),
        ("constv", [128, 8], F32), ("cfg", [1, 64], I32),
        ("flagsb", [1, 1], I32),
        ("sc", [128, 8], F32), ("modT", [128, 48], F32),
        ("ca1", [128, 8], F32), ("sab", [128, 8], F32),
        ("sab_b", [128, 8], BF16), ("cb1f", [128, 8], F32),
        ("sbf", [128, 8], F32),
        ("gafm", [128, 8], F32), ("gaT", [8, 128], F32),
        ("gatile", [128, D], F32),
        ("t8a", [128, 8], F32), ("t8b", [128, 8], F32),
        ("wabrow", [1, FQ], BF16), ("wablo", [1, FQ], BF16),
        ("u0q", [1, FQ], BF16), ("u0k", [1, FQ], BF16),
        ("u0a", [1, FQ], BF16), ("u0v", [1, D], BF16),
        ("zrow8b", [1, 8], BF16),
        ("Nst", [8, D], F32), ("Nbf", [8, D], BF16), ("Zst", [8, 1], F32),
        ("Pprod", [8, 1], F32),
        ("gf", [128, NCB * 8], F32), ("cghi", [128, NCB * 8], BF16),
        ("cglo", [128, NCB * 8], BF16), ("cgloF", [128, NCB * 8], F32),
        ("xt", [128, D], F32), ("h0", [128, D], F32),
        ("h0f", [128, D], BF16), ("h2lo", [128, D], BF16),
        ("sp1", [128, FQ], F32), ("sp2", [128, FQ], F32),
        ("mv", [128, 2], F32), ("bn12", [128, 12], F32),
        ("lnv", [128, 1], F32), ("rstd", [128, 1], F32),
        ("abpad", [128, 64], F32),
        ("aT", [8, 128], F32), ("bT", [8, 128], F32), ("bmT", [8, 128], F32),
        ("acuU", [40, 128], F32), ("eAT", [8, 128], F32),
        ("acutm", [128, 8], F32), ("utm", [128, 8], BF16),
        ("zl", [128, 8], F32), ("zT", [8, 128], F32), ("rT", [8, 128], F32),
        ("etf", [8, 128], F32), ("etb", [8, 128], BF16),
        ("rtm", [128, 8], F32),
        ("ssum", [128, 128], F32), ("ssb", [128, 128], BF16),
        ("sstsb", [128, 128], BF16),
        ("Vb", [128, D], BF16), ("yfm", [128, D], BF16),
        ("utm4", [128, 32], BF16), ("ePc", [8, 1], F32),
        ("dZc", [8, 1], F32), ("dZm", [8, 1], F32),
        ("HsT", [128, 64], BF16), ("zrowb", [1, 8], BF16),
        ("lg", [128, 8], F32), ("m8", [128, 8], F32),
        ("dd", [128, 1], F32), ("ee", [128, 1], F32), ("rr", [128, 1], F32),
        ("w2", [128, 1], F32), ("w1", [128, 1], F32),
        ("mk1", [128, 8], F32), ("mk2", [128, 8], F32),
        ("wful", [128, 8], F32),
    ]:
        SB(n, sh_, dt)

    _PARS = PARITY_SMALL
    _shapes = {
        "mv": ([128, 2], F32), "bn12": ([128, 12], F32),
        "lnv": ([128, 1], F32), "rstd": ([128, 1], F32),
        "abpad": ([128, 64], F32), "aT": ([8, 128], F32),
        "bT": ([8, 128], F32), "bmT": ([8, 128], F32),
        "acuU": ([40, 128], F32), "eAT": ([8, 128], F32),
        "acutm": ([128, 8], F32), "utm": ([128, 8], BF16),
        "dZm": ([8, 1], F32), "zl": ([128, 8], F32),
        "zT": ([8, 128], F32), "rT": ([8, 128], F32),
        "etf": ([8, 128], F32), "etb": ([8, 128], BF16),
        "rtm": ([128, 8], F32), "ssum": ([128, 128], F32),
        "ssb": ([128, 128], BF16), "sstsb": ([128, 128], BF16),
        "lg": ([128, 8], F32), "m8": ([128, 8], F32),
        "dd": ([128, 1], F32), "ee": ([128, 1], F32),
        "rr": ([128, 1], F32), "w2": ([128, 1], F32),
        "w1": ([128, 1], F32), "mk1": ([128, 8], F32),
        "mk2": ([128, 8], F32), "wful": ([128, 8], F32),
    }
    for _n in _PARS:
        if _n in ("bmhi", "bmlo", "bmlof"):
            continue
        _shp, _dt = _shapes[_n]
        SB(_n + "_p1", _shp, _dt)
    t["OH8B"] = ar[0:8, 62464:63488]
    t["MASKB"] = ar[0:128, 64512:64640]
    t["bmhi"] = ar[0:8, 63488:63616]
    t["bmlo"] = ar[0:8, 63616:63744]
    t["bmhi_p1"] = ar[0:8, 63744:63872]
    t["bmlo_p1"] = ar[0:8, 63872:64000]
    t["bmlof"] = arf[0:8, 32000:32128]
    t["bmlof_p1"] = arf[0:8, 32128:32256]
    t["_par0"] = {n: t[n][:] for n in _PARS}
    t["_par1"] = {n: t[n + "_p1"][:] for n in _PARS}
    t["_par0"]["xt"] = t["xt"][:]
    t["_par1"]["xt"] = t["gatile"][:]
    t["_parA0"] = {
        "h0": t["h0"][:], "h0f": t["h0f"][:],
        "sp1": t["sp1"][:], "sp2": t["sp2"][:],
        "Vb": t["Vb"][:], "yfm": t["yfm"][:],
        "E1": t["E1"], "Sall": t["Sall"],
    }
    t["_parA1"] = {
        "h0": arf[:, 24576:25600], "h0f": ar[:, 57344:58368],
        "sp1": arf[:, 25600:26112], "sp2": arf[:, 26112:26624],
        "Vb": ar[:, 58368:59392], "yfm": ar[:, 59392:60416],
        "E1": arf[:, 26624:27648], "Sall": arf[:, 27648:28672],
    }

    t["qs"] = t["sp1"][:]
    t["qk"] = t["sp2"][:]

    for _i, _v in enumerate([EPS_LN, 1.0, -0.5, -1.0, 0.0, 0.25, 2.0, -1e30]):
        nc.const_aps.register(F32, _v, t["constv"][0:128, _i:_i + 1])

    PSUM("psq", [128, 512], F32)
    PSUM("psk", [128, 512], F32)
    PSUM("psa", [128, 512], F32)
    PSUM("psv", [128, 1024], F32)
    PSUM("psy", [128, 1024], F32)
    PSUM("pss", [128, 512], F32)

    t["_posem"] = es.enter_context(nc.semaphore("PO"))
    t["_shfl"] = sh["SHFL"].ap()
    t["_regs"] = {}
    sem_h = {k: es.enter_context(nc.semaphore("SEM_" + k))
             for k in ("T", "V", "S", "GC", "GD", "SYD", "SYP", "GP")}
    tick_h = {ek: es.enter_context(nc.semaphore("TK_" + ek))
              for ek in ("sync", "tensor", "vector", "scalar", "gpsimd")}

    progs = []
    for core in range(N_CORES):
        prog = Prog()
        sems = {k: [sem_h[k], 0] for k in sem_h}
        ticks = {ek: [tick_h[ek], 0] for ek in tick_h}
        chc = Sched(prog, sems, ticks)
        emit_core(core, t, io, chc, sh)
        progs.append(prog)

    with nc.Block() as blk:
        def mk(ek):
            def body(eng):
                pid = eng.partition_id()
                for c in eng.Switch(pid, 8):
                    for f in progs[c].q[ek]:
                        f(eng)
            return body
        blk.sync(mk("sync"))
        blk.tensor(mk("tensor"))
        blk.vector(mk("vector"))
        blk.scalar(mk("scalar"))
        blk.gpsimd(mk("gpsimd"))

    es.close()
    return nc


_NC_CACHE = {}


def kernel(L0_attn_x, L0_attn_W_q, L0_attn_W_k, L0_attn_W_v,
           L0_attn_W_alpha_w, L0_attn_W_alpha_b, L0_attn_W_o,
           L0_ffn_gate_w, L0_ffn_expert_w, L0_c, L0_wmod):
    x = np.ascontiguousarray(np.asarray(L0_attn_x).reshape(S, D),
                             dtype=np.float32)
    i128 = np.eye(128, dtype=np.float32)
    mask = np.triu(np.ones((128, 128), np.float32), k=1) * np.float32(-1e30)
    oh = np.zeros((8, 8 * 128), np.float32)
    for h in range(8):
        oh[h, h * 128:(h + 1) * 128] = 1.0
    nonce = int(np.random.randint(1, 2 ** 30))
    cfg = np.zeros((1, 64), np.int32)
    cfg[0, 0] = nonce
    for j in range(max(ISL_LEN) // 128):
        cfg[0, 1 + j] = nonce + 1 + j

    if "nc" not in _NC_CACHE:
        _NC_CACHE["nc"] = build_nc()
    nc = _NC_CACHE["nc"]

    def asnp(a):
        return np.ascontiguousarray(np.asarray(a), dtype=np.float32)

    base = dict(
        XF=x,
        WQT=asnp(np.asarray(L0_attn_W_q).T),
        WKT=asnp(np.asarray(L0_attn_W_k).T),
        WAT=asnp(np.asarray(L0_attn_W_alpha_w).T),
        WVT=asnp(np.asarray(L0_attn_W_v).T),
        WOT=asnp(np.asarray(L0_attn_W_o).T),
        WAB=asnp(np.asarray(L0_attn_W_alpha_b).reshape(1, FQ)),
        GT=asnp(np.asarray(L0_ffn_gate_w).T),
        EXPT=asnp(np.asarray(L0_ffn_expert_w).transpose(0, 2, 1)),
        WMOD=asnp(L0_wmod),
        CVEC=asnp(np.asarray(L0_c).reshape(1, D)),
        I128F=i128, I128B=i128.astype(ml_dtypes.bfloat16), MASK=mask, OH8=oh,
        ONESB=np.ones((1, 128), ml_dtypes.bfloat16),
        CONSTV=np.tile(np.array([[EPS_LN, 1.0, -0.5, -1.0, 0.0, 0.25, 2.0,
                                  -1e30]], np.float32), (128, 1)),
        CFG=cfg,
    )
    in_maps = [dict(base) for _ in range(N_CORES)]
    res = run_bass_kernel_spmd(nc, in_maps, core_ids=list(range(N_CORES)))
    parts = [res.results[2 * k + 1]["OUT"][:ISL_LEN[k]] for k in range(4)]
    out = np.concatenate(parts, axis=0).reshape(1, S, D).astype(np.float32)
    return out


# version 27
# speedup vs baseline: 1.0073x; 1.0073x over previous
import numpy as np
import ml_dtypes

import concourse.bass as bass
import concourse.mybir as mybir
from concourse.bass_utils import run_bass_kernel_spmd

F32, BF16, I32 = mybir.dt.float32, mybir.dt.bfloat16, mybir.dt.int32
AF = mybir.ActivationFunctionType
ALU = mybir.AluOpType
AX = mybir.AxisListType

D = 1024
FQ = 512
NCB = 8
NEXP = 8
S = 16384
EPS_LN = 1e-5
N_CORES = 8

ISL_START = [0, 4096, 8192, 12288]
ISL_LEN = [4096, 4096, 4096, 4096]

SS = None

WLOAD_ALIASES = ("wmr0", "wmr1", "crow", "scrow", "wabf", "wabloF",
                 "rowbuf0", "rowbuf1")
ET_RES = tuple("EtL1" if e < 2 else ("EtL2" if e < 5 else "EtH")
               for e in range(8))
ET_ALIASES = ("h0pre", "shpk", "aTc", "bTc", "AcumTc", "UTe", "Hsum",
              "OH8", "E1", "Sall", "wmr0", "wmr1", "crow", "scrow",
              "wabf", "wabloF", "rowbuf0", "rowbuf1")


class Prog:
    def __init__(self):
        self.q = {e: [] for e in ("sync", "tensor", "vector", "scalar",
                                  "gpsimd")}

    def add(self, ek, f):
        self.q[ek].append(f)


class Sched:

    def __init__(self, prog, sems, ticks):
        self.p = prog
        self.sems = sems
        self.ticks = ticks
        self.res = {}
        self.rmap = {}

    def key_for(self, ek, dma):
        if ek == "sync":
            return "SYD"
        if ek == "gpsimd":
            return "GD" if dma else "GC"
        return {"tensor": "T", "vector": "V", "scalar": "S"}[ek]

    def stage(self, ek, fn, reads=(), writes=(), dma=False, ndma=1):
        key = self.key_for(ek, dma)
        pub_key = {"sync": "SYP", "gpsimd": "GP"}.get(ek) if dma else None
        deps = {}

        def add_dep(k2, v2):
            if v2 > deps.get(k2, -1):
                deps[k2] = v2

        reads = tuple(self.rmap.get(r, r) for r in reads)
        writes = tuple(self.rmap.get(w, w) for w in writes)
        for r in reads:
            rec = self.res.get(r)
            if rec and rec["w"]:
                add_dep(*rec["w"])
        for w in writes:
            rec = self.res.get(w)
            if rec:
                if rec["w"]:
                    add_dep(*rec["w"])
                for rd in rec["r"]:
                    add_dep(*rd)
        inc = 16 * ndma if dma else 1
        end_val = self.sems[key][1] + inc
        self.sems[key][1] = end_val
        if dma:
            pub_val = self.sems[pub_key][1] + 1
            self.sems[pub_key][1] = pub_val
            done = (pub_key, pub_val)
        else:
            done = (key, end_val)
        for w in writes:
            self.res[w] = {"w": done, "r": []}
        for r in reads:
            rec = self.res.setdefault(r, {"w": None, "r": []})
            rec["r"].append(done)

        handle = {k: self.sems[k][0] for k in self.sems}
        tick = self.ticks[ek]
        dep_items = sorted(deps.items())
        my_handle = self.sems[key][0]
        pub_handle = self.sems[pub_key][0] if dma else None

        def run(eng):
            global SS

            def ss(inst):
                tick[1] += 1
                inst.then_inc(tick[0], 1, skip_validation=True)
                eng.wait_ge(tick[0], tick[1])
                return inst
            SS = ss
            for k2, v2 in dep_items:
                eng.wait_ge(handle[k2], v2)
            ret = fn(eng)
            if isinstance(ret, list):
                assert dma and len(ret) == ndma, (dma, ndma, len(ret))
                for inst in ret:
                    inst.then_inc(my_handle, 16, skip_validation=True)
            elif dma:
                assert ndma == 1
                ret.then_inc(my_handle, 16, skip_validation=True)
            else:
                ret.then_inc(my_handle, 1, skip_validation=True)
            if dma:
                eng.wait_ge(my_handle, end_val)
                eng.nop().then_inc(pub_handle, 1, skip_validation=True)
        self.p.add(ek, run)


def emit_ln(ch, t):
    def st01(eng):
        eng.bn_stats(t["bn12"][:, 0:6], t["xt"][:, 0:512])
        SS(eng.bn_stats(t["bn12"][:, 6:12], t["xt"][:, 512:1024]))
        return eng.bn_aggr(t["mv"], t["bn12"])
    ch.stage("vector", st01, reads=("xt",), writes=("bn12", "mv"))

    def st02(eng):
        SS(eng.activation(t["lnv"], t["mv"][:, 1:2], AF.Ln, bias=EPS_LN))
        return eng.activation(t["rstd"], t["lnv"], AF.Exp, scale=-0.5)
    ch.stage("scalar", st02, reads=("mv", "constv"), writes=("lnv", "rstd"))

    ch.stage("vector", lambda e: e.tensor_scalar(
        t["h0"], t["xt"], t["mv"][:, 0:1], t["rstd"][:, 0:1],
        ALU.subtract, ALU.mult),
        reads=("xt", "mv", "rstd"), writes=("h0",))


def emit_transposes_h0(ch, t):
    def st04(eng):
        last = None
        for cb in range(8):
            last = eng.transpose(t["psv"][:, cb * 128:(cb + 1) * 128],
                                 t["h0"][:, cb * 128:(cb + 1) * 128],
                                 t["I128F"][:])
        return last
    ch.stage("tensor", st04, reads=("h0", "I128F"), writes=("psv",))


def emit_qka(ch, t):
    ONESR = t["ONESB"][0:1, 0:128]

    def st06(eng):
        last = None
        for wnm, ps in (("Wq", "psq"), ("Wk", "psk"), ("Wa", "psa")):
            for cb in range(8):
                last = eng.matmul(t[ps][:, 0:512],
                                  t["h0f"][:, cb * 128:(cb + 1) * 128],
                                  t[wnm][:, cb * 512:(cb + 1) * 512],
                                  start=(cb == 0), stop=False,
                                  skip_group_check=True)
            u0 = {"Wq": "u0q", "Wk": "u0k", "Wa": "u0a"}[wnm]
            last = eng.matmul(t[ps][:, 0:512], ONESR, t[u0][:],
                              start=False, stop=(wnm != "Wa"),
                              skip_group_check=True)
            if wnm == "Wa":
                last = eng.matmul(t[ps][:, 0:512], ONESR, t["wabrow"][:],
                                  start=False, stop=False,
                                  skip_group_check=True)
                last = eng.matmul(t[ps][:, 0:512], ONESR, t["wablo"][:],
                                  start=False, stop=True,
                                  skip_group_check=True)
        return last
    ch.stage("tensor", st06,
             reads=("h0f", "Wq", "Wk", "Wa", "u0q", "u0k", "u0a",
                    "wabrow", "wablo", "ONESB"),
             writes=("psq", "psk", "psa"))


def emit_ab(ch, t):
    ch.stage("vector", lambda e: e.tensor_copy(t["qs"], t["psq"][:, 0:512]),
             reads=("psq",), writes=("sp1",))

    def st08(eng):
        SS(eng.tensor_mul(t["qk"], t["psk"][:, 0:512], t["qs"]))
        return eng.tensor_reduce(
            t["abpad"][:, 32:40],
            t["qk"].rearrange("p (h s) -> p h s", h=8), AX.X, ALU.add)
    ch.stage("vector", st08, reads=("psk", "sp1"), writes=("sp2", "abpad"))

    def st09(eng):
        SS(eng.activation(t["sp1"], t["psa"][:, 0:512], AF.Abs))
        SS(eng.activation(t["sp2"], t["sp1"], AF.Exp, scale=-1.0))
        return eng.activation(t["sp1"], t["sp2"], AF.Ln, bias=1.0)
    ch.stage("scalar", st09, reads=("psa", "constv", "sp1", "sp2"),
             writes=("sp1", "sp2"))

    def st10(eng):
        SS(eng.tensor_scalar(t["sp2"], t["psa"][:, 0:512], 0.0, None,
                             ALU.max))
        SS(eng.tensor_add(t["sp1"], t["sp1"], t["sp2"]))
        return eng.tensor_reduce(
            t["abpad"][:, 0:8],
            t["sp1"].rearrange("p (h s) -> p h s", h=8),
            AX.X, ALU.add, negate=True)
    ch.stage("vector", st10, reads=("psa", "sp1", "sp2"),
             writes=("sp1", "sp2", "abpad"))


def emit_ab_tr(ch, t):
    ch.stage("tensor", lambda e: e.transpose(
        t["pss"][0:64, 0:128], t["abpad"][:, 0:64], t["I128F"][:]),
        reads=("abpad", "I128F"), writes=("pss",))


def emit_gates(ch, t):
    def st12(eng):
        eng.tensor_copy(t["aT"], t["pss"][0:8, 0:128])
        return eng.tensor_copy(t["bT"], t["pss"][32:40, 0:128])
    ch.stage("vector", st12, reads=("pss",), writes=("aT", "bT"))

    def st13(eng):
        SS(eng.tensor_tensor_scan(t["acuU"][0:8, :], t["aT"],
                                  t["zeros8c"][:, 0:128], 0.0, ALU.add,
                                  ALU.add))
        return eng.tensor_sub(t["bmT"], t["bT"], t["acuU"][0:8, :])
    ch.stage("vector", st13, reads=("aT", "bT", "zeros8c"),
             writes=("acuU", "bmT"))

    def st19(eng):
        last = None
        for h in range(8):
            last = eng.matmul(t["psy"][:, h * 128:(h + 1) * 128],
                              t["OH8"][:, h * 128:(h + 1) * 128],
                              t["bmT"], start=True, stop=True,
                              skip_group_check=True)
        return last
    ch.stage("tensor", st19, reads=("OH8", "bmT"), writes=("psy",))

    def st14(eng):
        SS(eng.activation(t["eAT"], t["acuU"][0:8, :], AF.Exp))
        return eng.activation(t["acuU"][32:40, :], t["bmT"], AF.Exp,
                              bias=t["acuU"][0:8, 127:128])
    ch.stage("scalar", st14, reads=("acuU", "bmT"), writes=("eAT", "acuU"))

    ch.stage("tensor", lambda e: e.transpose(
        t["pss"][0:128, 0:40], t["acuU"][:, :], t["I128F"][0:40, 0:40]),
        reads=("acuU", "I128F"), writes=("pss",))

    def st16(eng):
        eng.tensor_copy(t["acutm"], t["pss"][0:128, 0:8])
        eng.tensor_copy(t["utm"], t["pss"][0:128, 32:40])
        return eng.tensor_reduce(t["dZm"], t["acuU"][32:40, :], AX.X,
                                 ALU.add)
    ch.stage("vector", st16, reads=("pss", "acuU"),
             writes=("acutm", "utm", "dZm"))


def emit_v(ch, t):
    ONESR = t["ONESB"][0:1, 0:128]

    def st17(eng):
        last = None
        for hf in range(2):
            for cb in range(8):
                cbase = cb * D + hf * 512
                last = eng.matmul(t["psv"][:, hf * 512:(hf + 1) * 512],
                                  t["h0f"][:, cb * 128:(cb + 1) * 128],
                                  t["Wv"][:, cbase:cbase + 512],
                                  start=(cb == 0), stop=False,
                                  skip_group_check=True)
            last = eng.matmul(t["psv"][:, hf * 512:(hf + 1) * 512],
                              ONESR, t["u0v"][0:1, hf * 512:(hf + 1) * 512],
                              start=False, stop=True, skip_group_check=True)
        return last
    ch.stage("tensor", st17, reads=("h0f", "Wv", "u0v", "ONESB"),
             writes=("psv",))


def emit_warm(ch, t):
    ch.stage("tensor", lambda e: e.matmul(
        t["psq"][0:128, 0:512], t["ONESB"][0:1, 0:128],
        t["wabrow"][0:1, 0:512], start=True, stop=True,
        skip_group_check=True),
        reads=("ONESB", "wabrow"), writes=("psq",))


def emit_v_ev(ch, t):
    ch.stage("vector", lambda e: e.tensor_copy(t["Vb"], t["psv"][:]),
             reads=("psv",), writes=("Vb",))


def emit_score(ch, t):
    def st20(eng):
        last = None
        for h in range(8):
            last = eng.tensor_add(t["E1"][:, h * 128:(h + 1) * 128],
                                  t["psy"][:, h * 128:(h + 1) * 128],
                                  t["MASK"][:])
        return last
    ch.stage("vector", st20, reads=("psy", "MASK"), writes=("E1",))

    def st21(eng):
        last = None
        for h in range(8):
            last = eng.activation(t["Sall"][:, h * 128:(h + 1) * 128],
                                  t["E1"][:, h * 128:(h + 1) * 128],
                                  AF.Exp, bias=t["acutm"][:, h:h + 1])
        return last
    ch.stage("scalar", st21, reads=("E1", "acutm"), writes=("Sall",))

    ch.stage("vector", lambda e: e.tensor_reduce(
        t["zl"], t["Sall"].rearrange("p (h s) -> p h s", h=8),
        AX.X, ALU.add), reads=("Sall",), writes=("zl",))

    ch.stage("tensor", lambda e: e.transpose(t["pss"][0:8, 0:128],
                                             t["zl"], t["I128F"][:]),
             reads=("zl", "I128F"), writes=("pss",))

    def st24(eng):
        SS(eng.scalar_tensor_tensor(t["zT"], t["eAT"], t["Zst"][:, 0:1],
                                    t["pss"][0:8, 0:128], ALU.mult, ALU.add))
        SS(eng.reciprocal(t["rT"], t["zT"]))
        SS(eng.tensor_mul(t["etf"], t["eAT"], t["rT"]))
        return eng.tensor_copy(t["etb"], t["etf"])
    ch.stage("vector", st24, reads=("eAT", "Zst", "pss"),
             writes=("zT", "rT", "etf", "etb"))

    ch.stage("tensor", lambda e: e.transpose(t["pss"][0:128, 0:8],
                                             t["rT"], t["I128F"][0:8, 0:8]),
             reads=("rT", "I128F"), writes=("pss",))
    ch.stage("vector", lambda e: e.tensor_copy(t["rtm"],
                                               t["pss"][0:128, 0:8]),
             reads=("pss",), writes=("rtm",))


def emit_score_b(ch, t):
    def st27(eng):
        SS(eng.tensor_scalar(t["ssum"], t["Sall"][:, 0:128],
                             t["rtm"][:, 0:1], None, ALU.mult))
        for h in range(1, 8):
            SS(eng.scalar_tensor_tensor(t["ssum"],
                                        t["Sall"][:, h * 128:(h + 1) * 128],
                                        t["rtm"][:, h:h + 1], t["ssum"],
                                        ALU.mult, ALU.add))
        return eng.tensor_copy(t["ssb"], t["ssum"])
    ch.stage("vector", st27, reads=("Sall", "rtm"), writes=("ssum", "ssb"))

    ch.stage("tensor", lambda e: e.transpose(
        t["pss"][:].bitcast(BF16)[0:128, 0:128], t["ssb"], t["I128B"][:]),
        reads=("ssb", "I128B"), writes=("pss",))
    ch.stage("vector", lambda e: e.tensor_copy(
        t["sstsb"], t["pss"][:].bitcast(BF16)[0:128, 0:128]),
        reads=("pss",), writes=("sstsb",))


def emit_scan2(ch, t):
    def st29a(eng):
        last = None
        for cb in range(8):
            last = eng.matmul(t["psy"][:, cb * 128:(cb + 1) * 128],
                              t["Vb"][:, cb * 128:(cb + 1) * 128],
                              t["sstsb"], start=True, stop=False,
                              skip_group_check=True)
            last = eng.matmul(t["psy"][:, cb * 128:(cb + 1) * 128],
                              t["Nbf"][:, cb * 128:(cb + 1) * 128],
                              t["etb"], start=False, stop=True,
                              skip_group_check=True)
        last = eng.matmul(t["pss"][0:8, 0:512], t["utm"],
                          t["Vb"][:, 0:512], start=True, stop=True,
                          skip_group_check=True)
        return last
    ch.stage("tensor", st29a, reads=("Vb", "sstsb", "Nbf", "etb", "utm"),
             writes=("psy", "pss"))

    ch.stage("vector", lambda e: e.tensor_copy(t["yfm"], t["psy"][:]),
             reads=("psy",), writes=("yfm",))

    ch.stage("vector", lambda e: e.scalar_tensor_tensor(
        t["Nst"][:, 0:512], t["Nst"][:, 0:512], t["eAT"][:, 127:128],
        t["pss"][0:8, 0:512], ALU.mult, ALU.add),
        reads=("Nst", "eAT", "pss"), writes=("Nst",))

    ch.stage("tensor", lambda e: e.matmul(
        t["pss"][0:8, 0:512], t["utm"], t["Vb"][:, 512:1024],
        start=True, stop=True, skip_group_check=True),
        reads=("utm", "Vb"), writes=("pss",))

    def st30b(eng):
        SS(eng.scalar_tensor_tensor(t["Nst"][:, 512:1024],
                                    t["Nst"][:, 512:1024],
                                    t["eAT"][:, 127:128], t["pss"][0:8, 0:512],
                                    ALU.mult, ALU.add))
        return eng.scalar_tensor_tensor(t["Zst"][:], t["Zst"][:],
                                        t["eAT"][:, 127:128], t["dZm"],
                                        ALU.mult, ALU.add)
    ch.stage("vector", st30b, reads=("Nst", "eAT", "pss", "dZm", "Zst"),
             writes=("Nst", "Zst"))

    ch.stage("vector", lambda e: e.tensor_copy(t["Nbf"][:], t["Nst"][:]),
             reads=("Nst",), writes=("Nbf",))


def emit_wo(ch, t):
    def st34(eng):
        last = None
        for hf in range(2):
            for cb in range(8):
                cbase = cb * D + hf * 512
                last = eng.matmul(t["psy"][:, hf * 512:(hf + 1) * 512],
                                  t["yfm"][:, cb * 128:(cb + 1) * 128],
                                  t["Wo"][:, cbase:cbase + 512],
                                  start=(cb == 0), stop=(cb == 7),
                                  skip_group_check=True)
        return last
    ch.stage("tensor", st34, reads=("yfm", "Wo"), writes=("psy",))

    ch.stage("vector", lambda e: e.tensor_add(t["xt"], t["xt"],
                                              t["psy"][:]),
             reads=("xt", "psy"), writes=("xt",))


def emit_poll(eng, t, core, flslot, cfgslot, payload_dst, payload_src):
    rt = t["_regs"][f"_rt_{core}"]
    rx = t["_regs"][f"_rx_{core}"]
    re_ = t["_regs"][f"_re_{core}"]
    rd = t["_regs"][f"_rd_{core}"]
    po = t["_posem"]
    eng.load(re_, t["cfg"][0:1, cfgslot:cfgslot + 1])
    eng.reg_add(rt, rt, 16)
    eng.dma_start(t["flagsb"][:], t["_shfl"][0:1, flslot:flslot + 1]
                  ).then_inc(po, 16, skip_validation=True)
    eng.wait_ge(po, rt)
    eng.load(rx, t["flagsb"][0:1, 0:1])
    eng.reg_alu(rd, rx, re_, ALU.subtract)
    with eng.While(rd):
        eng.reg_add(rt, rt, 16)
        eng.dma_start(t["flagsb"][:], t["_shfl"][0:1, flslot:flslot + 1]
                      ).then_inc(po, 16, skip_validation=True)
        eng.wait_ge(po, rt)
        eng.load(rx, t["flagsb"][0:1, 0:1])
        eng.reg_alu(rd, rx, re_, ALU.subtract)
    eng.reg_add(rt, rt, 16)
    eng.dma_start(payload_dst, payload_src).then_inc(po, 16,
                                                     skip_validation=True)
    eng.wait_ge(po, rt)
    return eng.nop()


PARITY_SMALL = ['mv', 'bn12', 'lnv', 'rstd', 'abpad', 'aT', 'bT', 'bmT', 'acuU', 'eAT', 'acutm', 'utm', 'dZm', 'zl', 'zT', 'rT', 'etf', 'etb', 'rtm', 'ssum', 'ssb', 'sstsb', 'lg', 'm8', 'dd', 'ee', 'rr', 'w2', 'w1', 'mk1', 'mk2', 'wful']


class ChP:
    __slots__ = ("ch", "rmap")

    def __init__(self, ch, rmap):
        self.ch = ch
        self.rmap = rmap

    def stage(self, ek, fn, reads=(), writes=(), dma=False, ndma=1):
        self.ch.stage(ek, fn,
                      tuple(self.rmap.get(r, r) for r in reads),
                      tuple(self.rmap.get(w, w) for w in writes),
                      dma=dma, ndma=ndma)


def set_parity(ch, t, p, arena_par, gatile_ok, bffn=False):
    tt = dict(t)
    rmap = {}
    for nm in PARITY_SMALL:
        if p == 0:
            tt[nm] = t["_par0"][nm]
        else:
            tt[nm] = t["_par1"][nm]
            rmap[nm] = nm + "_p1"
    if gatile_ok and p == 1:
        tt["xt"] = t["_par1"]["xt"]
        rmap["xt"] = "gatile"
    else:
        tt["xt"] = t["_par0"]["xt"]
    pa = p if arena_par else 0
    src_map = t["_parA0"] if pa == 0 else t["_parA1"]
    for nm in ("h0", "h0f", "sp1", "sp2", "Vb", "yfm", "E1", "Sall"):
        tt[nm] = src_map[nm]
        if pa != 0:
            rmap[nm] = nm + "_ar1"
    tt["qs"] = tt["sp1"]
    tt["qk"] = tt["sp2"]
    tt["h2lo"] = t["h2lo"][:]
    if bffn and p == 1:
        tt["h0f"] = t["_parA0"]["Vb"]
        rmap["h0f"] = "Vb"
        tt["h2lo"] = t["_parA0"]["yfm"]
        rmap["h2lo"] = "yfm"
    tt["_ch"] = ChP(ch, rmap)
    return tt


def emit_core(core, t, io, ch, sh):
    isl = core // 2
    role_a = (core % 2 == 0)
    i0 = ISL_START[isl]
    ilen = ISL_LEN[isl]
    ntile = ilen // 128
    pfx_total = i0
    half = (pfx_total // 2 // 512) * 512
    pfx_lo, pfx_hi = (0, half) if role_a else (half, pfx_total)
    npc = (pfx_hi - pfx_lo) // 512
    ONESR = t["ONESB"][0:1, 0:128]

    def init_rt(eng):
        for key, nm in (("_rt", "rt"), ("_rx", "rx"), ("_re", "re"),
                        ("_rd", "rd")):
            t["_regs"][f"{key}_{core}"] = eng.alloc_register(f"{nm}_{core}")
        eng.reg_mov(t["_regs"][f"_rt_{core}"], 0)
        return eng.nop()
    ch.stage("gpsimd", init_rt)

    def st_loads(eng):
        out = []
        for nm in ("I128F", "I128B", "MASK", "ONESB"):
            out.append(eng.dma_start(t[nm][:], io[nm]))
        out.append(eng.dma_start(t["OH8"], io["OH8"]))
        out.append(eng.dma_start(t["crow"], io["CVEC"]))
        out.append(eng.dma_start(t["constv"][:], io["CONSTV"]))
        out.append(eng.dma_start(t["cfg"][:], io["CFG"]))
        out.append(eng.dma_start(t["wabf"], io["WAB"]))
        if not role_a:
            out.append(eng.dma_start(
                t["gf"][:].rearrange("p (cb w) -> p cb w", cb=NCB),
                io["GT"].rearrange("(cb p) w -> p cb w", p=128)))
        return out
    nload = 10 if not role_a else 9
    ch.stage("sync", st_loads, dma=True, ndma=nload,
             writes=("I128F", "I128B", "MASK", "ONESB", "OH8", "crow",
                     "constv", "cfg", "wabf") + (("gf",) if not role_a
                                                 else ()))

    def st_memset(eng):
        eng.memset(t["zeros8c"][:], 0.0)
        eng.memset(t["Nst"][:], 0.0)
        eng.memset(t["Nbf"][:], 0.0)
        eng.memset(t["Zst"][:], 0.0)
        eng.memset(t["zrow8b"][:], 0.0)
        return eng.memset(t["Pprod"][:], 1.0)
    ch.stage("vector", st_memset,
             writes=("zeros8c", "Nst", "Nbf", "Zst", "zrow8b",
                     "Pprod"))

    ch.stage("gpsimd", lambda e: e.dma_start(t["wabrow"][:], t["wabf"]),
             dma=True, reads=("wabf",), writes=("wabrow",))

    def st_wab2(eng):
        SS(eng.tensor_sub(t["wabloF"], t["wabf"], t["wabrow"][:]))
        return eng.tensor_copy(t["wablo"][:], t["wabloF"])
    ch.stage("vector", st_wab2, reads=("wabf", "wabrow"),
             writes=("wabloF", "wablo"))

    ch.stage("scalar", lambda e: e.activation(t["scrow"], t["crow"], AF.Silu),
             reads=("crow",), writes=("scrow",))

    def st_sct(eng):
        last = None
        for cb in range(8):
            last = eng.transpose(t["pss"][0:128, cb:cb + 1],
                                 t["scrow"][0:1, cb * 128:(cb + 1) * 128],
                                 t["I128F"][0:1, 0:1])
        return last
    ch.stage("tensor", st_sct, reads=("scrow", "I128F"), writes=("pss",))
    ch.stage("vector", lambda e: e.tensor_copy(t["sc"][:], t["pss"][:, 0:8]),
             reads=("pss",), writes=("sc",))

    wmrs = ("wmr0A", "wmr1A") if role_a else ("wmr0B", "wmr1B")
    rbn = "rowbufA" if role_a else "rowbuf0"

    wlist = (("Wq", io["WQT"]), ("Wk", io["WKT"]), ("Wa", io["WAT"])) + (
        (("Wv", io["WVT"]), ("Wo", io["WOT"])) if role_a else ())
    wnames = tuple(nm for nm, _ in wlist)

    def emit_wload1():
        def st_wload1(eng):
            out = []
            for nm, src in wlist[:3]:
                out.append(eng.dma_start(
                    t[nm].rearrange("p (cb w) -> p cb w", cb=NCB),
                    src.rearrange("(cb p) w -> p cb w", p=128)))
            return out
        ch.stage("gpsimd", st_wload1, dma=True, ndma=3,
                 writes=("Wq", "Wk", "Wa"))

    def emit_wload2():
        if role_a:
            def st_wload2(eng):
                out = []
                for nm, src in wlist[3:]:
                    out.append(eng.dma_start(
                        t[nm].rearrange("p (cb w) -> p cb w", cb=NCB),
                        src.rearrange("(cb p) w -> p cb w", p=128)))
                return out
            ch.stage("gpsimd", st_wload2, dma=True, ndma=2,
                     writes=("Wv", "Wo", "crow", "scrow", "wabf",
                             "wabloF"))
        else:
            def st_eth(eng):
                out = []
                for e_ in (5, 6, 7):
                    out.append(eng.dma_start(
                        t["Et"].rearrange("p (e cb w) -> p e cb w",
                                          e=NEXP, cb=NCB)[:, e_],
                        io["EXPT"][e_].rearrange("(cb p) w -> p cb w",
                                                 p=128)))
                return out
            ch.stage("gpsimd", st_eth, dma=True, ndma=3, writes=("EtH",))

    for r in range(12):
        wmr = wmrs[r % 2]
        rb = rbn

        def st_wmr(eng, r=r, wmr=wmr):
            return eng.dma_start(
                t[wmr].rearrange("p (cb w) -> p cb w", cb=8),
                io["WMOD"].rearrange("(cb p) w -> p cb w", p=128)
                [:, :, r * 512:(r + 1) * 512])
        ch.stage("sync", st_wmr, dma=True, writes=(wmr,))

        if r:
            def st_modt(eng, r=r - 1, rb=rbn):
                last = None
                for i in range(4):
                    s = r * 4 + i
                    last = eng.transpose(
                        t["pss"][0:128, s:s + 1],
                        t[rb][0:1, i * 128:(i + 1) * 128],
                        t["I128F"][0:1, 0:1])
                return last
            ch.stage("tensor", st_modt, reads=(rbn, "I128F"),
                     writes=("pss",))

        def st_modr(eng, r=r, wmr=wmr):
            last = None
            for cb2 in range(8):
                last = eng.matmul(t["psq"][0:1, 0:512],
                                  t["sc"][:, cb2:cb2 + 1],
                                  t[wmr][:, cb2 * 512:(cb2 + 1) * 512],
                                  start=(cb2 == 0), stop=(cb2 == 7),
                                  skip_group_check=True)
            return last
        ch.stage("tensor", st_modr, reads=(wmr, "sc"), writes=("psq",))
        ch.stage("vector", lambda e, rb=rb: e.tensor_copy(
            t[rb][:], t["psq"][0:1, 0:512]), reads=("psq",), writes=(rb,))
        if r == 4:
            emit_wload1()
        elif r == 7:
            emit_wload2()

    def st_modt_last(eng):
        last = None
        for i in range(4):
            s = 44 + i
            last = eng.transpose(t["pss"][0:128, s:s + 1],
                                 t[rbn][0:1, i * 128:(i + 1) * 128],
                                 t["I128F"][0:1, 0:1])
        return last
    ch.stage("tensor", st_modt_last, reads=(rbn, "I128F"),
             writes=("pss",))
    ch.stage("vector", lambda e: e.tensor_copy(t["modT"][:],
                                               t["pss"][:, 0:48]),
             reads=("pss",), writes=("modT",))

    if not role_a:
        ch.stage("vector", lambda e: e.nop(),
                 writes=("h0pre", "shpk", "aTc", "bTc", "AcumTc", "UTe",
                         "Hsum", wmrs[0], wmrs[1], rbn))
    ch.stage("vector", lambda e: e.memset(t["Hsum"][:], 0.0),
             writes=("Hsum",))

    gcol = (16, 24) if role_a else (40, 48)

    def st_mod2(eng):
        eng.tensor_scalar(t["ca1"][:], t["modT"][:, 8:16], 1.0, None, ALU.add)
        if not role_a:
            eng.tensor_scalar(t["cb1f"][:], t["modT"][:, 32:40], 1.0, None,
                              ALU.add)
            eng.tensor_copy(t["sbf"][:], t["modT"][:, 24:32])
        SS(eng.tensor_scalar(t["t8a"][:], t["modT"][:, gcol[0]:gcol[1]], 0.0,
                             None, ALU.is_ge))
        SS(eng.tensor_scalar(t["t8a"][:], t["t8a"][:], 2.0, -1.0, ALU.mult,
                             ALU.add))
        SS(eng.tensor_scalar(t["t8b"][:], t["modT"][:, gcol[0]:gcol[1]], -1.0,
                             None, ALU.mult))
        SS(eng.tensor_max(t["t8b"][:], t["t8b"][:],
                          t["modT"][:, gcol[0]:gcol[1]]))
        SS(eng.tensor_scalar(t["t8b"][:], t["t8b"][:], 0.25, None, ALU.max))
        SS(eng.tensor_mul(t["gafm"][:], t["t8a"][:], t["t8b"][:]))
        SS(eng.tensor_copy(t["sab"][:], t["modT"][:, 0:8]))
        return eng.tensor_copy(t["sab_b"][:], t["sab"][:])
    ch.stage("vector", st_mod2, reads=("modT", "constv"),
             writes=("ca1", "cb1f", "sbf", "t8a", "t8b", "gafm", "sab",
                     "sab_b"))

    ch.stage("tensor", lambda e: e.transpose(t["pss"][0:8, 0:128],
                                             t["gafm"][:], t["I128F"][:]),
             reads=("gafm", "I128F"), writes=("pss",))
    ch.stage("vector", lambda e: e.tensor_copy(t["gaT"][:],
                                               t["pss"][0:8, 0:128]),
             reads=("pss",), writes=("gaT",))

    def st_gat(eng):
        last = None
        for cb in range(8):
            last = eng.matmul(t["psy"][:, cb * 128:(cb + 1) * 128],
                              t["OH8"][:, cb * 128:(cb + 1) * 128],
                              t["gaT"][:], start=True, stop=True,
                              skip_group_check=True)
        return last
    ch.stage("tensor", st_gat, reads=("OH8", "gaT"), writes=("psy",))
    ch.stage("vector", lambda e: e.tensor_copy(t["gatile"][:], t["psy"][:]),
             reads=("psy",), writes=("gatile",))

    if not role_a:
        def st_etl1(eng):
            out = []
            for e_ in (0, 1):
                out.append(eng.dma_start(
                    t["Et"].rearrange("p (e cb w) -> p e cb w",
                                      e=NEXP, cb=NCB)[:, e_],
                    io["EXPT"][e_].rearrange("(cb p) w -> p cb w", p=128)))
            return out
        ch.stage("gpsimd", st_etl1, dma=True, ndma=2,
                 writes=("EtL1", "crow", "scrow", "wabf", "wabloF",
                         "rowbuf0"))

    u0list = (("Wq", "u0q", FQ), ("Wk", "u0k", FQ), ("Wa", "u0a", FQ)) + (
        (("Wv", "u0v", D),) if role_a else ())

    def emit_folds():
      for wnm, unm, wid in u0list:
        for hf in range(wid // 512):
            def st_u0(eng, wnm=wnm, wid=wid, hf=hf):
                last = None
                for cb in range(8):
                    cbase = cb * wid + hf * 512
                    last = eng.matmul(t["psq"][0:1, 0:512],
                                      t["sab_b"][:, cb:cb + 1],
                                      t[wnm][:, cbase:cbase + 512],
                                      start=(cb == 0), stop=(cb == 7),
                                      skip_group_check=True)
                return last
            ch.stage("tensor", st_u0, reads=("sab_b", wnm), writes=("psq",))
            if wnm == "Wq":
                ch.stage("vector", lambda e: e.tensor_scalar(
                    t["u0q"][:], t["psq"][0:1, 0:512], 0.125, None, ALU.mult),
                    reads=("psq",), writes=("u0q",))
            else:
                ch.stage("vector", lambda e, unm=unm, hf=hf: e.tensor_copy(
                    t[unm][0:1, hf * 512:(hf + 1) * 512],
                    t["psq"][0:1, 0:512]), reads=("psq",), writes=(unm,))

      def st_scale(eng):
        last = None
        for cb in range(8):
            last = eng.tensor_scalar(
                t["Wq"][:, cb * 512:(cb + 1) * 512],
                t["Wq"][:, cb * 512:(cb + 1) * 512],
                t["ca1"][:, cb:cb + 1], 0.125, ALU.mult, ALU.mult)
            for nm in ("Wk", "Wa"):
                last = eng.tensor_scalar(
                    t[nm][:, cb * 512:(cb + 1) * 512],
                    t[nm][:, cb * 512:(cb + 1) * 512],
                    t["ca1"][:, cb:cb + 1], None, ALU.mult)
            if role_a:
                last = eng.tensor_scalar(
                    t["Wv"][:, cb * 1024:(cb + 1) * 1024],
                    t["Wv"][:, cb * 1024:(cb + 1) * 1024],
                    t["ca1"][:, cb:cb + 1], None, ALU.mult)
        return last
      ch.stage("vector", st_scale, reads=("ca1",) + wnames, writes=wnames)

    def emit_scale_wo():
        def st_swo(eng):
            last = None
            for cb in range(8):
                last = eng.tensor_mul(
                    t["Wo"][:, cb * 1024:(cb + 1) * 1024],
                    t["Wo"][:, cb * 1024:(cb + 1) * 1024], t["gatile"][:])
            return last
        ch.stage("vector", st_swo, reads=("gatile", "Wo"), writes=("Wo",))

    if not role_a:
        def st_cg(eng):
            last = None
            for cb in range(8):
                last = eng.tensor_copy(t["cghi"][:, cb * 8:(cb + 1) * 8],
                                       t["gf"][:, cb * 8:(cb + 1) * 8])
            SS(last)
            for cb in range(8):
                last = eng.tensor_sub(t["cgloF"][:, cb * 8:(cb + 1) * 8],
                                      t["gf"][:, cb * 8:(cb + 1) * 8],
                                      t["cghi"][:, cb * 8:(cb + 1) * 8])
            SS(last)
            return eng.tensor_copy(t["cglo"][:], t["cgloF"][:])
        ch.stage("vector", st_cg, reads=("gf",),
                 writes=("cghi", "cgloF", "cglo"))

    nsub = npc * 4

    def pfx_front(tt, r0):
        tt["_ch"].stage("sync", lambda e, r0=r0, t=tt: e.dma_start(
            t["xt"], io["XF"][r0:r0 + 128, :]), dma=True, writes=("xt",))

    def pfx_tr(tt, t4):
        c = tt["_ch"]
        emit_transposes_h0(c, tt)

        def p05(eng, t4=t4, t=tt):
            eng.tensor_copy(t["h0f"], t["psv"][:])
            return eng.tensor_copy(t["h0pre"][:, t4 * D:(t4 + 1) * D],
                                   t["h0"])
        c.stage("vector", p05, reads=("psv", "h0"), writes=("h0f", "h0pre"))

    def pfx_ab(tt, t4):
        c = tt["_ch"]
        emit_ab(c, tt)
        emit_ab_tr(c, tt)

        def p12(eng, t4=t4, t=tt):
            eng.tensor_copy(t["aTc"][:, t4 * 128:(t4 + 1) * 128],
                            t["pss"][0:8, 0:128])
            return eng.tensor_copy(t["bTc"][:, t4 * 128:(t4 + 1) * 128],
                                   t["pss"][32:40, 0:128])
        c.stage("vector", p12, reads=("pss",), writes=("aTc", "bTc"))

    def pfx_combine():
        def p14(eng):
            SS(eng.tensor_tensor_scan(t["AcumTc"][:], t["aTc"][:],
                                      t["zeros8c"][:], 0.0, ALU.add, ALU.add))
            return eng.tensor_sub(t["UTe"][:], t["bTc"][:], t["AcumTc"][:])
        ch.stage("vector", p14, reads=("aTc", "bTc", "zeros8c"),
                 writes=("AcumTc", "UTe"))

        def p15(eng):
            SS(eng.activation(t["UTe"][:], t["UTe"][:], AF.Exp,
                              bias=t["AcumTc"][:, 511:512]))
            return eng.activation(t["ePc"][:], t["AcumTc"][:, 511:512],
                                  AF.Exp)
        ch.stage("scalar", p15, reads=("UTe", "AcumTc"),
                 writes=("UTe", "ePc"))

        ch.stage("vector", lambda e: e.tensor_reduce(t["dZc"][:], t["UTe"][:],
                                                     AX.X, ALU.add),
                 reads=("UTe",), writes=("dZc",))

        def p17(eng):
            last = None
            for t4 in range(4):
                last = eng.transpose(t["pss"][0:128, t4 * 8:(t4 + 1) * 8],
                                     t["UTe"][:, t4 * 128:(t4 + 1) * 128],
                                     t["I128F"][0:8, 0:8])
            return last
        ch.stage("tensor", p17, reads=("UTe", "I128F"), writes=("pss",))
        ch.stage("vector", lambda e: e.tensor_copy(t["utm4"][:],
                                                   t["pss"][0:128, 0:32]),
                 reads=("pss",), writes=("utm4",))

        for hf in range(2):
            def p19(eng, hf=hf):
                last = None
                for t4 in range(4):
                    last = eng.matmul(
                        t["pss"][0:8, 0:512],
                        t["utm4"][:, t4 * 8:(t4 + 1) * 8],
                        t["h0pre"][:, t4 * D + hf * 512:
                                   t4 * D + hf * 512 + 512],
                        start=(t4 == 0), stop=(t4 == 3),
                        skip_group_check=True)
                return last
            ch.stage("tensor", p19, reads=("utm4", "h0pre"), writes=("pss",))

            def p20(eng, hf=hf):
                return eng.scalar_tensor_tensor(
                    t["Hsum"][:, hf * 512:(hf + 1) * 512],
                    t["Hsum"][:, hf * 512:(hf + 1) * 512],
                    t["ePc"][:, 0:1], t["pss"][0:8, 0:512],
                    ALU.mult, ALU.add)
            ch.stage("vector", p20, reads=("Hsum", "ePc", "pss"),
                     writes=("Hsum",))

        def p22(eng):
            SS(eng.scalar_tensor_tensor(t["Zst"][:], t["Zst"][:],
                                        t["ePc"][:, 0:1], t["dZc"][:],
                                        ALU.mult, ALU.add))
            return eng.tensor_mul(t["Pprod"][:], t["Pprod"][:], t["ePc"][:])
        ch.stage("vector", p22, reads=("Zst", "ePc", "dZc", "Pprod"),
                 writes=("Zst", "Pprod"))

    subs = [set_parity(ch, t, n % 2, arena_par=role_a, gatile_ok=role_a,
                       bffn=(not role_a))
            for n in range(nsub)]
    if nsub:
        pfx_front(subs[0], pfx_lo)
        emit_ln(subs[0]["_ch"], subs[0])
        pfx_tr(subs[0], 0)
        emit_folds()
        if role_a:
            emit_scale_wo()
        emit_qka(subs[0]["_ch"], subs[0])
        for n in range(nsub):
            if n + 1 < nsub:
                pfx_front(subs[n + 1], pfx_lo + (n + 1) * 128)
            pfx_ab(subs[n], n % 4)
            if n + 1 < nsub:
                emit_ln(subs[n + 1]["_ch"], subs[n + 1])
            if n % 4 == 3:
                pfx_combine()
            if n + 1 < nsub:
                pfx_tr(subs[n + 1], (n + 1) % 4)
                emit_qka(subs[n + 1]["_ch"], subs[n + 1])
    tctr = nsub

    if not role_a:
        if pfx_total:
            def b_pack(eng):
                eng.tensor_copy(t["shpk"][:, 0:D], t["Hsum"][:])
                eng.tensor_copy(t["shpk"][:, D:D + 1], t["Zst"][:])
                return eng.tensor_copy(t["shpk"][:, D + 1:D + 2],
                                       t["Pprod"][:])
            ch.stage("vector", b_pack, reads=("Hsum", "Zst", "Pprod"),
                     writes=("shpk",))
            ch.stage("gpsimd", lambda e: e.dma_start(sh["SHH"].ap(),
                                                     t["shpk"][:]),
                     dma=True, reads=("shpk",), writes=("SHHres",))
            ch.stage("gpsimd", lambda e: e.dma_start(
                t["_shfl"][0:1, 0:1], t["cfg"][0:1, 0:1]), dma=True,
                reads=("SHHres", "cfg"), writes=("SHFLres",))
    else:
        if pfx_total:
            ch.stage("gpsimd", lambda e: emit_poll(
                e, t, core, 0, 0, t["shpk"][:], sh["SHH"].ap()),
                reads=("cfg",), writes=("shpk", "flagsb"))

            def a_comb(eng):
                SS(eng.scalar_tensor_tensor(t["Hsum"][:], t["Hsum"][:],
                                            t["shpk"][:, D + 1:D + 2],
                                            t["shpk"][:, 0:D],
                                            ALU.mult, ALU.add))
                return eng.scalar_tensor_tensor(t["Zst"][:], t["Zst"][:],
                                                t["shpk"][:, D + 1:D + 2],
                                                t["shpk"][:, D:D + 1],
                                                ALU.mult, ALU.add)
            ch.stage("vector", a_comb, reads=("Hsum", "Zst", "shpk"),
                     writes=("Hsum", "Zst"))

            def pf1(eng):
                last = None
                for cb in range(8):
                    last = eng.transpose(
                        t["pss"][0:128, cb * 8:(cb + 1) * 8],
                        t["Hsum"][:, cb * 128:(cb + 1) * 128],
                        t["I128F"][0:8, 0:8])
                return last
            ch.stage("tensor", pf1, reads=("Hsum", "I128F"), writes=("pss",))
            ch.stage("vector", lambda e: e.tensor_copy(t["HsT"][:],
                                                       t["pss"][0:128, 0:64]),
                     reads=("pss",), writes=("HsT",))
            ch.stage("tensor", lambda e: e.transpose(
                t["pss"][0:1, 0:8], t["Zst"][:, 0:1], t["I128F"][0:8, 0:8]),
                reads=("Zst", "I128F"), writes=("pss",))
            ch.stage("vector", lambda e: e.tensor_copy(t["zrowb"][:],
                                                       t["pss"][0:1, 0:8]),
                     reads=("pss",), writes=("zrowb",))
            for hf in range(2):
                def pf3(eng, hf=hf):
                    last = None
                    for cb in range(8):
                        last = eng.matmul(
                            t["psq"][0:8, 0:512],
                            t["HsT"][:, cb * 8:(cb + 1) * 8],
                            t["Wv"][:, cb * D + hf * 512:
                                    cb * D + hf * 512 + 512],
                            start=(cb == 0), stop=False,
                            skip_group_check=True)
                    last = eng.matmul(t["psq"][0:8, 0:512], t["zrowb"][:],
                                      t["u0v"][0:1, hf * 512:(hf + 1) * 512],
                                      start=False, stop=True,
                                      skip_group_check=True)
                    return last
                ch.stage("tensor", pf3, reads=("HsT", "Wv", "zrowb", "u0v"),
                         writes=("psq",))
                ch.stage("vector", lambda e, hf=hf: e.tensor_copy(
                    t["Nst"][:, hf * 512:(hf + 1) * 512],
                    t["psq"][0:8, 0:512]), reads=("psq",), writes=("Nst",))
            ch.stage("vector", lambda e: e.tensor_copy(t["Nbf"][:],
                                                       t["Nst"][:]),
                     reads=("Nst",), writes=("Nbf",))

    if role_a:
        mts = [set_parity(ch, t, (tctr + j) % 2, arena_par=True,
                          gatile_ok=True) for j in range(ntile)]

        def a_front(j):
            tt = mts[j]
            r0 = i0 + j * 128
            tt["_ch"].stage("sync", lambda e, r0=r0, t=tt: e.dma_start(
                t["xt"], io["XF"][r0:r0 + 128, :]), dma=True,
                writes=("xt",))

        def a_tr(j):
            tt = mts[j]
            emit_transposes_h0(tt["_ch"], tt)
            tt["_ch"].stage("vector", lambda e, t=tt: e.tensor_copy(
                t["h0f"], t["psv"][:]), reads=("psv",), writes=("h0f",))

        def a_out(j):
            tt = mts[j]
            tt["_ch"].stage("gpsimd", lambda e, j=j, t=tt: e.dma_start(
                sh["SHX"].ap()[j * 128:(j + 1) * 128, :], t["xt"]),
                dma=True, reads=("xt",), writes=(f"SHXres{j}",))
            tt["_ch"].stage("gpsimd", lambda e, j=j: e.dma_start(
                t["_shfl"][0:1, 1 + j:2 + j], t["cfg"][0:1, 1 + j:2 + j]),
                dma=True, reads=(f"SHXres{j}", "cfg"), writes=("SHFLres",))

        xtsl = [(t["_par0"]["xt"], "xt"), (t["_par1"]["xt"], "gatile"),
                (t["xt3"][:], "xt3")]
        for j in range(ntile):
            ap, nm = xtsl[j % 3]
            mts[j]["xt"] = ap
            mts[j]["_ch"].rmap["xt"] = nm
        ch.stage("vector", lambda e: e.nop(), writes=("xt3", "wmr0A"))

        a_front(0)
        emit_ln(mts[0]["_ch"], mts[0])
        a_tr(0)
        if pfx_total == 0:
            emit_folds()
            emit_scale_wo()
        emit_qka(mts[0]["_ch"], mts[0])
        emit_ab(mts[0]["_ch"], mts[0])
        for j in range(ntile):
            tt = mts[j]
            if j + 1 < ntile:
                a_front(j + 1)
            emit_v(tt["_ch"], tt)
            emit_v_ev(tt["_ch"], tt)
            emit_ab_tr(tt["_ch"], tt)
            emit_warm(tt["_ch"], tt)
            emit_gates(tt["_ch"], tt)
            if j + 1 < ntile:
                emit_ln(mts[j + 1]["_ch"], mts[j + 1])
                a_tr(j + 1)
                emit_qka(mts[j + 1]["_ch"], mts[j + 1])
            emit_score(tt["_ch"], tt)
            emit_warm(tt["_ch"], tt)
            emit_score_b(tt["_ch"], tt)
            emit_warm(tt["_ch"], tt)
            emit_scan2(tt["_ch"], tt)
            if j + 1 < ntile:
                emit_ab(mts[j + 1]["_ch"], mts[j + 1])
            emit_wo(tt["_ch"], tt)
            a_out(j)
        tctr += ntile
        ch.stage("gpsimd", lambda e: e.nop(), reads=("SHFLres",))
    else:
        def st_etl2(eng):
            out = []
            for e_ in (2, 3, 4):
                out.append(eng.dma_start(
                    t["Et"].rearrange("p (e cb w) -> p e cb w",
                                      e=NEXP, cb=NCB)[:, e_],
                    io["EXPT"][e_].rearrange("(cb p) w -> p cb w", p=128)))
            return out
        ch.stage("gpsimd", st_etl2, dma=True, ndma=3,
                 writes=("EtL2", "wmr0B", "wmr1B", "h0pre", "shpk", "aTc",
                         "bTc", "AcumTc", "UTe", "Hsum", "OH8", "E1",
                         "Sall"))

        for e_ in range(NEXP):
            def st_egb(eng, e_=e_):
                last = None
                for cb in range(8):
                    base = (e_ * NCB + cb) * D
                    last = eng.tensor_mul(t["Et"][:, base:base + D],
                                          t["Et"][:, base:base + D],
                                          t["gatile"][:])
                return last
            ch.stage("vector", st_egb, reads=(ET_RES[e_], "gatile"),
                     writes=(ET_RES[e_],))

        bts = [set_parity(ch, t, (tctr + j) % 2, arena_par=False,
                          gatile_ok=True, bffn=True) for j in range(ntile)]
        tctr += ntile

        def b_poll(j):
            tt = bts[j]
            tt["_ch"].stage("gpsimd", lambda e, j=j, t=tt: emit_poll(
                e, t, core, 1 + j, 1 + j, t["xt"],
                sh["SHX"].ap()[j * 128:(j + 1) * 128, :]),
                reads=("cfg",), writes=("xt", "flagsb"))

        def b_trb(j):
            tt = bts[j]
            c = tt["_ch"]
            emit_transposes_h0(c, tt)

            def f05(eng, t=tt):
                last = None
                for cb in range(8):
                    last = eng.tensor_scalar(
                        t["h0"][:, cb * 128:(cb + 1) * 128],
                        t["psv"][:, cb * 128:(cb + 1) * 128],
                        t["cb1f"][:, cb:cb + 1], t["sbf"][:, cb:cb + 1],
                        ALU.mult, ALU.add)
                SS(last)
                SS(eng.tensor_copy(t["h0f"], t["h0"]))
                return eng.tensor_sub(t["h2lo"], t["h0"], t["h0f"])
            c.stage("vector", f05, reads=("psv", "cb1f", "sbf", "h0"),
                    writes=("h0", "h0f", "h2lo"))

        def b_lg(j):
            tt = bts[j]
            c = tt["_ch"]

            def f_lg(eng, t=tt):
                last = None
                first = True
                for (mat, gw) in ((t["h0f"], "cghi"), (t["h0f"], "cglo"),
                                  (t["h2lo"], "cghi")):
                    for cb in range(8):
                        last = eng.matmul(t["psq"][0:128, 0:8],
                                          mat[:, cb * 128:(cb + 1) * 128],
                                          t[gw][:, cb * 8:(cb + 1) * 8],
                                          start=first, stop=False,
                                          skip_group_check=True)
                        first = False
                last = eng.matmul(t["psq"][0:128, 0:8], ONESR,
                                  t["zrow8b"][:], start=False, stop=True,
                                  skip_group_check=True)
                return last
            c.stage("tensor", f_lg,
                    reads=("h0f", "h2lo", "cghi", "cglo", "zrow8b",
                           "ONESB"),
                    writes=("psq",))
            c.stage("vector", lambda e, t=tt: e.tensor_copy(
                t["lg"], t["psq"][0:128, 0:8]),
                reads=("psq",), writes=("lg",))

        def b_top2(j):
            tt = bts[j]
            c = tt["_ch"]

            def f_top2(eng, t=tt):
                SS(eng.max(t["m8"], t["lg"]))
                return eng.tensor_sub(t["dd"], t["m8"][:, 1:2],
                                      t["m8"][:, 0:1])
            c.stage("vector", f_top2, reads=("lg",), writes=("m8", "dd"))
            c.stage("scalar", lambda e, t=tt: e.activation(t["ee"], t["dd"],
                                                           AF.Exp),
                    reads=("dd",), writes=("ee",))

            def f_w12(eng, t=tt):
                SS(eng.tensor_scalar(t["rr"], t["ee"], 1.0, None,
                                     ALU.add))
                SS(eng.reciprocal(t["rr"], t["rr"]))
                SS(eng.tensor_mul(t["w2"], t["ee"], t["rr"]))
                SS(eng.tensor_scalar(t["w1"], t["w2"], -1.0, 1.0,
                                     ALU.mult, ALU.add))
                SS(eng.tensor_scalar(t["mk1"], t["lg"], t["m8"][:, 0:1],
                                     None, ALU.is_equal))
                SS(eng.tensor_scalar(t["mk2"], t["lg"], t["m8"][:, 1:2],
                                     None, ALU.is_equal))
                SS(eng.tensor_scalar(t["mk1"], t["mk1"],
                                     t["w1"][:, 0:1], None, ALU.mult))
                return eng.scalar_tensor_tensor(t["wful"], t["mk2"],
                                                t["w2"][:, 0:1], t["mk1"],
                                                ALU.mult, ALU.add)
            c.stage("vector", f_w12, reads=("ee", "lg", "m8", "constv"),
                    writes=("rr", "w2", "w1", "mk1", "mk2", "wful"))

        def b_experts(j):
            tt = bts[j]
            c = tt["_ch"]
            for e_ in range(NEXP):
                psn = "psv" if e_ % 2 == 0 else "psy"

                def f_em(eng, e_=e_, psn=psn, t=tt):
                    last = None
                    for hf in range(2):
                        for cb in range(8):
                            base = (e_ * NCB + cb) * D + hf * 512
                            last = eng.matmul(
                                t[psn][:, hf * 512:(hf + 1) * 512],
                                t["h0f"][:, cb * 128:(cb + 1) * 128],
                                t["Et"][:, base:base + 512],
                                start=(cb == 0), stop=(cb == 7),
                                skip_group_check=True)
                    return last
                c.stage("tensor", f_em, reads=("h0f", ET_RES[e_]),
                        writes=(psn,))
                c.stage("vector",
                        lambda eng, e_=e_, psn=psn, t=tt:
                        eng.scalar_tensor_tensor(
                            t["xt"], t[psn][:], t["wful"][:, e_:e_ + 1],
                            t["xt"], ALU.mult, ALU.add),
                        reads=(psn, "wful", "xt"), writes=("xt",))

        def b_out(j):
            tt = bts[j]
            tt["_ch"].stage("gpsimd", lambda e, j=j, t=tt: e.dma_start(
                io["OUT"][j * 128:(j + 1) * 128, :], t["xt"]), dma=True,
                reads=("xt",), writes=(f"OUTres{j}",))

        b_poll(0)
        emit_ln(bts[0]["_ch"], bts[0])
        b_trb(0)
        b_lg(0)
        b_top2(0)
        for j in range(ntile):
            if j + 1 < ntile:
                b_poll(j + 1)
                emit_ln(bts[j + 1]["_ch"], bts[j + 1])
            b_experts(j)
            b_out(j)
            if j + 1 < ntile:
                b_trb(j + 1)
                b_lg(j + 1)
                b_top2(j + 1)
        ch.stage("gpsimd", lambda e: e.nop(),
                 reads=tuple(f"OUTres{j}" for j in range(ntile)))


def build_nc():
    from contextlib import ExitStack
    nc = bass.Bass("TRN2", target_bir_lowering=False)
    dram = nc.dram_tensor
    io = {}
    for nm, sh_, dt in [
        ("XF", [S, D], F32), ("WQT", [D, FQ], F32), ("WKT", [D, FQ], F32),
        ("WAT", [D, FQ], F32), ("WVT", [D, D], F32), ("WOT", [D, D], F32),
        ("WAB", [1, FQ], F32), ("GT", [D, NEXP], F32),
        ("EXPT", [NEXP, D, D], F32), ("WMOD", [D, 6 * D], F32),
        ("CVEC", [1, D], F32),
        ("I128F", [128, 128], F32), ("I128B", [128, 128], BF16),
        ("MASK", [128, 128], F32), ("OH8", [8, 8 * 128], F32),
        ("ONESB", [1, 128], BF16), ("CONSTV", [128, 8], F32),
        ("CFG", [1, 64], I32),
    ]:
        io[nm] = dram(nm, sh_, dt, kind="ExternalInput").ap()
    io["OUT"] = dram("OUT", [max(ISL_LEN), D], F32,
                     kind="ExternalOutput").ap()

    sh = {
        "SHX": dram("SHX", [max(ISL_LEN), D], F32, kind="Internal",
                    addr_space="Shared"),
        "SHH": dram("SHH", [8, D + 8], F32, kind="Internal",
                    addr_space="Shared"),
        "SHFL": dram("SHFL", [1, 64], I32, kind="Internal",
                     addr_space="Shared"),
    }

    es = ExitStack()
    t = {}

    def SB(name, shape, dt):
        t[name] = es.enter_context(nc.sbuf_tensor('sb_' + name, shape, dt))

    def PSUM(name, shape, dt):
        t[name] = es.enter_context(nc.psum_tensor('ps_' + name, shape, dt))

    SB("arena", [128, 77824], BF16)
    ar = t["arena"][:]
    t["Et"] = ar[:, 12288:77824]
    t["Wq"] = ar[:, 0:4096]
    t["Wk"] = ar[:, 4096:8192]
    t["Wa"] = ar[:, 8192:12288]
    t["Wv"] = ar[:, 12288:20480]
    t["Wo"] = ar[:, 20480:28672]
    arf = ar.bitcast(F32)
    t["wmr0"] = arf[:, 0:4096]
    t["wmr1"] = arf[:, 4096:8192]
    t["wmr0A"] = arf[:, 30208:34304]
    t["wmr1A"] = arf[:, 34304:38400]
    t["rowbufA"] = arf[0:1, 38400:38912]
    t["wmr0B"] = arf[:, 14336:18432]
    t["wmr1B"] = arf[:, 18432:22528]
    t["xt3"] = arf[:, 30208:31232]
    t["crow"] = arf[0:1, 8192:9216]
    t["scrow"] = arf[0:1, 9216:10240]
    t["wabf"] = arf[0:1, 10240:10752]
    t["wabloF"] = arf[0:1, 10752:11264]
    t["rowbuf0"] = arf[0:1, 11264:11776]
    t["rowbuf1"] = arf[0:1, 11776:12288]
    t["h0pre"] = ar[:, 28672:32768]
    t["shpk"] = arf[0:8, 16384:17416]
    t["aTc"] = arf[0:8, 17920:18432]
    t["bTc"] = arf[0:8, 18432:18944]
    t["AcumTc"] = arf[0:8, 18944:19456]
    t["UTe"] = arf[0:8, 19456:19968]
    t["Hsum"] = arf[0:8, 19968:20992]
    t["E1"] = arf[:, 21504:22528]
    t["Sall"] = arf[:, 22528:23552]
    t["OH8"] = arf[0:8, 23552:24576]

    for n, sh_, dt in [
        ("I128F", [128, 128], F32), ("I128B", [128, 128], BF16),
        ("MASK", [128, 128], F32),
        ("ONESB", [1, 128], BF16), ("zeros8c", [8, 512], F32),
        ("constv", [128, 8], F32), ("cfg", [1, 64], I32),
        ("flagsb", [1, 1], I32),
        ("sc", [128, 8], F32), ("modT", [128, 48], F32),
        ("ca1", [128, 8], F32), ("sab", [128, 8], F32),
        ("sab_b", [128, 8], BF16), ("cb1f", [128, 8], F32),
        ("sbf", [128, 8], F32),
        ("gafm", [128, 8], F32), ("gaT", [8, 128], F32),
        ("gatile", [128, D], F32),
        ("t8a", [128, 8], F32), ("t8b", [128, 8], F32),
        ("wabrow", [1, FQ], BF16), ("wablo", [1, FQ], BF16),
        ("u0q", [1, FQ], BF16), ("u0k", [1, FQ], BF16),
        ("u0a", [1, FQ], BF16), ("u0v", [1, D], BF16),
        ("zrow8b", [1, 8], BF16),
        ("Nst", [8, D], F32), ("Nbf", [8, D], BF16), ("Zst", [8, 1], F32),
        ("Pprod", [8, 1], F32),
        ("gf", [128, NCB * 8], F32), ("cghi", [128, NCB * 8], BF16),
        ("cglo", [128, NCB * 8], BF16), ("cgloF", [128, NCB * 8], F32),
        ("xt", [128, D], F32), ("h0", [128, D], F32),
        ("h0f", [128, D], BF16), ("h2lo", [128, D], BF16),
        ("sp1", [128, FQ], F32), ("sp2", [128, FQ], F32),
        ("mv", [128, 2], F32), ("bn12", [128, 12], F32),
        ("lnv", [128, 1], F32), ("rstd", [128, 1], F32),
        ("abpad", [128, 64], F32),
        ("aT", [8, 128], F32), ("bT", [8, 128], F32), ("bmT", [8, 128], F32),
        ("acuU", [40, 128], F32), ("eAT", [8, 128], F32),
        ("acutm", [128, 8], F32), ("utm", [128, 8], BF16),
        ("zl", [128, 8], F32), ("zT", [8, 128], F32), ("rT", [8, 128], F32),
        ("etf", [8, 128], F32), ("etb", [8, 128], BF16),
        ("rtm", [128, 8], F32),
        ("ssum", [128, 128], F32), ("ssb", [128, 128], BF16),
        ("sstsb", [128, 128], BF16),
        ("Vb", [128, D], BF16), ("yfm", [128, D], BF16),
        ("utm4", [128, 32], BF16), ("ePc", [8, 1], F32),
        ("dZc", [8, 1], F32), ("dZm", [8, 1], F32),
        ("HsT", [128, 64], BF16), ("zrowb", [1, 8], BF16),
        ("lg", [128, 8], F32), ("m8", [128, 8], F32),
        ("dd", [128, 1], F32), ("ee", [128, 1], F32), ("rr", [128, 1], F32),
        ("w2", [128, 1], F32), ("w1", [128, 1], F32),
        ("mk1", [128, 8], F32), ("mk2", [128, 8], F32),
        ("wful", [128, 8], F32),
    ]:
        SB(n, sh_, dt)

    _PARS = ['mv', 'bn12', 'lnv', 'rstd', 'abpad', 'aT', 'bT', 'bmT', 'acuU', 'eAT', 'acutm', 'utm', 'dZm', 'zl', 'zT', 'rT', 'etf', 'etb', 'rtm', 'ssum', 'ssb', 'sstsb', 'lg', 'm8', 'dd', 'ee', 'rr', 'w2', 'w1', 'mk1', 'mk2', 'wful']
    _shapes = {
        "mv": ([128, 2], F32), "bn12": ([128, 12], F32),
        "lnv": ([128, 1], F32), "rstd": ([128, 1], F32),
        "abpad": ([128, 64], F32), "aT": ([8, 128], F32),
        "bT": ([8, 128], F32), "bmT": ([8, 128], F32),
        "acuU": ([40, 128], F32), "eAT": ([8, 128], F32),
        "acutm": ([128, 8], F32), "utm": ([128, 8], BF16),
        "dZm": ([8, 1], F32), "zl": ([128, 8], F32),
        "zT": ([8, 128], F32), "rT": ([8, 128], F32),
        "etf": ([8, 128], F32), "etb": ([8, 128], BF16),
        "rtm": ([128, 8], F32), "ssum": ([128, 128], F32),
        "ssb": ([128, 128], BF16), "sstsb": ([128, 128], BF16),
        "lg": ([128, 8], F32), "m8": ([128, 8], F32),
        "dd": ([128, 1], F32), "ee": ([128, 1], F32),
        "rr": ([128, 1], F32), "w2": ([128, 1], F32),
        "w1": ([128, 1], F32), "mk1": ([128, 8], F32),
        "mk2": ([128, 8], F32), "wful": ([128, 8], F32),
    }
    for _n in _PARS:
        _shp, _dt = _shapes[_n]
        SB(_n + "_p1", _shp, _dt)
    t["_par0"] = {n: t[n][:] for n in _PARS}
    t["_par1"] = {n: t[n + "_p1"][:] for n in _PARS}
    t["_par0"]["xt"] = t["xt"][:]
    t["_par1"]["xt"] = t["gatile"][:]
    t["_parA0"] = {
        "h0": t["h0"][:], "h0f": t["h0f"][:],
        "sp1": t["sp1"][:], "sp2": t["sp2"][:],
        "Vb": t["Vb"][:], "yfm": t["yfm"][:],
        "E1": t["E1"], "Sall": t["Sall"],
    }
    t["_parA1"] = {
        "h0": arf[:, 24576:25600], "h0f": ar[:, 57344:58368],
        "sp1": arf[:, 25600:26112], "sp2": arf[:, 26112:26624],
        "Vb": ar[:, 58368:59392], "yfm": ar[:, 59392:60416],
        "E1": arf[:, 26624:27648], "Sall": arf[:, 27648:28672],
    }

    t["qs"] = t["sp1"][:]
    t["qk"] = t["sp2"][:]

    for _i, _v in enumerate([EPS_LN, 1.0, -0.5, -1.0, 0.0, 0.25, 2.0, -1e30]):
        nc.const_aps.register(F32, _v, t["constv"][0:128, _i:_i + 1])

    PSUM("psq", [128, 512], F32)
    PSUM("psk", [128, 512], F32)
    PSUM("psa", [128, 512], F32)
    PSUM("psv", [128, 1024], F32)
    PSUM("psy", [128, 1024], F32)
    PSUM("pss", [128, 512], F32)

    t["_posem"] = es.enter_context(nc.semaphore("PO"))
    t["_shfl"] = sh["SHFL"].ap()
    t["_regs"] = {}
    sem_h = {k: es.enter_context(nc.semaphore("SEM_" + k))
             for k in ("T", "V", "S", "GC", "GD", "SYD", "SYP", "GP")}
    tick_h = {ek: es.enter_context(nc.semaphore("TK_" + ek))
              for ek in ("sync", "tensor", "vector", "scalar", "gpsimd")}

    progs = []
    for core in range(N_CORES):
        prog = Prog()
        sems = {k: [sem_h[k], 0] for k in sem_h}
        ticks = {ek: [tick_h[ek], 0] for ek in tick_h}
        chc = Sched(prog, sems, ticks)
        emit_core(core, t, io, chc, sh)
        progs.append(prog)

    with nc.Block() as blk:
        def mk(ek):
            def body(eng):
                pid = eng.partition_id()
                for c in eng.Switch(pid, 8):
                    for f in progs[c].q[ek]:
                        f(eng)
            return body
        blk.sync(mk("sync"))
        blk.tensor(mk("tensor"))
        blk.vector(mk("vector"))
        blk.scalar(mk("scalar"))
        blk.gpsimd(mk("gpsimd"))

    es.close()
    return nc


_NC_CACHE = {}


def kernel(L0_attn_x, L0_attn_W_q, L0_attn_W_k, L0_attn_W_v,
           L0_attn_W_alpha_w, L0_attn_W_alpha_b, L0_attn_W_o,
           L0_ffn_gate_w, L0_ffn_expert_w, L0_c, L0_wmod):
    x = np.ascontiguousarray(np.asarray(L0_attn_x).reshape(S, D),
                             dtype=np.float32)
    i128 = np.eye(128, dtype=np.float32)
    mask = np.triu(np.ones((128, 128), np.float32), k=1) * np.float32(-1e30)
    oh = np.zeros((8, 8 * 128), np.float32)
    for h in range(8):
        oh[h, h * 128:(h + 1) * 128] = 1.0
    nonce = int(np.random.randint(1, 2 ** 30))
    cfg = np.zeros((1, 64), np.int32)
    cfg[0, 0] = nonce
    for j in range(max(ISL_LEN) // 128):
        cfg[0, 1 + j] = nonce + 1 + j

    if "nc" not in _NC_CACHE:
        _NC_CACHE["nc"] = build_nc()
    nc = _NC_CACHE["nc"]

    def asnp(a):
        return np.ascontiguousarray(np.asarray(a), dtype=np.float32)

    base = dict(
        XF=x,
        WQT=asnp(np.asarray(L0_attn_W_q).T),
        WKT=asnp(np.asarray(L0_attn_W_k).T),
        WAT=asnp(np.asarray(L0_attn_W_alpha_w).T),
        WVT=asnp(np.asarray(L0_attn_W_v).T),
        WOT=asnp(np.asarray(L0_attn_W_o).T),
        WAB=asnp(np.asarray(L0_attn_W_alpha_b).reshape(1, FQ)),
        GT=asnp(np.asarray(L0_ffn_gate_w).T),
        EXPT=asnp(np.asarray(L0_ffn_expert_w).transpose(0, 2, 1)),
        WMOD=asnp(L0_wmod),
        CVEC=asnp(np.asarray(L0_c).reshape(1, D)),
        I128F=i128, I128B=i128.astype(ml_dtypes.bfloat16), MASK=mask, OH8=oh,
        ONESB=np.ones((1, 128), ml_dtypes.bfloat16),
        CONSTV=np.tile(np.array([[EPS_LN, 1.0, -0.5, -1.0, 0.0, 0.25, 2.0,
                                  -1e30]], np.float32), (128, 1)),
        CFG=cfg,
    )
    in_maps = [dict(base) for _ in range(N_CORES)]
    res = run_bass_kernel_spmd(nc, in_maps, core_ids=list(range(N_CORES)))
    parts = [res.results[2 * k + 1]["OUT"][:ISL_LEN[k]] for k in range(4)]
    out = np.concatenate(parts, axis=0).reshape(1, S, D).astype(np.float32)
    return out


# version 30
# speedup vs baseline: 1.0493x; 1.0417x over previous
import numpy as np
import ml_dtypes

import concourse.bass as bass
import concourse.mybir as mybir
from concourse.bass_utils import run_bass_kernel_spmd

F32, BF16, I32 = mybir.dt.float32, mybir.dt.bfloat16, mybir.dt.int32
AF = mybir.ActivationFunctionType
ALU = mybir.AluOpType
AX = mybir.AxisListType

D = 1024
FQ = 512
NCB = 8
NEXP = 8
S = 16384
EPS_LN = 1e-5
N_CORES = 8

ISL_START = [0, 4096, 8192, 12288]
ISL_LEN = [4096, 4096, 4096, 4096]

SS = None

WLOAD_ALIASES = ("wmr0", "wmr1", "crow", "scrow", "wabf", "wabloF",
                 "rowbuf0", "rowbuf1")
ET_RES = tuple("EtL1" if e < 2 else ("EtL2" if e < 5 else "EtH")
               for e in range(8))
ET_ALIASES = ("h0pre", "shpk", "aTc", "bTc", "AcumTc", "UTe", "Hsum",
              "OH8", "E1", "Sall", "wmr0", "wmr1", "crow", "scrow",
              "wabf", "wabloF", "rowbuf0", "rowbuf1")


class Prog:
    def __init__(self):
        self.q = {e: [] for e in ("sync", "tensor", "vector", "scalar",
                                  "gpsimd")}

    def add(self, ek, f):
        self.q[ek].append(f)


class Sched:

    def __init__(self, prog, sems, ticks):
        self.p = prog
        self.sems = sems
        self.ticks = ticks
        self.res = {}
        self.rmap = {}

    def key_for(self, ek, dma):
        if ek == "sync":
            return "SYD"
        if ek == "gpsimd":
            return "GD" if dma else "GC"
        return {"tensor": "T", "vector": "V", "scalar": "S"}[ek]

    def stage(self, ek, fn, reads=(), writes=(), dma=False, ndma=1):
        key = self.key_for(ek, dma)
        pub_key = {"sync": "SYP", "gpsimd": "GP"}.get(ek) if dma else None
        deps = {}

        def add_dep(k2, v2):
            if v2 > deps.get(k2, -1):
                deps[k2] = v2

        reads = tuple(self.rmap.get(r, r) for r in reads)
        writes = tuple(self.rmap.get(w, w) for w in writes)
        for r in reads:
            rec = self.res.get(r)
            if rec and rec["w"]:
                add_dep(*rec["w"])
        for w in writes:
            rec = self.res.get(w)
            if rec:
                if rec["w"]:
                    add_dep(*rec["w"])
                for rd in rec["r"]:
                    add_dep(*rd)
        inc = 16 * ndma if dma else 1
        end_val = self.sems[key][1] + inc
        self.sems[key][1] = end_val
        if dma:
            pub_val = self.sems[pub_key][1] + 1
            self.sems[pub_key][1] = pub_val
            done = (pub_key, pub_val)
        else:
            done = (key, end_val)
        for w in writes:
            self.res[w] = {"w": done, "r": []}
        for r in reads:
            rec = self.res.setdefault(r, {"w": None, "r": []})
            rec["r"].append(done)

        handle = {k: self.sems[k][0] for k in self.sems}
        tick = self.ticks[ek]
        dep_items = sorted(deps.items())
        my_handle = self.sems[key][0]
        pub_handle = self.sems[pub_key][0] if dma else None

        def run(eng):
            global SS

            def ss(inst):
                tick[1] += 1
                inst.then_inc(tick[0], 1, skip_validation=True)
                eng.wait_ge(tick[0], tick[1])
                return inst
            SS = ss
            for k2, v2 in dep_items:
                eng.wait_ge(handle[k2], v2)
            ret = fn(eng)
            if isinstance(ret, list):
                assert dma and len(ret) == ndma, (dma, ndma, len(ret))
                for inst in ret:
                    inst.then_inc(my_handle, 16, skip_validation=True)
            elif dma:
                assert ndma == 1
                ret.then_inc(my_handle, 16, skip_validation=True)
            else:
                ret.then_inc(my_handle, 1, skip_validation=True)
            if dma:
                eng.wait_ge(my_handle, end_val)
                eng.nop().then_inc(pub_handle, 1, skip_validation=True)
        self.p.add(ek, run)


def emit_ln(ch, t):
    def st01(eng):
        eng.bn_stats(t["bn12"][:, 0:6], t["xt"][:, 0:512])
        SS(eng.bn_stats(t["bn12"][:, 6:12], t["xt"][:, 512:1024]))
        return eng.bn_aggr(t["mv"], t["bn12"])
    ch.stage("vector", st01, reads=("xt",), writes=("bn12", "mv"))

    def st02(eng):
        SS(eng.activation(t["lnv"], t["mv"][:, 1:2], AF.Ln, bias=EPS_LN))
        return eng.activation(t["rstd"], t["lnv"], AF.Exp, scale=-0.5)
    ch.stage("scalar", st02, reads=("mv", "constv"), writes=("lnv", "rstd"))

    ch.stage("vector", lambda e: e.tensor_scalar(
        t["h0"], t["xt"], t["mv"][:, 0:1], t["rstd"][:, 0:1],
        ALU.subtract, ALU.mult),
        reads=("xt", "mv", "rstd"), writes=("h0",))


def emit_ln_b(ch, t):
    def st01(eng):
        eng.bn_stats(t["bn12"][:, 0:6], t["xt"][:, 0:512])
        SS(eng.bn_stats(t["bn12"][:, 6:12], t["xt"][:, 512:1024]))
        return eng.bn_aggr(t["mv"], t["bn12"])
    ch.stage("vector", st01, reads=("xt",), writes=("bn12", "mv"))

    def st02(eng):
        SS(eng.activation(t["lnv"], t["mv"][:, 1:2], AF.Ln, bias=EPS_LN))
        return eng.activation(t["rstd"], t["lnv"], AF.Exp, scale=-0.5)
    ch.stage("scalar", st02, reads=("mv", "constv"), writes=("lnv", "rstd"))

    ch.stage("vector", lambda e: e.tensor_scalar(
        t["h0b"], t["xt"], t["mv"][:, 0:1], t["rstd"][:, 0:1],
        ALU.subtract, ALU.mult),
        reads=("xt", "mv", "rstd"), writes=("h0",))


def emit_transposes_h0_b(ch, t):
    def st04(eng):
        last = None
        pb = t["psv"][:].bitcast(BF16)
        for cb in range(8):
            last = eng.transpose(pb[0:128, cb * 128:(cb + 1) * 128],
                                 t["h0b"][:, cb * 128:(cb + 1) * 128],
                                 t["I128B"][:])
        return last
    ch.stage("tensor", st04, reads=("h0", "I128B"), writes=("psv",))


def emit_transposes_h0(ch, t):
    def st04(eng):
        last = None
        for cb in range(8):
            last = eng.transpose(t["psv"][:, cb * 128:(cb + 1) * 128],
                                 t["h0"][:, cb * 128:(cb + 1) * 128],
                                 t["I128F"][:])
        return last
    ch.stage("tensor", st04, reads=("h0", "I128F"), writes=("psv",))


def emit_qka(ch, t):
    ONESR = t["ONESB"][0:1, 0:128]

    def st06(eng):
        last = None
        for wnm, ps in (("Wq", "psq"), ("Wk", "psk"), ("Wa", "psa")):
            for cb in range(8):
                last = eng.matmul(t[ps][:, 0:512],
                                  t["h0f"][:, cb * 128:(cb + 1) * 128],
                                  t[wnm][:, cb * 512:(cb + 1) * 512],
                                  start=(cb == 0), stop=False,
                                  skip_group_check=True)
            u0 = {"Wq": "u0q", "Wk": "u0k", "Wa": "u0a"}[wnm]
            last = eng.matmul(t[ps][:, 0:512], ONESR, t[u0][:],
                              start=False, stop=(wnm != "Wa"),
                              skip_group_check=True)
            if wnm == "Wa":
                last = eng.matmul(t[ps][:, 0:512], ONESR, t["wabrow"][:],
                                  start=False, stop=False,
                                  skip_group_check=True)
                last = eng.matmul(t[ps][:, 0:512], ONESR, t["wablo"][:],
                                  start=False, stop=True,
                                  skip_group_check=True)
        return last
    ch.stage("tensor", st06,
             reads=("h0f", "Wq", "Wk", "Wa", "u0q", "u0k", "u0a",
                    "wabrow", "wablo", "ONESB"),
             writes=("psq", "psk", "psa"))


def emit_ab(ch, t):
    ch.stage("vector", lambda e: e.tensor_copy(t["qs"], t["psq"][:, 0:512]),
             reads=("psq",), writes=("sp1",))

    def st08(eng):
        SS(eng.tensor_mul(t["qk"], t["psk"][:, 0:512], t["qs"]))
        return eng.tensor_reduce(
            t["abpad"][:, 32:40],
            t["qk"].rearrange("p (h s) -> p h s", h=8), AX.X, ALU.add)
    ch.stage("vector", st08, reads=("psk", "sp1"), writes=("sp2", "abpad"))

    def st09(eng):
        SS(eng.activation(t["sp1"], t["psa"][:, 0:512], AF.Abs))
        SS(eng.activation(t["sp2"], t["sp1"], AF.Exp, scale=-1.0))
        return eng.activation(t["sp1"], t["sp2"], AF.Ln, bias=1.0)
    ch.stage("scalar", st09, reads=("psa", "constv", "sp1", "sp2"),
             writes=("sp1", "sp2"))

    def st10(eng):
        SS(eng.tensor_scalar(t["sp2"], t["psa"][:, 0:512], 0.0, None,
                             ALU.max))
        SS(eng.tensor_add(t["sp1"], t["sp1"], t["sp2"]))
        return eng.tensor_reduce(
            t["abpad"][:, 0:8],
            t["sp1"].rearrange("p (h s) -> p h s", h=8),
            AX.X, ALU.add, negate=True)
    ch.stage("vector", st10, reads=("psa", "sp1", "sp2"),
             writes=("sp1", "sp2", "abpad"))


def emit_ab_tr(ch, t):
    ch.stage("tensor", lambda e: e.transpose(
        t["pss"][0:64, 0:128], t["abpad"][:, 0:64], t["I128F"][:]),
        reads=("abpad", "I128F"), writes=("pss",))


def emit_gates(ch, t):
    def st12(eng):
        eng.tensor_copy(t["aT"], t["pss"][0:8, 0:128])
        return eng.tensor_copy(t["bT"], t["pss"][32:40, 0:128])
    ch.stage("vector", st12, reads=("pss",), writes=("aT", "bT"))

    def st13(eng):
        SS(eng.tensor_tensor_scan(t["acuU"][0:8, :], t["aT"],
                                  t["zeros8c"][:, 0:128], 0.0, ALU.add,
                                  ALU.add))
        SS(eng.tensor_sub(t["bmT"], t["bT"], t["acuU"][0:8, :]))
        SS(eng.tensor_copy(t["bmhi"], t["bmT"]))
        SS(eng.tensor_sub(t["bmlof"], t["bmT"], t["bmhi"]))
        return eng.tensor_copy(t["bmlo"], t["bmlof"])
    ch.stage("vector", st13, reads=("aT", "bT", "zeros8c"),
             writes=("acuU", "bmT", "bmhi", "bmlo", "bmlof"))

    def st19(eng):
        last = None
        for h in range(8):
            last = eng.matmul(t["psy"][:, h * 128:(h + 1) * 128],
                              t["OH8B"][:, h * 128:(h + 1) * 128],
                              t["bmhi"], start=True, stop=False,
                              skip_group_check=True)
            last = eng.matmul(t["psy"][:, h * 128:(h + 1) * 128],
                              t["OH8B"][:, h * 128:(h + 1) * 128],
                              t["bmlo"], start=False, stop=True,
                              skip_group_check=True)
        return last
    ch.stage("tensor", st19, reads=("OH8B", "bmhi", "bmlo"),
             writes=("psy",))

    def st14(eng):
        SS(eng.activation(t["eAT"], t["acuU"][0:8, :], AF.Exp))
        return eng.activation(t["acuU"][32:40, :], t["bmT"], AF.Exp,
                              bias=t["acuU"][0:8, 127:128])
    ch.stage("scalar", st14, reads=("acuU", "bmT"), writes=("eAT", "acuU"))

    ch.stage("tensor", lambda e: e.transpose(
        t["pss"][0:128, 0:40], t["acuU"][:, :], t["I128F"][0:40, 0:40]),
        reads=("acuU", "I128F"), writes=("pss",))

    def st16(eng):
        eng.tensor_copy(t["acutm"], t["pss"][0:128, 0:8])
        eng.tensor_copy(t["utm"], t["pss"][0:128, 32:40])
        return eng.tensor_reduce(t["dZm"], t["acuU"][32:40, :], AX.X,
                                 ALU.add)
    ch.stage("vector", st16, reads=("pss", "acuU"),
             writes=("acutm", "utm", "dZm"))


def emit_v(ch, t):
    ONESR = t["ONESB"][0:1, 0:128]

    def st17(eng):
        last = None
        for hf in range(2):
            for cb in range(8):
                cbase = cb * D + hf * 512
                last = eng.matmul(t["psv"][:, hf * 512:(hf + 1) * 512],
                                  t["h0f"][:, cb * 128:(cb + 1) * 128],
                                  t["Wv"][:, cbase:cbase + 512],
                                  start=(cb == 0), stop=False,
                                  skip_group_check=True)
            last = eng.matmul(t["psv"][:, hf * 512:(hf + 1) * 512],
                              ONESR, t["u0v"][0:1, hf * 512:(hf + 1) * 512],
                              start=False, stop=True, skip_group_check=True)
        return last
    ch.stage("tensor", st17, reads=("h0f", "Wv", "u0v", "ONESB"),
             writes=("psv",))


def emit_warm(ch, t):
    ch.stage("tensor", lambda e: e.matmul(
        t["psq"][0:128, 0:512], t["ONESB"][0:1, 0:128],
        t["wabrow"][0:1, 0:512], start=True, stop=True,
        skip_group_check=True),
        reads=("ONESB", "wabrow"), writes=("psq",))


def emit_v_ev(ch, t):
    ch.stage("vector", lambda e: e.tensor_copy(t["Vb"], t["psv"][:]),
             reads=("psv",), writes=("Vb",))


def emit_score(ch, t):
    def st20(eng):
        last = None
        for h in range(8):
            last = eng.tensor_add(t["E1"][:, h * 128:(h + 1) * 128],
                                  t["psy"][:, h * 128:(h + 1) * 128],
                                  t["MASK"][:])
        return last
    ch.stage("vector", st20, reads=("psy", "MASK"), writes=("E1",))

    def st21(eng):
        last = None
        for h in range(8):
            last = eng.activation(t["Sall"][:, h * 128:(h + 1) * 128],
                                  t["E1"][:, h * 128:(h + 1) * 128],
                                  AF.Exp, bias=t["acutm"][:, h:h + 1])
        return last
    ch.stage("scalar", st21, reads=("E1", "acutm"), writes=("Sall",))

    ch.stage("vector", lambda e: e.tensor_reduce(
        t["zl"], t["Sall"].rearrange("p (h s) -> p h s", h=8),
        AX.X, ALU.add), reads=("Sall",), writes=("zl",))

    ch.stage("tensor", lambda e: e.transpose(t["pss"][0:8, 0:128],
                                             t["zl"], t["I128F"][:]),
             reads=("zl", "I128F"), writes=("pss",))

    def st24(eng):
        SS(eng.scalar_tensor_tensor(t["zT"], t["eAT"], t["Zst"][:, 0:1],
                                    t["pss"][0:8, 0:128], ALU.mult, ALU.add))
        SS(eng.reciprocal(t["rT"], t["zT"]))
        SS(eng.tensor_mul(t["etf"], t["eAT"], t["rT"]))
        return eng.tensor_copy(t["etb"], t["etf"])
    ch.stage("vector", st24, reads=("eAT", "Zst", "pss"),
             writes=("zT", "rT", "etf", "etb"))

    ch.stage("tensor", lambda e: e.transpose(t["pss"][0:128, 0:8],
                                             t["rT"], t["I128F"][0:8, 0:8]),
             reads=("rT", "I128F"), writes=("pss",))
    ch.stage("vector", lambda e: e.tensor_copy(t["rtm"],
                                               t["pss"][0:128, 0:8]),
             reads=("pss",), writes=("rtm",))


def emit_score_b(ch, t):
    def st27(eng):
        SS(eng.tensor_scalar(t["ssum"], t["Sall"][:, 0:128],
                             t["rtm"][:, 0:1], None, ALU.mult))
        for h in range(1, 8):
            SS(eng.scalar_tensor_tensor(t["ssum"],
                                        t["Sall"][:, h * 128:(h + 1) * 128],
                                        t["rtm"][:, h:h + 1], t["ssum"],
                                        ALU.mult, ALU.add))
        return eng.tensor_copy(t["ssb"], t["ssum"])
    ch.stage("vector", st27, reads=("Sall", "rtm"), writes=("ssum", "ssb"))

    ch.stage("tensor", lambda e: e.transpose(
        t["pss"][:].bitcast(BF16)[0:128, 0:128], t["ssb"], t["I128B"][:]),
        reads=("ssb", "I128B"), writes=("pss",))
    ch.stage("vector", lambda e: e.tensor_copy(
        t["sstsb"], t["pss"][:].bitcast(BF16)[0:128, 0:128]),
        reads=("pss",), writes=("sstsb",))


def emit_scan2(ch, t):
    def st29a(eng):
        last = None
        for cb in range(8):
            last = eng.matmul(t["psy"][:, cb * 128:(cb + 1) * 128],
                              t["Vb"][:, cb * 128:(cb + 1) * 128],
                              t["sstsb"], start=True, stop=False,
                              skip_group_check=True)
            last = eng.matmul(t["psy"][:, cb * 128:(cb + 1) * 128],
                              t["Nbf"][:, cb * 128:(cb + 1) * 128],
                              t["etb"], start=False, stop=True,
                              skip_group_check=True)
        last = eng.matmul(t["pss"][0:8, 0:512], t["utm"],
                          t["Vb"][:, 0:512], start=True, stop=True,
                          skip_group_check=True)
        last = eng.matmul(t["pss"][32:40, 0:512], t["utm"],
                          t["Vb"][:, 512:1024], start=True, stop=True,
                          skip_group_check=True)
        return last
    ch.stage("tensor", st29a, reads=("Vb", "sstsb", "Nbf", "etb", "utm"),
             writes=("psy", "pss"))

    ch.stage("vector", lambda e: e.tensor_copy(t["yfm"], t["psy"][:]),
             reads=("psy",), writes=("yfm",))

    def st30b(eng):
        SS(eng.scalar_tensor_tensor(
            t["Nst"][:, 0:512], t["Nst"][:, 0:512], t["eAT"][:, 127:128],
            t["pss"][0:8, 0:512], ALU.mult, ALU.add))
        SS(eng.scalar_tensor_tensor(t["Nst"][:, 512:1024],
                                    t["Nst"][:, 512:1024],
                                    t["eAT"][:, 127:128],
                                    t["pss"][32:40, 0:512],
                                    ALU.mult, ALU.add))
        return eng.scalar_tensor_tensor(t["Zst"][:], t["Zst"][:],
                                        t["eAT"][:, 127:128], t["dZm"],
                                        ALU.mult, ALU.add)
    ch.stage("vector", st30b, reads=("Nst", "eAT", "pss", "dZm", "Zst"),
             writes=("Nst", "Zst"))

    ch.stage("vector", lambda e: e.tensor_copy(t["Nbf"][:], t["Nst"][:]),
             reads=("Nst",), writes=("Nbf",))


def emit_wo(ch, t):
    def st34(eng):
        last = None
        for hf in range(2):
            for cb in range(8):
                cbase = cb * D + hf * 512
                last = eng.matmul(t["psy"][:, hf * 512:(hf + 1) * 512],
                                  t["yfm"][:, cb * 128:(cb + 1) * 128],
                                  t["Wo"][:, cbase:cbase + 512],
                                  start=(cb == 0), stop=(cb == 7),
                                  skip_group_check=True)
        return last
    ch.stage("tensor", st34, reads=("yfm", "Wo"), writes=("psy",))

    ch.stage("vector", lambda e: e.tensor_add(t["xt"], t["xt"],
                                              t["psy"][:]),
             reads=("xt", "psy"), writes=("xt",))


def emit_poll(eng, t, core, flslot, cfgslot, payload_dst, payload_src):
    rt = t["_regs"][f"_rt_{core}"]
    rx = t["_regs"][f"_rx_{core}"]
    re_ = t["_regs"][f"_re_{core}"]
    rd = t["_regs"][f"_rd_{core}"]
    po = t["_posem"]
    eng.load(re_, t["cfg"][0:1, cfgslot:cfgslot + 1])
    eng.reg_add(rt, rt, 16)
    eng.dma_start(t["flagsb"][:], t["_shfl"][0:1, flslot:flslot + 1]
                  ).then_inc(po, 16, skip_validation=True)
    eng.wait_ge(po, rt)
    eng.load(rx, t["flagsb"][0:1, 0:1])
    eng.reg_alu(rd, rx, re_, ALU.subtract)
    with eng.While(rd):
        eng.reg_add(rt, rt, 16)
        eng.dma_start(t["flagsb"][:], t["_shfl"][0:1, flslot:flslot + 1]
                      ).then_inc(po, 16, skip_validation=True)
        eng.wait_ge(po, rt)
        eng.load(rx, t["flagsb"][0:1, 0:1])
        eng.reg_alu(rd, rx, re_, ALU.subtract)
    eng.reg_add(rt, rt, 16)
    eng.dma_start(payload_dst, payload_src).then_inc(po, 16,
                                                     skip_validation=True)
    eng.wait_ge(po, rt)
    return eng.nop()


PARITY_SMALL = ['mv', 'bn12', 'lnv', 'rstd', 'abpad', 'aT', 'bT', 'bmT', 'acuU', 'eAT', 'acutm', 'utm', 'dZm', 'zl', 'zT', 'rT', 'etf', 'etb', 'rtm', 'ssum', 'ssb', 'sstsb', 'lg', 'm8', 'dd', 'ee', 'rr', 'w2', 'w1', 'mk1', 'mk2', 'wful', 'bmhi', 'bmlo', 'bmlof']


class ChP:
    __slots__ = ("ch", "rmap")

    def __init__(self, ch, rmap):
        self.ch = ch
        self.rmap = rmap

    def stage(self, ek, fn, reads=(), writes=(), dma=False, ndma=1):
        self.ch.stage(ek, fn,
                      tuple(self.rmap.get(r, r) for r in reads),
                      tuple(self.rmap.get(w, w) for w in writes),
                      dma=dma, ndma=ndma)


def set_parity(ch, t, p, arena_par, gatile_ok, bffn=False):
    tt = dict(t)
    rmap = {}
    for nm in PARITY_SMALL:
        if p == 0:
            tt[nm] = t["_par0"][nm]
        else:
            tt[nm] = t["_par1"][nm]
            rmap[nm] = nm + "_p1"
    if gatile_ok and p == 1:
        tt["xt"] = t["_par1"]["xt"]
        rmap["xt"] = "gatile"
    else:
        tt["xt"] = t["_par0"]["xt"]
    pa = p if arena_par else 0
    src_map = t["_parA0"] if pa == 0 else t["_parA1"]
    for nm in ("h0", "h0f", "sp1", "sp2", "Vb", "yfm", "E1", "Sall"):
        tt[nm] = src_map[nm]
        if pa != 0:
            rmap[nm] = nm + "_ar1"
    tt["qs"] = tt["sp1"]
    tt["qk"] = tt["sp2"]
    tt["h0b"] = tt["h0"].bitcast(BF16)[0:128, 0:1024]
    tt["h2lo"] = t["h2lo"][:]
    if bffn and p == 1:
        tt["h0f"] = t["_parA0"]["Vb"]
        rmap["h0f"] = "Vb"
        tt["h2lo"] = t["_parA0"]["yfm"]
        rmap["h2lo"] = "yfm"
    tt["_ch"] = ChP(ch, rmap)
    return tt


def emit_core(core, t, io, ch, sh):
    isl = core // 2
    role_a = (core % 2 == 0)
    i0 = ISL_START[isl]
    ilen = ISL_LEN[isl]
    ntile = ilen // 128
    pfx_total = i0
    half = (pfx_total // 2 // 512) * 512
    pfx_lo, pfx_hi = (0, half) if role_a else (half, pfx_total)
    npc = (pfx_hi - pfx_lo) // 512
    ONESR = t["ONESB"][0:1, 0:128]

    def init_rt(eng):
        for key, nm in (("_rt", "rt"), ("_rx", "rx"), ("_re", "re"),
                        ("_rd", "rd")):
            t["_regs"][f"{key}_{core}"] = eng.alloc_register(f"{nm}_{core}")
        eng.reg_mov(t["_regs"][f"_rt_{core}"], 0)
        return eng.nop()
    ch.stage("gpsimd", init_rt)

    def st_loads(eng):
        out = []
        for nm in ("I128F", "I128B", "MASK", "ONESB"):
            out.append(eng.dma_start(t[nm][:], io[nm]))
        out.append(eng.dma_start(t["OH8"], io["OH8"]))
        out.append(eng.dma_start(t["crow"], io["CVEC"]))
        out.append(eng.dma_start(t["constv"][:], io["CONSTV"]))
        out.append(eng.dma_start(t["cfg"][:], io["CFG"]))
        out.append(eng.dma_start(t["wabf"], io["WAB"]))
        if not role_a:
            out.append(eng.dma_start(
                t["gf"][:].rearrange("p (cb w) -> p cb w", cb=NCB),
                io["GT"].rearrange("(cb p) w -> p cb w", p=128)))
        return out
    nload = 10 if not role_a else 9
    ch.stage("sync", st_loads, dma=True, ndma=nload,
             writes=("I128F", "I128B", "MASK", "ONESB", "OH8", "crow",
                     "constv", "cfg", "wabf") + (("gf",) if not role_a
                                                 else ()))

    def st_memset(eng):
        eng.memset(t["zeros8c"][:], 0.0)
        eng.memset(t["Nst"][:], 0.0)
        eng.memset(t["Nbf"][:], 0.0)
        eng.memset(t["Zst"][:], 0.0)
        eng.memset(t["zrow8b"][:], 0.0)
        return eng.memset(t["Pprod"][:], 1.0)
    ch.stage("vector", st_memset,
             writes=("zeros8c", "Nst", "Nbf", "Zst", "zrow8b",
                     "Pprod"))

    ch.stage("gpsimd", lambda e: e.dma_start(t["wabrow"][:], t["wabf"]),
             dma=True, reads=("wabf",), writes=("wabrow",))

    def st_wab2(eng):
        SS(eng.tensor_sub(t["wabloF"], t["wabf"], t["wabrow"][:]))
        return eng.tensor_copy(t["wablo"][:], t["wabloF"])
    ch.stage("vector", st_wab2, reads=("wabf", "wabrow"),
             writes=("wabloF", "wablo"))

    ch.stage("scalar", lambda e: e.activation(t["scrow"], t["crow"], AF.Silu),
             reads=("crow",), writes=("scrow",))

    def st_sct(eng):
        last = None
        for cb in range(8):
            last = eng.transpose(t["pss"][0:128, cb:cb + 1],
                                 t["scrow"][0:1, cb * 128:(cb + 1) * 128],
                                 t["I128F"][0:1, 0:1])
        return last
    ch.stage("tensor", st_sct, reads=("scrow", "I128F"), writes=("pss",))
    ch.stage("vector", lambda e: e.tensor_copy(t["sc"][:], t["pss"][:, 0:8]),
             reads=("pss",), writes=("sc",))

    wmrs = ("wmr0A", "wmr1A") if role_a else ("wmr0B", "wmr1B")
    rbn = "rowbufA" if role_a else "rowbuf0"

    wlist = (("Wq", io["WQT"]), ("Wk", io["WKT"]), ("Wa", io["WAT"])) + (
        (("Wv", io["WVT"]), ("Wo", io["WOT"])) if role_a else ())
    wnames = tuple(nm for nm, _ in wlist)

    def emit_wload1():
        def st_wload1(eng):
            out = []
            for nm, src in wlist[:3]:
                out.append(eng.dma_start(
                    t[nm].rearrange("p (cb w) -> p cb w", cb=NCB),
                    src.rearrange("(cb p) w -> p cb w", p=128)))
            return out
        ch.stage("gpsimd", st_wload1, dma=True, ndma=3,
                 writes=("Wq", "Wk", "Wa"))

    def emit_wload2():
        if role_a:
            def st_wload2(eng):
                out = []
                for nm, src in wlist[3:]:
                    out.append(eng.dma_start(
                        t[nm].rearrange("p (cb w) -> p cb w", cb=NCB),
                        src.rearrange("(cb p) w -> p cb w", p=128)))
                return out
            ch.stage("gpsimd", st_wload2, dma=True, ndma=2,
                     writes=("Wv", "Wo", "crow", "scrow", "wabf",
                             "wabloF"))
        else:
            def st_eth(eng):
                out = []
                for e_ in (5, 6, 7):
                    out.append(eng.dma_start(
                        t["Et"].rearrange("p (e cb w) -> p e cb w",
                                          e=NEXP, cb=NCB)[:, e_],
                        io["EXPT"][e_].rearrange("(cb p) w -> p cb w",
                                                 p=128)))
                return out
            ch.stage("gpsimd", st_eth, dma=True, ndma=3, writes=("EtH",))

    for r in range(12):
        wmr = wmrs[r % 2]
        rb = rbn

        def st_wmr(eng, r=r, wmr=wmr):
            return eng.dma_start(
                t[wmr].rearrange("p (cb w) -> p cb w", cb=8),
                io["WMOD"].rearrange("(cb p) w -> p cb w", p=128)
                [:, :, r * 512:(r + 1) * 512])
        ch.stage("sync", st_wmr, dma=True, writes=(wmr,))

        if r:
            def st_modt(eng, r=r - 1, rb=rbn):
                last = None
                for i in range(4):
                    s = r * 4 + i
                    last = eng.transpose(
                        t["pss"][0:128, s:s + 1],
                        t[rb][0:1, i * 128:(i + 1) * 128],
                        t["I128F"][0:1, 0:1])
                return last
            ch.stage("tensor", st_modt, reads=(rbn, "I128F"),
                     writes=("pss",))

        def st_modr(eng, r=r, wmr=wmr):
            last = None
            for cb2 in range(8):
                last = eng.matmul(t["psq"][0:1, 0:512],
                                  t["sc"][:, cb2:cb2 + 1],
                                  t[wmr][:, cb2 * 512:(cb2 + 1) * 512],
                                  start=(cb2 == 0), stop=(cb2 == 7),
                                  skip_group_check=True)
            return last
        ch.stage("tensor", st_modr, reads=(wmr, "sc"), writes=("psq",))
        ch.stage("vector", lambda e, rb=rb: e.tensor_copy(
            t[rb][:], t["psq"][0:1, 0:512]), reads=("psq",), writes=(rb,))
        if r == 4:
            emit_wload1()
        elif r == 7:
            emit_wload2()

    def st_modt_last(eng):
        last = None
        for i in range(4):
            s = 44 + i
            last = eng.transpose(t["pss"][0:128, s:s + 1],
                                 t[rbn][0:1, i * 128:(i + 1) * 128],
                                 t["I128F"][0:1, 0:1])
        return last
    ch.stage("tensor", st_modt_last, reads=(rbn, "I128F"),
             writes=("pss",))
    ch.stage("vector", lambda e: e.tensor_copy(t["modT"][:],
                                               t["pss"][:, 0:48]),
             reads=("pss",), writes=("modT",))

    if not role_a:
        ch.stage("vector", lambda e: e.nop(),
                 writes=("h0pre", "shpk", "aTc", "bTc", "AcumTc", "UTe",
                         "Hsum", wmrs[0], wmrs[1], rbn))
    ch.stage("vector", lambda e: e.memset(t["Hsum"][:], 0.0),
             writes=("Hsum",))

    gcol = (16, 24) if role_a else (40, 48)

    def st_mod2(eng):
        eng.tensor_scalar(t["ca1"][:], t["modT"][:, 8:16], 1.0, None, ALU.add)
        if not role_a:
            eng.tensor_scalar(t["cb1f"][:], t["modT"][:, 32:40], 1.0, None,
                              ALU.add)
            eng.tensor_copy(t["sbf"][:], t["modT"][:, 24:32])
        SS(eng.tensor_scalar(t["t8a"][:], t["modT"][:, gcol[0]:gcol[1]], 0.0,
                             None, ALU.is_ge))
        SS(eng.tensor_scalar(t["t8a"][:], t["t8a"][:], 2.0, -1.0, ALU.mult,
                             ALU.add))
        SS(eng.tensor_scalar(t["t8b"][:], t["modT"][:, gcol[0]:gcol[1]], -1.0,
                             None, ALU.mult))
        SS(eng.tensor_max(t["t8b"][:], t["t8b"][:],
                          t["modT"][:, gcol[0]:gcol[1]]))
        SS(eng.tensor_scalar(t["t8b"][:], t["t8b"][:], 0.25, None, ALU.max))
        SS(eng.tensor_mul(t["gafm"][:], t["t8a"][:], t["t8b"][:]))
        SS(eng.tensor_copy(t["sab"][:], t["modT"][:, 0:8]))
        return eng.tensor_copy(t["sab_b"][:], t["sab"][:])
    ch.stage("vector", st_mod2, reads=("modT", "constv"),
             writes=("ca1", "cb1f", "sbf", "t8a", "t8b", "gafm", "sab",
                     "sab_b"))

    ch.stage("tensor", lambda e: e.transpose(t["pss"][0:8, 0:128],
                                             t["gafm"][:], t["I128F"][:]),
             reads=("gafm", "I128F"), writes=("pss",))
    ch.stage("vector", lambda e: e.tensor_copy(t["gaT"][:],
                                               t["pss"][0:8, 0:128]),
             reads=("pss",), writes=("gaT",))

    def st_gat(eng):
        last = None
        for cb in range(8):
            last = eng.matmul(t["psy"][:, cb * 128:(cb + 1) * 128],
                              t["OH8"][:, cb * 128:(cb + 1) * 128],
                              t["gaT"][:], start=True, stop=True,
                              skip_group_check=True)
        return last
    ch.stage("tensor", st_gat, reads=("OH8", "gaT"), writes=("psy",))
    ch.stage("vector", lambda e: e.tensor_copy(t["gatile"][:], t["psy"][:]),
             reads=("psy",), writes=("gatile",))

    if not role_a:
        def st_etl1(eng):
            out = []
            for e_ in (0, 1):
                out.append(eng.dma_start(
                    t["Et"].rearrange("p (e cb w) -> p e cb w",
                                      e=NEXP, cb=NCB)[:, e_],
                    io["EXPT"][e_].rearrange("(cb p) w -> p cb w", p=128)))
            return out
        ch.stage("gpsimd", st_etl1, dma=True, ndma=2,
                 writes=("EtL1", "crow", "scrow", "wabf", "wabloF",
                         "rowbuf0"))

    u0list = (("Wq", "u0q", FQ), ("Wk", "u0k", FQ), ("Wa", "u0a", FQ)) + (
        (("Wv", "u0v", D),) if role_a else ())

    def emit_folds():
      for wnm, unm, wid in u0list:
        for hf in range(wid // 512):
            def st_u0(eng, wnm=wnm, wid=wid, hf=hf):
                last = None
                for cb in range(8):
                    cbase = cb * wid + hf * 512
                    last = eng.matmul(t["psq"][0:1, 0:512],
                                      t["sab_b"][:, cb:cb + 1],
                                      t[wnm][:, cbase:cbase + 512],
                                      start=(cb == 0), stop=(cb == 7),
                                      skip_group_check=True)
                return last
            ch.stage("tensor", st_u0, reads=("sab_b", wnm), writes=("psq",))
            if wnm == "Wq":
                ch.stage("vector", lambda e: e.tensor_scalar(
                    t["u0q"][:], t["psq"][0:1, 0:512], 0.125, None, ALU.mult),
                    reads=("psq",), writes=("u0q",))
            else:
                ch.stage("vector", lambda e, unm=unm, hf=hf: e.tensor_copy(
                    t[unm][0:1, hf * 512:(hf + 1) * 512],
                    t["psq"][0:1, 0:512]), reads=("psq",), writes=(unm,))

      def st_scale(eng):
        last = None
        for cb in range(8):
            last = eng.tensor_scalar(
                t["Wq"][:, cb * 512:(cb + 1) * 512],
                t["Wq"][:, cb * 512:(cb + 1) * 512],
                t["ca1"][:, cb:cb + 1], 0.125, ALU.mult, ALU.mult)
            for nm in ("Wk", "Wa"):
                last = eng.tensor_scalar(
                    t[nm][:, cb * 512:(cb + 1) * 512],
                    t[nm][:, cb * 512:(cb + 1) * 512],
                    t["ca1"][:, cb:cb + 1], None, ALU.mult)
            if role_a:
                last = eng.tensor_scalar(
                    t["Wv"][:, cb * 1024:(cb + 1) * 1024],
                    t["Wv"][:, cb * 1024:(cb + 1) * 1024],
                    t["ca1"][:, cb:cb + 1], None, ALU.mult)
        return last
      ch.stage("vector", st_scale, reads=("ca1",) + wnames, writes=wnames)

    def emit_scale_wo():
        def st_swo(eng):
            last = None
            for cb in range(8):
                last = eng.tensor_mul(
                    t["Wo"][:, cb * 1024:(cb + 1) * 1024],
                    t["Wo"][:, cb * 1024:(cb + 1) * 1024], t["gatile"][:])
            return last
        ch.stage("vector", st_swo, reads=("gatile", "Wo"), writes=("Wo",))

    if not role_a:
        def st_cg(eng):
            last = None
            for cb in range(8):
                last = eng.tensor_copy(t["cghi"][:, cb * 8:(cb + 1) * 8],
                                       t["gf"][:, cb * 8:(cb + 1) * 8])
            SS(last)
            for cb in range(8):
                last = eng.tensor_sub(t["cgloF"][:, cb * 8:(cb + 1) * 8],
                                      t["gf"][:, cb * 8:(cb + 1) * 8],
                                      t["cghi"][:, cb * 8:(cb + 1) * 8])
            SS(last)
            return eng.tensor_copy(t["cglo"][:], t["cgloF"][:])
        ch.stage("vector", st_cg, reads=("gf",),
                 writes=("cghi", "cgloF", "cglo"))

    nsub = npc * 4

    def pfx_front(tt, r0):
        tt["_ch"].stage("sync", lambda e, r0=r0, t=tt: e.dma_start(
            t["xt"], io["XF"][r0:r0 + 128, :]), dma=True, writes=("xt",))

    def pfx_tr(tt, t4):
        c = tt["_ch"]
        emit_transposes_h0_b(c, tt)

        def p05(eng, t4=t4, t=tt):
            eng.tensor_copy(t["h0f"],
                            t["psv"][:].bitcast(BF16)[0:128, 0:1024])
            return eng.tensor_copy(t["h0pre"][:, t4 * D:(t4 + 1) * D],
                                   t["h0b"])
        c.stage("vector", p05, reads=("psv", "h0"), writes=("h0f", "h0pre"))

    def pfx_ab(tt, t4):
        c = tt["_ch"]
        emit_ab(c, tt)
        emit_ab_tr(c, tt)

        def p12(eng, t4=t4, t=tt):
            eng.tensor_copy(t["aTc"][:, t4 * 128:(t4 + 1) * 128],
                            t["pss"][0:8, 0:128])
            return eng.tensor_copy(t["bTc"][:, t4 * 128:(t4 + 1) * 128],
                                   t["pss"][32:40, 0:128])
        c.stage("vector", p12, reads=("pss",), writes=("aTc", "bTc"))

    def pfx_combine():
        def p14(eng):
            SS(eng.tensor_tensor_scan(t["AcumTc"][:], t["aTc"][:],
                                      t["zeros8c"][:], 0.0, ALU.add, ALU.add))
            return eng.tensor_sub(t["UTe"][:], t["bTc"][:], t["AcumTc"][:])
        ch.stage("vector", p14, reads=("aTc", "bTc", "zeros8c"),
                 writes=("AcumTc", "UTe"))

        def p15(eng):
            SS(eng.activation(t["UTe"][:], t["UTe"][:], AF.Exp,
                              bias=t["AcumTc"][:, 511:512]))
            return eng.activation(t["ePc"][:], t["AcumTc"][:, 511:512],
                                  AF.Exp)
        ch.stage("scalar", p15, reads=("UTe", "AcumTc"),
                 writes=("UTe", "ePc"))

        ch.stage("vector", lambda e: e.tensor_reduce(t["dZc"][:], t["UTe"][:],
                                                     AX.X, ALU.add),
                 reads=("UTe",), writes=("dZc",))

        def p17(eng):
            last = None
            for t4 in range(4):
                last = eng.transpose(t["pss"][0:128, t4 * 8:(t4 + 1) * 8],
                                     t["UTe"][:, t4 * 128:(t4 + 1) * 128],
                                     t["I128F"][0:8, 0:8])
            return last
        ch.stage("tensor", p17, reads=("UTe", "I128F"), writes=("pss",))
        ch.stage("vector", lambda e: e.tensor_copy(t["utm4"][:],
                                                   t["pss"][0:128, 0:32]),
                 reads=("pss",), writes=("utm4",))

        for hf in range(2):
            def p19(eng, hf=hf):
                last = None
                for t4 in range(4):
                    last = eng.matmul(
                        t["pss"][0:8, 0:512],
                        t["utm4"][:, t4 * 8:(t4 + 1) * 8],
                        t["h0pre"][:, t4 * D + hf * 512:
                                   t4 * D + hf * 512 + 512],
                        start=(t4 == 0), stop=(t4 == 3),
                        skip_group_check=True)
                return last
            ch.stage("tensor", p19, reads=("utm4", "h0pre"), writes=("pss",))

            def p20(eng, hf=hf):
                return eng.scalar_tensor_tensor(
                    t["Hsum"][:, hf * 512:(hf + 1) * 512],
                    t["Hsum"][:, hf * 512:(hf + 1) * 512],
                    t["ePc"][:, 0:1], t["pss"][0:8, 0:512],
                    ALU.mult, ALU.add)
            ch.stage("vector", p20, reads=("Hsum", "ePc", "pss"),
                     writes=("Hsum",))

        def p22(eng):
            SS(eng.scalar_tensor_tensor(t["Zst"][:], t["Zst"][:],
                                        t["ePc"][:, 0:1], t["dZc"][:],
                                        ALU.mult, ALU.add))
            return eng.tensor_mul(t["Pprod"][:], t["Pprod"][:], t["ePc"][:])
        ch.stage("vector", p22, reads=("Zst", "ePc", "dZc", "Pprod"),
                 writes=("Zst", "Pprod"))

    subs = [set_parity(ch, t, n % 2, arena_par=role_a, gatile_ok=role_a,
                       bffn=(not role_a))
            for n in range(nsub)]
    if nsub:
        pfx_front(subs[0], pfx_lo)
        emit_ln_b(subs[0]["_ch"], subs[0])
        pfx_tr(subs[0], 0)
        emit_folds()
        if role_a:
            emit_scale_wo()
        emit_qka(subs[0]["_ch"], subs[0])
        for n in range(nsub):
            if n + 1 < nsub:
                pfx_front(subs[n + 1], pfx_lo + (n + 1) * 128)
            pfx_ab(subs[n], n % 4)
            if n + 1 < nsub:
                emit_ln_b(subs[n + 1]["_ch"], subs[n + 1])
            if n % 4 == 3:
                pfx_combine()
            if n + 1 < nsub:
                pfx_tr(subs[n + 1], (n + 1) % 4)
                emit_qka(subs[n + 1]["_ch"], subs[n + 1])
    tctr = nsub

    if not role_a:
        if pfx_total:
            def b_pack(eng):
                eng.tensor_copy(t["shpk"][:, 0:D], t["Hsum"][:])
                eng.tensor_copy(t["shpk"][:, D:D + 1], t["Zst"][:])
                return eng.tensor_copy(t["shpk"][:, D + 1:D + 2],
                                       t["Pprod"][:])
            ch.stage("vector", b_pack, reads=("Hsum", "Zst", "Pprod"),
                     writes=("shpk",))
            ch.stage("gpsimd", lambda e: e.dma_start(sh["SHH"].ap(),
                                                     t["shpk"][:]),
                     dma=True, reads=("shpk",), writes=("SHHres",))
            ch.stage("gpsimd", lambda e: e.dma_start(
                t["_shfl"][0:1, 0:1], t["cfg"][0:1, 0:1]), dma=True,
                reads=("SHHres", "cfg"), writes=("SHFLres",))
    else:
        if pfx_total:
            ch.stage("gpsimd", lambda e: emit_poll(
                e, t, core, 0, 0, t["shpk"][:], sh["SHH"].ap()),
                reads=("cfg",), writes=("shpk", "flagsb"))

            def a_comb(eng):
                SS(eng.scalar_tensor_tensor(t["Hsum"][:], t["Hsum"][:],
                                            t["shpk"][:, D + 1:D + 2],
                                            t["shpk"][:, 0:D],
                                            ALU.mult, ALU.add))
                return eng.scalar_tensor_tensor(t["Zst"][:], t["Zst"][:],
                                                t["shpk"][:, D + 1:D + 2],
                                                t["shpk"][:, D:D + 1],
                                                ALU.mult, ALU.add)
            ch.stage("vector", a_comb, reads=("Hsum", "Zst", "shpk"),
                     writes=("Hsum", "Zst"))

            def pf1(eng):
                last = None
                for cb in range(8):
                    last = eng.transpose(
                        t["pss"][0:128, cb * 8:(cb + 1) * 8],
                        t["Hsum"][:, cb * 128:(cb + 1) * 128],
                        t["I128F"][0:8, 0:8])
                return last
            ch.stage("tensor", pf1, reads=("Hsum", "I128F"), writes=("pss",))
            ch.stage("vector", lambda e: e.tensor_copy(t["HsT"][:],
                                                       t["pss"][0:128, 0:64]),
                     reads=("pss",), writes=("HsT",))
            ch.stage("tensor", lambda e: e.transpose(
                t["pss"][0:1, 0:8], t["Zst"][:, 0:1], t["I128F"][0:8, 0:8]),
                reads=("Zst", "I128F"), writes=("pss",))
            ch.stage("vector", lambda e: e.tensor_copy(t["zrowb"][:],
                                                       t["pss"][0:1, 0:8]),
                     reads=("pss",), writes=("zrowb",))
            for hf in range(2):
                def pf3(eng, hf=hf):
                    last = None
                    for cb in range(8):
                        last = eng.matmul(
                            t["psq"][0:8, 0:512],
                            t["HsT"][:, cb * 8:(cb + 1) * 8],
                            t["Wv"][:, cb * D + hf * 512:
                                    cb * D + hf * 512 + 512],
                            start=(cb == 0), stop=False,
                            skip_group_check=True)
                    last = eng.matmul(t["psq"][0:8, 0:512], t["zrowb"][:],
                                      t["u0v"][0:1, hf * 512:(hf + 1) * 512],
                                      start=False, stop=True,
                                      skip_group_check=True)
                    return last
                ch.stage("tensor", pf3, reads=("HsT", "Wv", "zrowb", "u0v"),
                         writes=("psq",))
                ch.stage("vector", lambda e, hf=hf: e.tensor_copy(
                    t["Nst"][:, hf * 512:(hf + 1) * 512],
                    t["psq"][0:8, 0:512]), reads=("psq",), writes=("Nst",))
            ch.stage("vector", lambda e: e.tensor_copy(t["Nbf"][:],
                                                       t["Nst"][:]),
                     reads=("Nst",), writes=("Nbf",))

    if role_a:
        mts = [set_parity(ch, t, (tctr + j) % 2, arena_par=True,
                          gatile_ok=True) for j in range(ntile)]

        def a_front(j):
            tt = mts[j]
            r0 = i0 + j * 128
            tt["_ch"].stage("sync", lambda e, r0=r0, t=tt: e.dma_start(
                t["xt"], io["XF"][r0:r0 + 128, :]), dma=True,
                writes=("xt",))

        def a_tr(j):
            tt = mts[j]
            emit_transposes_h0_b(tt["_ch"], tt)
            tt["_ch"].stage("vector", lambda e, t=tt: e.tensor_copy(
                t["h0f"], t["psv"][:].bitcast(BF16)[0:128, 0:1024]),
                reads=("psv",), writes=("h0f",))

        def a_out(j):
            tt = mts[j]
            tt["_ch"].stage("gpsimd", lambda e, j=j, t=tt: e.dma_start(
                sh["SHX"].ap()[j * 128:(j + 1) * 128, :], t["xt"]),
                dma=True, reads=("xt",), writes=(f"SHXres{j}",))
            tt["_ch"].stage("gpsimd", lambda e, j=j: e.dma_start(
                t["_shfl"][0:1, 1 + j:2 + j], t["cfg"][0:1, 1 + j:2 + j]),
                dma=True, reads=(f"SHXres{j}", "cfg"), writes=("SHFLres",))

        xtsl = [(t["_par0"]["xt"], "xt"), (t["_par1"]["xt"], "gatile"),
                (t["xt3"][:], "xt3")]
        for j in range(ntile):
            ap, nm = xtsl[j % 3]
            mts[j]["xt"] = ap
            mts[j]["_ch"].rmap["xt"] = nm
        ch.stage("vector", lambda e: e.nop(), writes=("xt3", "wmr0A"))
        ch.stage("vector", lambda e: e.tensor_copy(t["OH8B"][:], t["OH8"]),
                 reads=("OH8", "wmr0A"),
                 writes=("OH8B", "bmhi", "bmlo", "bmlof", "bmhi_p1",
                         "bmlo_p1", "bmlof_p1"))

        a_front(0)
        emit_ln_b(mts[0]["_ch"], mts[0])
        a_tr(0)
        if pfx_total == 0:
            emit_folds()
            emit_scale_wo()
        emit_qka(mts[0]["_ch"], mts[0])
        emit_ab(mts[0]["_ch"], mts[0])
        for j in range(ntile):
            tt = mts[j]
            if j + 1 < ntile:
                a_front(j + 1)
            emit_v(tt["_ch"], tt)
            emit_ab_tr(tt["_ch"], tt)
            emit_gates(tt["_ch"], tt)
            emit_v_ev(tt["_ch"], tt)
            if j + 1 < ntile:
                emit_ln_b(mts[j + 1]["_ch"], mts[j + 1])
                a_tr(j + 1)
                emit_qka(mts[j + 1]["_ch"], mts[j + 1])
            emit_score(tt["_ch"], tt)
            emit_score_b(tt["_ch"], tt)
            emit_scan2(tt["_ch"], tt)
            if j + 1 < ntile:
                emit_ab(mts[j + 1]["_ch"], mts[j + 1])
            emit_wo(tt["_ch"], tt)
            a_out(j)
        tctr += ntile
        ch.stage("gpsimd", lambda e: e.nop(), reads=("SHFLres",))
    else:
        def st_etl2(eng):
            out = []
            for e_ in (2, 3, 4):
                out.append(eng.dma_start(
                    t["Et"].rearrange("p (e cb w) -> p e cb w",
                                      e=NEXP, cb=NCB)[:, e_],
                    io["EXPT"][e_].rearrange("(cb p) w -> p cb w", p=128)))
            return out
        ch.stage("gpsimd", st_etl2, dma=True, ndma=3,
                 writes=("EtL2", "wmr0B", "wmr1B", "h0pre", "shpk", "aTc",
                         "bTc", "AcumTc", "UTe", "Hsum", "OH8", "E1",
                         "Sall"))

        for e_ in range(NEXP):
            def st_egb(eng, e_=e_):
                last = None
                for cb in range(8):
                    base = (e_ * NCB + cb) * D
                    last = eng.tensor_mul(t["Et"][:, base:base + D],
                                          t["Et"][:, base:base + D],
                                          t["gatile"][:])
                return last
            ch.stage("vector", st_egb, reads=(ET_RES[e_], "gatile"),
                     writes=(ET_RES[e_],))

        bts = [set_parity(ch, t, (tctr + j) % 2, arena_par=False,
                          gatile_ok=True, bffn=True) for j in range(ntile)]
        tctr += ntile

        def b_poll(j):
            tt = bts[j]
            tt["_ch"].stage("gpsimd", lambda e, j=j, t=tt: emit_poll(
                e, t, core, 1 + j, 1 + j, t["xt"],
                sh["SHX"].ap()[j * 128:(j + 1) * 128, :]),
                reads=("cfg",), writes=("xt", "flagsb"))

        def b_trb(j):
            tt = bts[j]
            c = tt["_ch"]
            emit_transposes_h0(c, tt)

            def f05(eng, t=tt):
                last = None
                for cb in range(8):
                    last = eng.tensor_scalar(
                        t["h0"][:, cb * 128:(cb + 1) * 128],
                        t["psv"][:, cb * 128:(cb + 1) * 128],
                        t["cb1f"][:, cb:cb + 1], t["sbf"][:, cb:cb + 1],
                        ALU.mult, ALU.add)
                SS(last)
                SS(eng.tensor_copy(t["h0f"], t["h0"]))
                return eng.tensor_sub(t["h2lo"], t["h0"], t["h0f"])
            c.stage("vector", f05, reads=("psv", "cb1f", "sbf", "h0"),
                    writes=("h0", "h0f", "h2lo"))

        def b_lg(j):
            tt = bts[j]
            c = tt["_ch"]

            def f_lg(eng, t=tt):
                last = None
                first = True
                for (mat, gw) in ((t["h0f"], "cghi"), (t["h0f"], "cglo"),
                                  (t["h2lo"], "cghi")):
                    for cb in range(8):
                        last = eng.matmul(t["psq"][0:128, 0:8],
                                          mat[:, cb * 128:(cb + 1) * 128],
                                          t[gw][:, cb * 8:(cb + 1) * 8],
                                          start=first, stop=False,
                                          skip_group_check=True)
                        first = False
                last = eng.matmul(t["psq"][0:128, 0:8], ONESR,
                                  t["zrow8b"][:], start=False, stop=True,
                                  skip_group_check=True)
                return last
            c.stage("tensor", f_lg,
                    reads=("h0f", "h2lo", "cghi", "cglo", "zrow8b",
                           "ONESB"),
                    writes=("psq",))
            c.stage("vector", lambda e, t=tt: e.tensor_copy(
                t["lg"], t["psq"][0:128, 0:8]),
                reads=("psq",), writes=("lg",))

        def b_top2(j):
            tt = bts[j]
            c = tt["_ch"]

            def f_top2(eng, t=tt):
                SS(eng.max(t["m8"], t["lg"]))
                return eng.tensor_sub(t["dd"], t["m8"][:, 1:2],
                                      t["m8"][:, 0:1])
            c.stage("vector", f_top2, reads=("lg",), writes=("m8", "dd"))
            c.stage("scalar", lambda e, t=tt: e.activation(t["ee"], t["dd"],
                                                           AF.Exp),
                    reads=("dd",), writes=("ee",))

            def f_w12(eng, t=tt):
                SS(eng.tensor_scalar(t["rr"], t["ee"], 1.0, None,
                                     ALU.add))
                SS(eng.reciprocal(t["rr"], t["rr"]))
                SS(eng.tensor_mul(t["w2"], t["ee"], t["rr"]))
                SS(eng.tensor_scalar(t["w1"], t["w2"], -1.0, 1.0,
                                     ALU.mult, ALU.add))
                SS(eng.tensor_scalar(t["mk1"], t["lg"], t["m8"][:, 0:1],
                                     None, ALU.is_equal))
                SS(eng.tensor_scalar(t["mk2"], t["lg"], t["m8"][:, 1:2],
                                     None, ALU.is_equal))
                SS(eng.tensor_scalar(t["mk1"], t["mk1"],
                                     t["w1"][:, 0:1], None, ALU.mult))
                return eng.scalar_tensor_tensor(t["wful"], t["mk2"],
                                                t["w2"][:, 0:1], t["mk1"],
                                                ALU.mult, ALU.add)
            c.stage("vector", f_w12, reads=("ee", "lg", "m8", "constv"),
                    writes=("rr", "w2", "w1", "mk1", "mk2", "wful"))

        def b_experts(j):
            tt = bts[j]
            c = tt["_ch"]
            for e_ in range(NEXP):
                psn = "psv" if e_ % 2 == 0 else "psy"

                def f_em(eng, e_=e_, psn=psn, t=tt):
                    last = None
                    for hf in range(2):
                        for cb in range(8):
                            base = (e_ * NCB + cb) * D + hf * 512
                            last = eng.matmul(
                                t[psn][:, hf * 512:(hf + 1) * 512],
                                t["h0f"][:, cb * 128:(cb + 1) * 128],
                                t["Et"][:, base:base + 512],
                                start=(cb == 0), stop=(cb == 7),
                                skip_group_check=True)
                    return last
                c.stage("tensor", f_em, reads=("h0f", ET_RES[e_]),
                        writes=(psn,))
                c.stage("vector",
                        lambda eng, e_=e_, psn=psn, t=tt:
                        eng.scalar_tensor_tensor(
                            t["xt"], t[psn][:], t["wful"][:, e_:e_ + 1],
                            t["xt"], ALU.mult, ALU.add),
                        reads=(psn, "wful", "xt"), writes=("xt",))

        def b_out(j):
            tt = bts[j]
            tt["_ch"].stage("gpsimd", lambda e, j=j, t=tt: e.dma_start(
                io["OUT"][j * 128:(j + 1) * 128, :], t["xt"]), dma=True,
                reads=("xt",), writes=(f"OUTres{j}",))

        b_poll(0)
        emit_ln(bts[0]["_ch"], bts[0])
        b_trb(0)
        b_lg(0)
        b_top2(0)
        for j in range(ntile):
            if j + 1 < ntile:
                b_poll(j + 1)
                emit_ln(bts[j + 1]["_ch"], bts[j + 1])
            b_experts(j)
            b_out(j)
            if j + 1 < ntile:
                b_trb(j + 1)
                b_lg(j + 1)
                b_top2(j + 1)
        ch.stage("gpsimd", lambda e: e.nop(),
                 reads=tuple(f"OUTres{j}" for j in range(ntile)))


def build_nc():
    from contextlib import ExitStack
    nc = bass.Bass("TRN2", target_bir_lowering=False)
    dram = nc.dram_tensor
    io = {}
    for nm, sh_, dt in [
        ("XF", [S, D], F32), ("WQT", [D, FQ], F32), ("WKT", [D, FQ], F32),
        ("WAT", [D, FQ], F32), ("WVT", [D, D], F32), ("WOT", [D, D], F32),
        ("WAB", [1, FQ], F32), ("GT", [D, NEXP], F32),
        ("EXPT", [NEXP, D, D], F32), ("WMOD", [D, 6 * D], F32),
        ("CVEC", [1, D], F32),
        ("I128F", [128, 128], F32), ("I128B", [128, 128], BF16),
        ("MASK", [128, 128], F32), ("OH8", [8, 8 * 128], F32),
        ("ONESB", [1, 128], BF16), ("CONSTV", [128, 8], F32),
        ("CFG", [1, 64], I32),
    ]:
        io[nm] = dram(nm, sh_, dt, kind="ExternalInput").ap()
    io["OUT"] = dram("OUT", [max(ISL_LEN), D], F32,
                     kind="ExternalOutput").ap()

    sh = {
        "SHX": dram("SHX", [max(ISL_LEN), D], F32, kind="Internal",
                    addr_space="Shared"),
        "SHH": dram("SHH", [8, D + 8], F32, kind="Internal",
                    addr_space="Shared"),
        "SHFL": dram("SHFL", [1, 64], I32, kind="Internal",
                     addr_space="Shared"),
    }

    es = ExitStack()
    t = {}

    def SB(name, shape, dt):
        t[name] = es.enter_context(nc.sbuf_tensor('sb_' + name, shape, dt))

    def PSUM(name, shape, dt):
        t[name] = es.enter_context(nc.psum_tensor('ps_' + name, shape, dt))

    SB("arena", [128, 77824], BF16)
    ar = t["arena"][:]
    t["Et"] = ar[:, 12288:77824]
    t["Wq"] = ar[:, 0:4096]
    t["Wk"] = ar[:, 4096:8192]
    t["Wa"] = ar[:, 8192:12288]
    t["Wv"] = ar[:, 12288:20480]
    t["Wo"] = ar[:, 20480:28672]
    arf = ar.bitcast(F32)
    t["wmr0"] = arf[:, 0:4096]
    t["wmr1"] = arf[:, 4096:8192]
    t["wmr0A"] = arf[:, 30208:34304]
    t["wmr1A"] = arf[:, 34304:38400]
    t["rowbufA"] = arf[0:1, 38400:38912]
    t["wmr0B"] = arf[:, 14336:18432]
    t["wmr1B"] = arf[:, 18432:22528]
    t["xt3"] = arf[:, 30208:31232]
    t["crow"] = arf[0:1, 8192:9216]
    t["scrow"] = arf[0:1, 9216:10240]
    t["wabf"] = arf[0:1, 10240:10752]
    t["wabloF"] = arf[0:1, 10752:11264]
    t["rowbuf0"] = arf[0:1, 11264:11776]
    t["rowbuf1"] = arf[0:1, 11776:12288]
    t["h0pre"] = ar[:, 28672:32768]
    t["shpk"] = arf[0:8, 16384:17416]
    t["aTc"] = arf[0:8, 17920:18432]
    t["bTc"] = arf[0:8, 18432:18944]
    t["AcumTc"] = arf[0:8, 18944:19456]
    t["UTe"] = arf[0:8, 19456:19968]
    t["Hsum"] = arf[0:8, 19968:20992]
    t["E1"] = arf[:, 21504:22528]
    t["Sall"] = arf[:, 22528:23552]
    t["OH8"] = arf[0:8, 23552:24576]

    for n, sh_, dt in [
        ("I128F", [128, 128], F32), ("I128B", [128, 128], BF16),
        ("MASK", [128, 128], F32),
        ("ONESB", [1, 128], BF16), ("zeros8c", [8, 512], F32),
        ("constv", [128, 8], F32), ("cfg", [1, 64], I32),
        ("flagsb", [1, 1], I32),
        ("sc", [128, 8], F32), ("modT", [128, 48], F32),
        ("ca1", [128, 8], F32), ("sab", [128, 8], F32),
        ("sab_b", [128, 8], BF16), ("cb1f", [128, 8], F32),
        ("sbf", [128, 8], F32),
        ("gafm", [128, 8], F32), ("gaT", [8, 128], F32),
        ("gatile", [128, D], F32),
        ("t8a", [128, 8], F32), ("t8b", [128, 8], F32),
        ("wabrow", [1, FQ], BF16), ("wablo", [1, FQ], BF16),
        ("u0q", [1, FQ], BF16), ("u0k", [1, FQ], BF16),
        ("u0a", [1, FQ], BF16), ("u0v", [1, D], BF16),
        ("zrow8b", [1, 8], BF16),
        ("Nst", [8, D], F32), ("Nbf", [8, D], BF16), ("Zst", [8, 1], F32),
        ("Pprod", [8, 1], F32),
        ("gf", [128, NCB * 8], F32), ("cghi", [128, NCB * 8], BF16),
        ("cglo", [128, NCB * 8], BF16), ("cgloF", [128, NCB * 8], F32),
        ("xt", [128, D], F32), ("h0", [128, D], F32),
        ("h0f", [128, D], BF16), ("h2lo", [128, D], BF16),
        ("sp1", [128, FQ], F32), ("sp2", [128, FQ], F32),
        ("mv", [128, 2], F32), ("bn12", [128, 12], F32),
        ("lnv", [128, 1], F32), ("rstd", [128, 1], F32),
        ("abpad", [128, 64], F32),
        ("aT", [8, 128], F32), ("bT", [8, 128], F32), ("bmT", [8, 128], F32),
        ("acuU", [40, 128], F32), ("eAT", [8, 128], F32),
        ("acutm", [128, 8], F32), ("utm", [128, 8], BF16),
        ("zl", [128, 8], F32), ("zT", [8, 128], F32), ("rT", [8, 128], F32),
        ("etf", [8, 128], F32), ("etb", [8, 128], BF16),
        ("rtm", [128, 8], F32),
        ("ssum", [128, 128], F32), ("ssb", [128, 128], BF16),
        ("sstsb", [128, 128], BF16),
        ("Vb", [128, D], BF16), ("yfm", [128, D], BF16),
        ("utm4", [128, 32], BF16), ("ePc", [8, 1], F32),
        ("dZc", [8, 1], F32), ("dZm", [8, 1], F32),
        ("HsT", [128, 64], BF16), ("zrowb", [1, 8], BF16),
        ("lg", [128, 8], F32), ("m8", [128, 8], F32),
        ("dd", [128, 1], F32), ("ee", [128, 1], F32), ("rr", [128, 1], F32),
        ("w2", [128, 1], F32), ("w1", [128, 1], F32),
        ("mk1", [128, 8], F32), ("mk2", [128, 8], F32),
        ("wful", [128, 8], F32),
    ]:
        SB(n, sh_, dt)

    _PARS = PARITY_SMALL
    _shapes = {
        "mv": ([128, 2], F32), "bn12": ([128, 12], F32),
        "lnv": ([128, 1], F32), "rstd": ([128, 1], F32),
        "abpad": ([128, 64], F32), "aT": ([8, 128], F32),
        "bT": ([8, 128], F32), "bmT": ([8, 128], F32),
        "acuU": ([40, 128], F32), "eAT": ([8, 128], F32),
        "acutm": ([128, 8], F32), "utm": ([128, 8], BF16),
        "dZm": ([8, 1], F32), "zl": ([128, 8], F32),
        "zT": ([8, 128], F32), "rT": ([8, 128], F32),
        "etf": ([8, 128], F32), "etb": ([8, 128], BF16),
        "rtm": ([128, 8], F32), "ssum": ([128, 128], F32),
        "ssb": ([128, 128], BF16), "sstsb": ([128, 128], BF16),
        "lg": ([128, 8], F32), "m8": ([128, 8], F32),
        "dd": ([128, 1], F32), "ee": ([128, 1], F32),
        "rr": ([128, 1], F32), "w2": ([128, 1], F32),
        "w1": ([128, 1], F32), "mk1": ([128, 8], F32),
        "mk2": ([128, 8], F32), "wful": ([128, 8], F32),
    }
    for _n in _PARS:
        if _n in ("bmhi", "bmlo", "bmlof"):
            continue
        _shp, _dt = _shapes[_n]
        SB(_n + "_p1", _shp, _dt)
    t["OH8B"] = ar[0:8, 62464:63488]
    t["bmhi"] = ar[0:8, 63488:63616]
    t["bmlo"] = ar[0:8, 63616:63744]
    t["bmhi_p1"] = ar[0:8, 63744:63872]
    t["bmlo_p1"] = ar[0:8, 63872:64000]
    t["bmlof"] = arf[0:8, 32000:32128]
    t["bmlof_p1"] = arf[0:8, 32128:32256]
    t["_par0"] = {n: t[n][:] for n in _PARS}
    t["_par1"] = {n: t[n + "_p1"][:] for n in _PARS}
    t["_par0"]["xt"] = t["xt"][:]
    t["_par1"]["xt"] = t["gatile"][:]
    t["_parA0"] = {
        "h0": t["h0"][:], "h0f": t["h0f"][:],
        "sp1": t["sp1"][:], "sp2": t["sp2"][:],
        "Vb": t["Vb"][:], "yfm": t["yfm"][:],
        "E1": t["E1"], "Sall": t["Sall"],
    }
    t["_parA1"] = {
        "h0": arf[:, 24576:25600], "h0f": ar[:, 57344:58368],
        "sp1": arf[:, 25600:26112], "sp2": arf[:, 26112:26624],
        "Vb": ar[:, 58368:59392], "yfm": ar[:, 59392:60416],
        "E1": arf[:, 26624:27648], "Sall": arf[:, 27648:28672],
    }

    t["qs"] = t["sp1"][:]
    t["qk"] = t["sp2"][:]

    for _i, _v in enumerate([EPS_LN, 1.0, -0.5, -1.0, 0.0, 0.25, 2.0, -1e30]):
        nc.const_aps.register(F32, _v, t["constv"][0:128, _i:_i + 1])

    PSUM("psq", [128, 512], F32)
    PSUM("psk", [128, 512], F32)
    PSUM("psa", [128, 512], F32)
    PSUM("psv", [128, 1024], F32)
    PSUM("psy", [128, 1024], F32)
    PSUM("pss", [128, 512], F32)

    t["_posem"] = es.enter_context(nc.semaphore("PO"))
    t["_shfl"] = sh["SHFL"].ap()
    t["_regs"] = {}
    sem_h = {k: es.enter_context(nc.semaphore("SEM_" + k))
             for k in ("T", "V", "S", "GC", "GD", "SYD", "SYP", "GP")}
    tick_h = {ek: es.enter_context(nc.semaphore("TK_" + ek))
              for ek in ("sync", "tensor", "vector", "scalar", "gpsimd")}

    progs = []
    for core in range(N_CORES):
        prog = Prog()
        sems = {k: [sem_h[k], 0] for k in sem_h}
        ticks = {ek: [tick_h[ek], 0] for ek in tick_h}
        chc = Sched(prog, sems, ticks)
        emit_core(core, t, io, chc, sh)
        progs.append(prog)

    with nc.Block() as blk:
        def mk(ek):
            def body(eng):
                pid = eng.partition_id()
                for c in eng.Switch(pid, 8):
                    for f in progs[c].q[ek]:
                        f(eng)
            return body
        blk.sync(mk("sync"))
        blk.tensor(mk("tensor"))
        blk.vector(mk("vector"))
        blk.scalar(mk("scalar"))
        blk.gpsimd(mk("gpsimd"))

    es.close()
    return nc


_NC_CACHE = {}


def kernel(L0_attn_x, L0_attn_W_q, L0_attn_W_k, L0_attn_W_v,
           L0_attn_W_alpha_w, L0_attn_W_alpha_b, L0_attn_W_o,
           L0_ffn_gate_w, L0_ffn_expert_w, L0_c, L0_wmod):
    x = np.ascontiguousarray(np.asarray(L0_attn_x).reshape(S, D),
                             dtype=np.float32)
    i128 = np.eye(128, dtype=np.float32)
    mask = np.triu(np.ones((128, 128), np.float32), k=1) * np.float32(-1e30)
    oh = np.zeros((8, 8 * 128), np.float32)
    for h in range(8):
        oh[h, h * 128:(h + 1) * 128] = 1.0
    nonce = int(np.random.randint(1, 2 ** 30))
    cfg = np.zeros((1, 64), np.int32)
    cfg[0, 0] = nonce
    for j in range(max(ISL_LEN) // 128):
        cfg[0, 1 + j] = nonce + 1 + j

    if "nc" not in _NC_CACHE:
        _NC_CACHE["nc"] = build_nc()
    nc = _NC_CACHE["nc"]

    def asnp(a):
        return np.ascontiguousarray(np.asarray(a), dtype=np.float32)

    base = dict(
        XF=x,
        WQT=asnp(np.asarray(L0_attn_W_q).T),
        WKT=asnp(np.asarray(L0_attn_W_k).T),
        WAT=asnp(np.asarray(L0_attn_W_alpha_w).T),
        WVT=asnp(np.asarray(L0_attn_W_v).T),
        WOT=asnp(np.asarray(L0_attn_W_o).T),
        WAB=asnp(np.asarray(L0_attn_W_alpha_b).reshape(1, FQ)),
        GT=asnp(np.asarray(L0_ffn_gate_w).T),
        EXPT=asnp(np.asarray(L0_ffn_expert_w).transpose(0, 2, 1)),
        WMOD=asnp(L0_wmod),
        CVEC=asnp(np.asarray(L0_c).reshape(1, D)),
        I128F=i128, I128B=i128.astype(ml_dtypes.bfloat16), MASK=mask, OH8=oh,
        ONESB=np.ones((1, 128), ml_dtypes.bfloat16),
        CONSTV=np.tile(np.array([[EPS_LN, 1.0, -0.5, -1.0, 0.0, 0.25, 2.0,
                                  -1e30]], np.float32), (128, 1)),
        CFG=cfg,
    )
    in_maps = [dict(base) for _ in range(N_CORES)]
    res = run_bass_kernel_spmd(nc, in_maps, core_ids=list(range(N_CORES)))
    parts = [res.results[2 * k + 1]["OUT"][:ISL_LEN[k]] for k in range(4)]
    out = np.concatenate(parts, axis=0).reshape(1, S, D).astype(np.float32)
    return out
